# version 06
# baseline (speedup 1.0000x reference)
import numpy as np
import ml_dtypes
from contextlib import ExitStack

import concourse.bass as bass
import concourse.mybir as mybir
from concourse.bass_utils import run_bass_kernel_spmd
from concourse.ap import AP
from concourse.bass_core_rust import IndirectOffsetOnAxis

F32 = mybir.dt.float32
BF16 = mybir.dt.bfloat16
AF = mybir.ActivationFunctionType
ALU = mybir.AluOpType
AX = mybir.AxisListType

D = 1024
NT = 4096
HALO = 1536
TOT = NT + HALO
BLK = 128
STT = 256
NST = TOT // STT
NHALO = HALO // STT
NE = 8
FF = 4096
NSLOT = 15
JT = 512
NJ = NT // JT
EPS = 1e-5


class Ctr:
    def __init__(self, sem):
        self.sem = sem
        self.n = 0

    def inc(self, ins, k=1):
        ins.then_inc(self.sem, k)
        self.n += k
        return self.n


def build(debug=False):
    nc = bass.Bass("TRN2", target_bir_lowering=False)

    x0_d = nc.dram_tensor("x0s", [TOT, D], F32, kind="ExternalInput").ap()
    msk_d = nc.dram_tensor("mask1", [1, 1], F32, kind="ExternalInput").ap()
    win_d = nc.dram_tensor("w_in_s", [D, 2048], F32, kind="ExternalInput").ap()
    bcin_d = nc.dram_tensor("b_in_cm", [128, 16], F32, kind="ExternalInput").ap()
    kc_d = [nc.dram_tensor(f"kc{j}", [128, 16], F32, kind="ExternalInput").ap() for j in range(3)]
    bcv_d = nc.dram_tensor("bcv_cm", [128, 16], F32, kind="ExternalInput").ap()
    fbc_d = nc.dram_tensor("fb_cm", [128, 8], F32, kind="ExternalInput").ap()
    b1_d = nc.dram_tensor("b1c", [64, 1], F32, kind="ExternalInput").ap()
    w1f_d = nc.dram_tensor("w1fT", [64, 3], F32, kind="ExternalInput").ap()
    w2f_d = nc.dram_tensor("w2f", [64, 64], F32, kind="ExternalInput").ap()
    b2f_d = nc.dram_tensor("b2f", [64, 1], F32, kind="ExternalInput").ap()
    w3f_d = nc.dram_tensor("w3f", [64, D], F32, kind="ExternalInput").ap()
    wout_d = nc.dram_tensor("w_out", [D, D], F32, kind="ExternalInput").ap()
    bout_d = nc.dram_tensor("b_out_r", [1, D], F32, kind="ExternalInput").ap()
    cT_d = nc.dram_tensor("cT", [D, 8], F32, kind="ExternalInput").ap()
    lng_d = nc.dram_tensor("ln_g", [8, D], F32, kind="ExternalInput").ap()
    lnb_d = nc.dram_tensor("ln_b", [8, D], F32, kind="ExternalInput").ap()
    w1_d = nc.dram_tensor("fw1", [NE, D, FF], F32, kind="ExternalInput").ap()
    fb1_d = nc.dram_tensor("fb1_cm", [NE * 128, 32], F32, kind="ExternalInput").ap()
    w2e_d = nc.dram_tensor("fw2", [NE, FF, D], F32, kind="ExternalInput").ap()
    fb2_d = nc.dram_tensor("fb2", [8, D], F32, kind="ExternalInput").ap()
    idf_d = nc.dram_tensor("id_f32", [128, 128], F32, kind="ExternalInput").ap()
    sl128_d = nc.dram_tensor("sl128", [128, 128], F32, kind="ExternalInput").ap()
    su8_d = nc.dram_tensor("su8", [8, 8], F32, kind="ExternalInput").ap()
    su32_d = nc.dram_tensor("su32", [32, 32], F32, kind="ExternalInput").ap()
    ev8_d = nc.dram_tensor("ev8", [8, 1], F32, kind="ExternalInput").ap()
    ones32_d = nc.dram_tensor("ones32c", [32, 1], F32, kind="ExternalInput").ap()
    iota15_d = nc.dram_tensor("iota15", [15, 1], F32, kind="ExternalInput").ap()
    iota15r_d = nc.dram_tensor("iota15r", [1, 16], F32, kind="ExternalInput").ap()
    evr_d = nc.dram_tensor("evrow_in", [1, 8], F32, kind="ExternalInput").ap()
    iotapf_d = nc.dram_tensor("iotapf", [128, 32], F32, kind="ExternalInput").ap()
    idb_d = nc.dram_tensor("id_b16", [128, 128], F32, kind="ExternalInput").ap()
    y_d = nc.dram_tensor("y", [NT, D], F32, kind="ExternalOutput").ap()

    dbg = "ExternalOutput" if debug else "Internal"
    hbuf_d = nc.dram_tensor("hbuf", [2048], BF16, kind=dbg).ap()
    x1_d = nc.dram_tensor("x1scr", [NT, D], F32, kind=dbg).ap()
    xln_d = nc.dram_tensor("xlnscr", [NT, D], BF16, kind=dbg).ap()
    xls_d = nc.dram_tensor("xlsort", [NSLOT * JT, D], BF16, kind=dbg).ap()
    ffs_d = nc.dram_tensor("ffsort", [NSLOT * JT, D], BF16, kind=dbg).ap()
    pos_d = nc.dram_tensor("posscr", [NT], mybir.dt.int32, kind=dbg).ap()
    sdat_d = nc.dram_tensor("sdat", [64], F32, kind=dbg).ap()
    c2_d = nc.dram_tensor("c2scr", [32, 8], F32, kind="Internal").ap()
    w1b_d = nc.dram_tensor("w1bscr", [NE * D, FF], BF16, kind="Internal").ap()
    fb2b_d = nc.dram_tensor("fb2bscr", [8, D], BF16, kind="Internal").ap()
    w2b_d = nc.dram_tensor("w2bscr", [NE * FF, D], BF16, kind="Internal").ap()
    if debug:
        dbg_d = {nm: nc.dram_tensor("dbg_" + nm, shp, dt_, kind="ExternalOutput").ap()
                 for nm, shp, dt_ in [
                     ("u2", [128, 16 * 258], F32), ("s0c", [128, 2048], F32),
                     ("vch", [128, 2048], F32), ("gb16", [128, 2048], BF16),
                     ("lnT", [128, 2048], BF16), ("ring", [128, 12 * 1024], BF16),
                     ("ydbg", [128, 2048], F32), ("x0", [128, 2048], F32),
                     ("lnb16", [128, 2048], BF16), ("stat", [128, 16], F32),
                     ("slot", [15, 32], F32), ("rk", [NT], F32), ("cs", [NT], F32),
                     ("oh", [128, 256], F32), ("w1g", [128, FF], BF16),
                     ("xT0", [128, 512], BF16), ("h10", [128, 32 * 512], BF16),
                     ("o0", [128, 1024], BF16), ("w2g0", [128, 8192], BF16),
                     ("ps0", [128, 512], F32)]}

    es = ExitStack()

    def sb(name, shape, dt):
        return es.enter_context(nc.sbuf_tensor(name, shape, dt))

    sQX = Ctr(es.enter_context(nc.semaphore("sQX")))
    sQW = Ctr(es.enter_context(nc.semaphore("sQW")))
    sPE = Ctr(es.enter_context(nc.semaphore("sPE")))
    sDV = Ctr(es.enter_context(nc.semaphore("sDV")))
    sAC = Ctr(es.enter_context(nc.semaphore("sAC")))
    sGP = Ctr(es.enter_context(nc.semaphore("sGP")))
    sST = Ctr(es.enter_context(nc.semaphore("sST")))

    idf = sb("idf", [128, 128], F32)
    idb = sb("idb", [128, 128], BF16)
    cTs = sb("cTs", [128, 64], F32)
    lngb = sb("lngb", [8, D], BF16)
    lnbb = sb("lnbb", [8, D], BF16)
    ohsb = sb("ohsb", [128, 256], F32)
    ohsb16 = sb("ohsb16", [128, 256], BF16)
    alps = sb("alps", [128, 32], F32)
    m1s = sb("m1s", [128, 32], F32)
    r1s = sb("r1s", [128, 32], F32)
    mskc = sb("mskc", [128, 1], F32)
    onesk = sb("onesk", [1, 128], BF16)
    boutb = sb("boutb", [1, D], BF16)
    mcol = sb("mcol", [128, 1], F32)
    sscol = sb("sscol", [128, 1], F32)
    ecol = sb("ecol", [128, 1], F32)
    tcol = sb("tcol", [128, 1], F32)
    vcol = sb("vcol", [128, 1], F32)
    rcol = sb("rcol", [128, 1], F32)
    m2col = sb("m2col", [128, 1], F32)
    r2col = sb("r2col", [128, 1], F32)
    affs = sb("affs", [128, 256], F32)
    amax = sb("amax", [128, 1], F32)
    ohblk = sb("ohblk", [128, 8], F32)
    tm4 = sb("tm4", [128, 6], F32)
    ohTf = sb("ohTf", [8, 128], F32)
    ohTb = sb("ohTb", [8, 128], BF16)
    iw1a = sb("iw1a", [128, 15 * 8], mybir.dt.int32)
    iw2a = sb("iw2a", [128, 15 * 32], mybir.dt.int32)
    ib1a = sb("ib1a", [128, 15], mybir.dt.int32)
    iesa = sb("iesa", [128, 15], mybir.dt.int32)
    statd = sb("statd", [128, 16], F32)
    bnst = sb("bnst", [128, 12], F32)
    bnagg = sb("bnagg", [128, 2], F32)

    t, v, s, g, q = nc.tensor, nc.vector, nc.scalar, nc.gpsimd, nc.sync

    def qdma(out, in_):
        q.dma_start(out=out, in_=in_).then_inc(sQX.sem, 16)
        sQX.n += 16
        return sQX.n

    def gdma(out, in_):
        g.dma_start(out=out, in_=in_).then_inc(sQW.sem, 16)
        sQW.n += 16
        return sQW.n

    def barrier():
        marks = [(sPE, t), (sDV, v), (sAC, s)]
        for c, eng in marks:
            c.inc(eng.nop())
        for c, eng in marks:
            for c2, _ in marks:
                if c2 is not c:
                    eng.wait_ge(c2.sem, c2.n)

    def ln_stats(src_ap, sq_sink, mean_col, rstd_col, dv_ready=0):
        v.bn_stats(bnst[:, 0:6], src_ap[:, 0:512])
        ins = v.bn_stats(bnst[:, 6:12], src_ap[:, 512:1024])
        sDV.inc(ins)
        v.wait_ge(sDV.sem, sDV.n)
        ins = v.bn_aggr(bnagg[:], bnst[:])
        sDV.inc(ins)
        v.wait_ge(sDV.sem, sDV.n)
        ins = v.tensor_copy(mean_col, bnagg[:, 0:1])
        sDV.inc(ins)
        v.wait_ge(sDV.sem, sDV.n)
        ins = v.tensor_scalar_add(vcol[:], bnagg[:, 1:2], EPS)
        sDV.inc(ins)
        s.wait_ge(sDV.sem, sDV.n)
        ins = s.activation(rcol[:], vcol[:], AF.Ln)
        sAC.inc(ins)
        s.wait_ge(sAC.sem, sAC.n)
        ins = s.activation(rstd_col, rcol[:], AF.Exp, scale=-0.5)
        sAC.inc(ins)
        v.wait_ge(sAC.sem, sAC.n)

    esA = ExitStack()

    def sba(name, shape, dt):
        return esA.enter_context(nc.sbuf_tensor(name, shape, dt))

    winsb = sba("winsb", [128, 8 * 2048], BF16)
    woutsb = sba("woutsb", [128, 8 * 1024], BF16)
    Rsb = sba("Rsb", [128, 10 * 256], BF16)
    ring = sba("ring", [128, 12 * 1024], BF16)
    x0t = sba("x0t", [128, 2 * 2048], F32)
    lnb16 = sba("lnb16", [128, 2048], BF16)
    lnT = sba("lnT", [128, 8 * 256], BF16)
    u2 = sba("u2", [128, 16 * 258], F32)
    s0c = sba("s0c", [128, 8 * 256], F32)
    vch = sba("vch", [128, 8 * 256], F32)
    vrev = sba("vrev", [128, 16 * 128], BF16)
    gb16 = sba("gb16", [128, 8 * 256], BF16)
    tmpA = sba("tmpA", [128, 256], F32)
    tmpB = sba("tmpB", [128, 256], F32)
    tmpC = sba("tmpC", [128, 256], F32)
    x1t = sba("x1t", [128, 1024], F32)
    ln1t = sba("ln1t", [128, 1024], F32)
    xlnb = sba("xlnb", [128, 1024], BF16)
    wcb = sba("wcb", [128, 2 * 4096], BF16)
    kcs = [sba(f"kcs{j}", [128, 16], F32) for j in range(3)]
    bcvs = sba("bcvs", [128, 16], F32)
    bcins = sba("bcins", [128, 16], F32)
    bcinm = sba("bcinm", [128, 16], F32)
    fbcs = sba("fbcs", [128, 8], F32)
    b1s = sba("b1s", [64, 1], F32)
    w1fs = sba("w1fs", [64, 3], F32)
    w1ss = sba("w1ss", [64, 1], F32)
    w2fs = sba("w2fs", [64, 64], F32)
    b2fs = sba("b2fs", [64, 1], F32)
    w3fs = sba("w3fs", [64, D], F32)
    r1f = sba("r1f", [64, 1], F32)
    r2f = sba("r2f", [64, 1], F32)
    hcol = sba("hcol", [128, 8], F32)
    h8 = sba("h8", [8, 128], F32)
    h8b = sba("h8b", [8, 128], BF16)
    z8 = sba("z8", [8, 256], BF16)
    fb2bb = sba("fb2bb", [8, D], BF16)

    sqt = vch[:, 0:1024]
    ln1T8 = s0c[:, 0:1024]

    psA = [esA.enter_context(nc.psum_tensor(f"psA{i}", [128, 512], F32)) for i in range(6)]
    ptA = [esA.enter_context(nc.psum_tensor(f"ptA{i}", [128, 1024], BF16)) for i in range(2)]

    blkA = esA.enter_context(nc.Block())

    qdma(idf[:], idf_d[:])
    for k in range(8):
        qdma(cTs[:, k * 8:(k + 1) * 8], cT_d[k * 128:(k + 1) * 128, :])
    for j in range(3):
        qdma(kcs[j][:], kc_d[j][:])
    qdma(bcvs[:], bcv_d[:])
    qdma(bcins[:], bcin_d[:])
    qdma(fbcs[:], fbc_d[:])
    qdma(b1s[:], b1_d[:])
    qdma(w1fs[:], w1f_d[:])
    qdma(w2fs[:], w2f_d[:])
    qdma(b2fs[:], b2f_d[:])
    qdma(w3fs[:], w3f_d[:])
    qdma(mskc[:], AP(msk_d.tensor, 0, [[0, 128], [1, 1]]))
    gdma(idb[:], idb_d[:])
    gdma(fb2bb[:], fb2_d[:])
    gdma(lngb[:], lng_d[:])
    gdma(lnbb[:], lnb_d[:])
    gdma(boutb[:], bout_d[:])
    for k in range(8):
        gdma(winsb[:, k * 2048:(k + 1) * 2048], win_d[k * 128:(k + 1) * 128, :])
        gdma(woutsb[:, k * 1024:(k + 1) * 1024], wout_d[k * 128:(k + 1) * 128, :])
    setup_qw = sQW.n
    setup_qx = sQX.n

    v.memset(u2[:], 0.0)
    v.memset(onesk[:], 1.0)
    ins = v.memset(z8[:], 0.0)
    sDV.inc(ins)
    v.wait_ge(sQX.sem, setup_qx)
    ins = v.tensor_scalar(bcinm[:], bcins[:], mskc[:], None, ALU.mult)
    sDV.inc(ins)

    v.wait_ge(sQX.sem, setup_qx)
    ins = v.reduce_sum(w1ss[:], w1fs[:], axis=AX.X)
    sDV.inc(ins)
    v.wait_ge(sDV.sem, sDV.n)
    ins = v.scalar_tensor_tensor(w1ss[:], w1ss[:], 0.0, b1s[:], ALU.mult, ALU.add)
    sDV.inc(ins)
    s.wait_ge(sDV.sem, sDV.n)
    ins = s.activation(r1f[:], w1ss[:], AF.Silu)
    sAC.inc(ins)
    t.wait_ge(sAC.sem, sAC.n)
    t.matmul(psA[0][0:64, 0:1], w2fs[:], r1f[:], start=True, stop=True)
    ins = t.matmul(psA[3][:, 0:128], idb[:], idb[:], start=True, stop=True)
    sPE.inc(ins)
    s.wait_ge(sPE.sem, sPE.n)
    ins = s.activation(r2f[:], psA[0][0:64, 0:1], AF.Silu, bias=b2fs[:])
    sAC.inc(ins)
    t.wait_ge(sAC.sem, sAC.n)
    for c in range(8):
        ins = t.matmul(psA[1][:, c:c + 1], w3fs[:, c * 128:(c + 1) * 128], r2f[:],
                       start=True, stop=True)
    ins = t.matmul(psA[3][:, 0:128], idb[:], idb[:], start=True, stop=True)
    sPE.inc(ins)
    v.wait_ge(sPE.sem, sPE.n)
    ins = v.tensor_copy(hcol[:], psA[1][:, 0:8])
    sDV.inc(ins)
    t.wait_ge(sDV.sem, sDV.n)
    t.transpose(psA[2][0:8, 0:128], hcol[:], idf[:])
    ins = t.matmul(psA[3][:, 0:128], idb[:], idb[:], start=True, stop=True)
    sPE.inc(ins)
    v.wait_ge(sPE.sem, sPE.n)
    v.tensor_copy(h8[:], psA[2][0:8, 0:128])
    ins = v.tensor_copy(h8b[:], h8[:])
    sDV.inc(ins)
    q.wait_ge(sDV.sem, sDV.n)
    q.wait_ge(sQW.sem, setup_qw)
    qdma(fb2b_d[:], fb2bb[:])
    qdma(AP(hbuf_d.tensor, 0, [[256, 8], [1, 256]]), z8[:])
    q.wait_ge(sQX.sem, sQX.n)
    qdma(AP(hbuf_d.tensor, 512, [[128, 8], [1, 128]]), h8b[:])
    q.wait_ge(sQX.sem, sQX.n)
    for ip, p in enumerate(range(-1, 9)):
        qdma(Rsb[:, ip * 256:(ip + 1) * 256],
             AP(hbuf_d.tensor, 512 + 128 * p - 127, [[1, 128], [1, 256]]))
    setup_qx2 = sQX.n

    x0buf = [x0t[:, 0:2048], x0t[:, 2048:4096]]
    for b in range(2):
        qdma(x0buf[0][:, b * 1024:(b + 1) * 1024], x0_d[b * 128:(b + 1) * 128, :])
    x0_ready = [sQX.n]

    for eng in (t, v, s, g):
        eng.wait_ge(sQX.sem, setup_qx2)
        eng.wait_ge(sQW.sem, setup_qw)

    evac_u = [0, 0]
    evac_pt = [0, 0]
    evac_y = [0, 0]
    evac_w = [0, 0]
    x0_free = [0, 0]
    ring_rd = 0
    tr_i = 0
    prev_stores = 0

    def transpose_b16(src_ap, dst_ap):
        nonlocal tr_i
        i = tr_i % 2
        tr_i += 1
        t.wait_ge(sAC.sem, evac_pt[i])
        ins = t.transpose(ptA[i][:, 0:128], src_ap, idb[:])
        sPE.inc(ins)
        s.wait_ge(sPE.sem, sPE.n)
        ins = s.copy(dst_ap, ptA[i][:, 0:128])
        evac_pt[i] = sAC.inc(ins)

    def transpose_f32(src_ap, dst_ap, bank, width=128):
        if bank < 2:
            t.wait_ge(sAC.sem, evac_u[bank])
        ins = t.transpose(psA[bank][0:width, 0:128], src_ap, idf[:])
        sPE.inc(ins)
        s.wait_ge(sPE.sem, sPE.n)
        ins = s.copy(dst_ap, psA[bank][0:width, 0:128])
        if bank < 2:
            evac_u[bank] = sAC.inc(ins)
        else:
            sAC.inc(ins)

    dv_chunks_prev = 0
    for st_i in range(NST):
        halo = st_i < NHALO
        cur = st_i % 2
        if st_i + 1 < NST:
            nxt = (st_i + 1) % 2
            q.wait_ge(sDV.sem, x0_free[nxt])
            for b in range(2):
                r0 = (st_i + 1) * STT + b * BLK
                qdma(x0buf[nxt][:, b * 1024:(b + 1) * 1024], x0_d[r0:r0 + 128, :])
            x0_ready.append(sQX.n)

        v.wait_ge(sQX.sem, x0_ready[st_i])
        s.wait_ge(sQX.sem, x0_ready[st_i])
        for b in range(2):
            xb = x0buf[cur][:, b * 1024:(b + 1) * 1024]
            ln_stats(xb, sqt, mcol[:], rcol[:])
            if debug and st_i == 14:
                v.tensor_copy(statd[:, b * 8 + 0: b * 8 + 1], mcol[:])
                v.tensor_copy(statd[:, b * 8 + 1: b * 8 + 2], sscol[:])
                v.tensor_copy(statd[:, b * 8 + 2: b * 8 + 3], vcol[:])
                v.tensor_copy(statd[:, b * 8 + 3: b * 8 + 4], rcol[:])
            ins = v.tensor_scalar(lnb16[:, b * 1024:(b + 1) * 1024], xb, mcol[:], rcol[:],
                                  ALU.subtract, ALU.mult)
            sDV.inc(ins)
        lnb16_done = sDV.n
        if debug and st_i == 14:
            q.wait_ge(sDV.sem, lnb16_done)
            qdma(dbg_d["x0"][:], x0buf[cur][:])
            qdma(dbg_d["lnb16"][:], lnb16[:])
            qdma(dbg_d["stat"][:], statd[:])
        t.wait_ge(sDV.sem, lnb16_done)
        for b in range(2):
            for k in range(8):
                transpose_b16(lnb16[:, b * 1024 + k * 128: b * 1024 + (k + 1) * 128],
                              lnT[:, k * 256 + b * 128: k * 256 + (b + 1) * 128])
        lnT_done = sAC.n

        for c in range(16):
            pi = c % 2
            t.wait_ge(sAC.sem, max(lnT_done, evac_u[pi]))
            for k in range(8):
                ins = t.matmul(psA[pi][:, 0:256],
                               winsb[:, k * 2048 + c * 128: k * 2048 + (c + 1) * 128],
                               lnT[:, k * 256:(k + 1) * 256],
                               start=(k == 0), stop=(k == 7))
            mm_done = sPE.inc(ins)
            s.wait_ge(sPE.sem, mm_done)
            if c == 0:
                s.wait_ge(sDV.sem, dv_chunks_prev)
            uc = u2[:, c * 258:(c + 1) * 258]
            if halo:
                ins = s.activation(uc[:, 2:258], psA[pi][:, 0:256], AF.Identity,
                                   scale=mskc[:], bias=bcinm[:, c:c + 1])
            else:
                ins = s.activation(uc[:, 2:258], psA[pi][:, 0:256], AF.Identity,
                                   bias=bcins[:, c:c + 1])
            evac_u[pi] = sAC.inc(ins)
            v.wait_ge(sAC.sem, evac_u[pi])
            v.tensor_scalar(tmpA[:], uc[:, 2:258], kcs[2][:, c:c + 1], bcvs[:, c:c + 1],
                            ALU.mult, ALU.add)
            v.tensor_scalar(tmpB[:], uc[:, 1:257], kcs[1][:, c:c + 1], None, ALU.mult)
            v.tensor_scalar(tmpC[:], uc[:, 0:256], kcs[0][:, c:c + 1], None, ALU.mult)
            ins = v.tensor_tensor(tmpB[:], tmpA[:], tmpB[:], ALU.add)
            sDV.inc(ins)
            if c < 8:
                ins = v.tensor_tensor(s0c[:, c * 256:(c + 1) * 256], tmpB[:], tmpC[:], ALU.add)
                sDV.inc(ins)
            else:
                cc = c - 8
                vd = vch[:, cc * 256:(cc + 1) * 256]
                ins = v.tensor_tensor(tmpA[:], tmpB[:], tmpC[:], ALU.add)
                sDV.inc(ins)
                ins = v.tensor_tensor(vd, tmpA[:], s0c[:, cc * 256:(cc + 1) * 256], ALU.mult)
                sDV.inc(ins)
                for b in range(2):
                    seg = vd[:, b * 128:(b + 1) * 128]
                    vcol_ = vrev[:, (cc * 2 + b) * 128:(cc * 2 + b + 1) * 128]
                    if halo:
                        ins = v.tensor_scalar(vcol_, seg[:, ::-1], mskc[:], None, ALU.mult)
                    else:
                        ins = v.tensor_copy(vcol_, seg[:, ::-1])
                    vrev_done = sDV.inc(ins)
                    blk_id = 2 * st_i + b
                    slot = blk_id % 12
                    t.wait_ge(sDV.sem, vrev_done)
                    t.wait_ge(sPE.sem, ring_rd)
                    transpose_b16(vcol_,
                                  ring[:, slot * 1024 + cc * 128: slot * 1024 + (cc + 1) * 128])
            ins = v.tensor_copy(uc[:, 0:2], uc[:, 256:258])
            sDV.inc(ins)
        ring_wr = sAC.n
        dv_chunks_prev = sDV.n

        if halo:
            x0_free[cur] = sDV.n
            continue

        t.wait_ge(sAC.sem, ring_wr)
        for cc in range(8):
            yi = 2 + (cc % 2)
            t.wait_ge(sDV.sem, evac_y[cc % 2])
            for ip, p in enumerate(range(-1, 9)):
                slot = (2 * st_i - p) % 12
                ins = t.matmul(psA[yi][:, 0:256],
                               ring[:, slot * 1024 + cc * 128: slot * 1024 + (cc + 1) * 128],
                               Rsb[:, ip * 256:(ip + 1) * 256],
                               start=(ip == 0), stop=(ip == 9))
            y_done = sPE.inc(ins)
            v.wait_ge(sPE.sem, y_done)
            ins = v.scalar_tensor_tensor(tmpB[:], vch[:, cc * 256:(cc + 1) * 256],
                                         fbcs[:, cc:cc + 1], psA[yi][:, 0:256],
                                         ALU.mult, ALU.add)
            sDV.inc(ins)
            ins = v.tensor_tensor(gb16[:, cc * 256:(cc + 1) * 256], tmpB[:],
                                  s0c[:, cc * 256:(cc + 1) * 256], ALU.mult)
            evac_y[cc % 2] = sDV.inc(ins)
        g_done = sDV.n
        ring_rd = sPE.n
        if debug and st_i == 14:
            q.wait_ge(sDV.sem, g_done)
            qdma(dbg_d["u2"][:], u2[:])
            qdma(dbg_d["s0c"][:], s0c[:])
            qdma(dbg_d["vch"][:], vch[:])
            qdma(dbg_d["gb16"][:], gb16[:])
            qdma(dbg_d["lnT"][:], lnT[:])
            qdma(dbg_d["ring"][:], ring[:])

        for b in range(2):
            rb = 2 * st_i + b - 12
            v.wait_ge(sQX.sem, prev_stores)
            s.wait_ge(sQX.sem, prev_stores)
            t.wait_ge(sDV.sem, g_done)
            for grp in range(2):
                t.wait_ge(sDV.sem, evac_w[grp])
                for cc in range(8):
                    ins = t.matmul(psA[4 + grp][:, 0:512],
                                   gb16[:, cc * 256 + b * 128: cc * 256 + (b + 1) * 128],
                                   woutsb[:, cc * 1024 + grp * 512: cc * 1024 + (grp + 1) * 512],
                                   start=(cc == 0), stop=False)
                ins = t.matmul(psA[4 + grp][:, 0:512], onesk[:],
                               boutb[0:1, grp * 512:(grp + 1) * 512], start=False, stop=True)
                w_done = sPE.inc(ins)
                v.wait_ge(sPE.sem, w_done)
                ins = v.tensor_tensor(x1t[:, grp * 512:(grp + 1) * 512],
                                      x0buf[cur][:, b * 1024 + grp * 512: b * 1024 + (grp + 1) * 512],
                                      psA[4 + grp][:, 0:512], ALU.add)
                evac_w[grp] = sDV.inc(ins)
            x1_done = sDV.n
            if b == 1:
                x0_free[cur] = x1_done
            q.wait_ge(sDV.sem, x1_done)
            qdma(x1_d[rb * 128:(rb + 1) * 128, :], x1t[:])

            ln_stats(x1t[:], sqt, m2col[:], r2col[:], dv_ready=x1_done)
            v.tensor_copy(m1s[:, rb:rb + 1], m2col[:])
            v.tensor_copy(r1s[:, rb:rb + 1], r2col[:])
            ins = v.tensor_scalar(ln1t[:], x1t[:], m2col[:], r2col[:], ALU.subtract, ALU.mult)
            ln1_done = sDV.inc(ins)

            t.wait_ge(sDV.sem, ln1_done)
            for k in range(8):
                transpose_f32(ln1t[:, k * 128:(k + 1) * 128],
                              ln1T8[:, k * 128:(k + 1) * 128], k % 2)
            lt_done = sAC.n
            t.wait_ge(sAC.sem, lt_done)
            t.wait_ge(sDV.sem, evac_w[0])
            for k in range(8):
                ins = t.matmul(psA[4][:, 0:8], ln1T8[:, k * 128:(k + 1) * 128],
                               cTs[:, k * 8:(k + 1) * 8], start=(k == 0), stop=(k == 7))
            t.wait_ge(sDV.sem, evac_w[1])
            ins = t.matmul(psA[5][:, 0:128], idb[:], idb[:], start=True, stop=True)
            aff_done = sPE.inc(ins)
            s.wait_ge(sPE.sem, aff_done)
            ins = s.copy(affs[:], psA[4][:, 0:256])
            aff_ev = sAC.inc(ins)
            v.wait_ge(sAC.sem, aff_ev)

            ins = v.tensor_tensor(tm4[:, 0:4], affs[:, 0:4], affs[:, 4:8], ALU.max)
            sDV.inc(ins)
            v.wait_ge(sDV.sem, sDV.n)
            ins = v.tensor_tensor(tm4[:, 4:6], tm4[:, 0:2], tm4[:, 2:4], ALU.max)
            sDV.inc(ins)
            v.wait_ge(sDV.sem, sDV.n)
            ins = v.tensor_tensor(amax[:], tm4[:, 4:5], tm4[:, 5:6], ALU.max)
            sDV.inc(ins)
            v.wait_ge(sDV.sem, sDV.n)
            ins = v.tensor_scalar(ohblk[:], affs[:, 0:8], amax[:], None, ALU.is_equal)
            oh_done = sDV.inc(ins)
            v.wait_ge(sDV.sem, sDV.n)
            if debug and rb == 16:
                v.wait_ge(sDV.sem, oh_done)
                v.tensor_copy(statd[:, 0:4], tm4[:, 0:4])
                v.tensor_copy(statd[:, 5:13], affs[:, 0:8])
                v.tensor_copy(statd[:, 13:14], amax[:])
                ins = v.tensor_copy(statd[:, 14:16], tm4[:, 4:6])
                sDV.inc(ins)
                q.wait_ge(sDV.sem, sDV.n)
                qdma(dbg_d["stat"][:], statd[:])
            v.tensor_copy(alps[:, rb:rb + 1], amax[:])
            v.tensor_copy(ohsb[:, rb * 8:(rb + 1) * 8], ohblk[:])
            ins = v.tensor_copy(ohsb16[:, rb * 8:(rb + 1) * 8], ohblk[:])
            sDV.inc(ins)
            t.wait_ge(sDV.sem, oh_done)
            t.wait_ge(sDV.sem, evac_y[1])
            transpose_f32(ohblk[:], ohTf[:], 3, width=8)
            ohT_ev = sAC.n
            v.wait_ge(sAC.sem, ohT_ev)
            ins = v.tensor_copy(ohTb[:], ohTf[:])
            ohtb_done = sDV.inc(ins)

            ln_stats(ln1t[:], sqt, m2col[:], r2col[:], dv_ready=ln1_done)
            ins = v.tensor_scalar(ln1t[:], ln1t[:], m2col[:], r2col[:], ALU.subtract, ALU.mult)
            xn_done = sDV.inc(ins)

            t.wait_ge(sDV.sem, ohtb_done)
            for grp in range(2):
                t.wait_ge(sDV.sem, evac_w[grp])
                ins = t.matmul(psA[4 + grp][:, 0:512], ohTb[:],
                               lngb[0:8, grp * 512:(grp + 1) * 512], start=True, stop=True)
                sPE.inc(ins)
            for grp in range(2):
                t.wait_ge(sDV.sem, evac_y[grp])
                ins = t.matmul(psA[2 + grp][:, 0:512], ohTb[:],
                               lnbb[0:8, grp * 512:(grp + 1) * 512], start=True, stop=True)
                sPE.inc(ins)
            gains_done = sPE.n
            v.wait_ge(sPE.sem, gains_done)
            v.wait_ge(sDV.sem, xn_done)
            for grp in range(2):
                sl = slice(grp * 512, (grp + 1) * 512)
                v.tensor_tensor(ln1t[:, sl], ln1t[:, sl], psA[4 + grp][:, 0:512], ALU.mult)
                ins = v.tensor_tensor(ln1t[:, sl], ln1t[:, sl], psA[2 + grp][:, 0:512], ALU.add)
                evac_w[grp] = sDV.inc(ins)
                evac_y[grp] = evac_w[grp]
            ins = v.tensor_copy(xlnb[:], ln1t[:])
            xlnb_done = sDV.inc(ins)
            q.wait_ge(sDV.sem, xlnb_done)
            qdma(xln_d[rb * 128:(rb + 1) * 128, :], xlnb[:])
            prev_stores = sQX.n

    pc_jobs = []
    for e in range(NE):
        for k in range(8):
            off = (e * 1024 + k * 128) * FF
            pc_jobs.append((AP(w1_d.tensor, off, [[FF, 128], [1, FF]]),
                            AP(w1b_d.tensor, off, [[FF, 128], [1, FF]])))
    for e in range(NE):
        for j in range(8):
            off = (e * FF + j * 512) * D
            pc_jobs.append((AP(w2e_d.tensor, off, [[4096, 128], [1, 4096]]),
                            AP(w2b_d.tensor, off, [[4096, 128], [1, 4096]])))
    pc_out = [0, 0]
    for pi_, (srcap, dstap) in enumerate(pc_jobs):
        buf = pi_ % 2
        g.wait_ge(sQW.sem, pc_out[buf])
        lin = gdma(wcb[:, buf * 4096:(buf + 1) * 4096], srcap)
        g.wait_ge(sQW.sem, lin)
        pc_out[buf] = gdma(dstap, wcb[:, buf * 4096:(buf + 1) * 4096])
    precast_done = sQW.n

    phaseA_stores = sQX.n
    esA.close()

    esS = ExitStack()

    def sbs(name, shape, dt):
        return esS.enter_context(nc.sbuf_tensor(name, shape, dt))

    sl128 = sbs("sl128s", [128, 128], BF16)
    su8 = sbs("su8s", [8, 8], F32)
    su32 = sbs("su32s", [32, 32], F32)
    ev8 = sbs("ev8s", [8, 1], F32)
    ones32 = sbs("ones32s", [32, 1], F32)
    iota15 = sbs("iota15s", [15, 1], F32)
    iotapf = sbs("iotapfs", [128, 32], F32)
    onec = sbs("onec", [128, 1], BF16)
    bcT = sbs("bcT", [8, 32], F32)
    bc32 = sbs("bc32", [32, 8], F32)
    csum = sbs("csumS", [32, 8], F32)
    n8 = sbs("n8", [8, 1], F32)
    m8 = sbs("m8", [8, 1], F32)
    t8a = sbs("t8a", [8, 1], F32)
    S8 = sbs("S8", [8, 1], F32)
    SM8 = sbs("SM8", [8, 1], F32)
    S5r = sbs("S5r", [1, 8], F32)
    r83 = sbs("r83", [8, 2], F32)
    Stile = sbs("Stile", [15, 8], F32)
    SMtile = sbs("SMtile", [15, 8], F32)
    both = sbs("both", [15, 8], F32)
    bothT = sbs("bothTs", [8, 16], F32)
    i15r = sbs("i15r", [8, 16], F32)
    evrow = sbs("evrow", [15, 8], F32)
    geT = sbs("geT", [8, 16], F32)
    ltT = sbs("ltT", [8, 16], F32)
    slotm = sbs("slotm", [16, 2], F32)
    rk8 = sbs("rk8", [128, 8], F32)
    rkev = sbs("rkev", [128, 256], F32)
    c2t = sbs("c2t", [128, 8], F32)
    selc = sbs("selc", [128, 1], F32)
    posf = sbs("posf", [128, 1], F32)
    posi = sbs("posi", [128, 4], mybir.dt.int32)
    xgs = sbs("xgs", [128, 4 * 1024], BF16)
    c2all = sbs("c2all", [128, 256], F32)
    escols = sbs("escols", [128, 15], F32)
    tmpe = sbs("tmpe", [128, 1], F32)
    tmpc8 = sbs("tmpc8", [128, 8], F32)
    onesf = sbs("onesf", [1, 32], F32)

    psS = [esS.enter_context(nc.psum_tensor(f"psS{i2}", [128, 512], F32)) for i2 in range(4)]

    blkS = esS.enter_context(nc.Block())
    barrier()
    ins = s.activation(alps[:], alps[:], AF.Sigmoid)
    sAC.inc(ins)

    qdma(su8[:], su8_d[:])
    qdma(su32[:], su32_d[:])
    qdma(ev8[:], ev8_d[:])
    qdma(ones32[:], ones32_d[:])
    qdma(iota15[:], iota15_d[:])
    qdma(i15r[:], AP(iota15r_d.tensor, 0, [[0, 8], [1, 16]]))
    qdma(iotapf[:], iotapf_d[:])
    gdma(sl128[:], sl128_d[:])
    sload = sQX.n
    sloadw = sQW.n
    ins = v.memset(onec[:], 1.0)
    sDV.inc(ins)
    v.wait_ge(sDV.sem, sDV.n)
    ins = v.memset(onesf[:], 1.0)
    sDV.inc(ins)
    for eng in (t, v, s):
        eng.wait_ge(sQX.sem, sload)
        eng.wait_ge(sQW.sem, sloadw)
        eng.wait_ge(sDV.sem, sDV.n)

    bc_ev = [0, 0]
    for rb in range(32):
        pb = rb % 2
        t.wait_ge(sAC.sem, bc_ev[pb])
        ins = t.matmul(psS[pb][0:8, 0:1], ohsb16[:, rb * 8:(rb + 1) * 8], onec[:],
                       start=True, stop=True)
        ins = t.matmul(psS[3][:, 128:256], idb[:], idb[:], start=True, stop=True)
        mm = sPE.inc(ins)
        s.wait_ge(sPE.sem, mm)
        ins = s.copy(bcT[:, rb:rb + 1], psS[pb][0:8, 0:1])
        bc_ev[pb] = sAC.inc(ins)
    t.wait_ge(sAC.sem, sAC.n)
    t.transpose(psS[2][0:32, 0:8], bcT[:], idf[0:8, 0:8])
    ins = t.matmul(psS[3][:, 128:256], idb[:], idb[:], start=True, stop=True)
    sPE.inc(ins)
    s.wait_ge(sPE.sem, sPE.n)
    ins = s.copy(bc32[:], psS[2][0:32, 0:8])
    sAC.inc(ins)
    t.wait_ge(sAC.sem, sAC.n)
    ins = t.matmul(psS[0][0:32, 0:8], su32[:], bc32[:], start=True, stop=False)
    sPE.inc(ins)
    t.matmul(psS[1][0:8, 0:1], bc32[:], ones32[:], start=True, stop=True)
    ins = t.matmul(psS[3][:, 128:256], idb[:], idb[:], start=True, stop=True)
    sPE.inc(ins)
    s.wait_ge(sPE.sem, sPE.n)
    ins = s.copy(n8[:], psS[1][0:8, 0:1])
    n8_ev = sAC.inc(ins)
    v.wait_ge(sAC.sem, n8_ev)
    ins = v.memset(m8[:], 0.0)
    sDV.inc(ins)
    v.wait_ge(sDV.sem, sDV.n)
    for jj in range(8):
        ins = v.tensor_scalar(t8a[:], n8[:], float(512 * jj), None, ALU.is_gt)
        sDV.inc(ins)
        v.wait_ge(sDV.sem, sDV.n)
        ins = v.tensor_tensor(m8[:], m8[:], t8a[:], ALU.add)
    m8_done = sDV.inc(ins)
    v.wait_ge(sDV.sem, sDV.n)
    t.wait_ge(sDV.sem, m8_done)
    ins = t.matmul(psS[1][0:8, 1:2], su8[:], m8[:], start=True, stop=True)
    sPE.inc(ins)
    v.wait_ge(sPE.sem, sPE.n)
    ins = v.tensor_copy(S8[:], psS[1][0:8, 1:2])
    sDV.inc(ins)
    v.wait_ge(sDV.sem, sDV.n)
    ins = v.tensor_tensor(SM8[:], psS[1][0:8, 1:2], m8[:], ALU.add)
    S8_done = sDV.inc(ins)
    v.wait_ge(sDV.sem, sDV.n)
    t.wait_ge(sDV.sem, S8_done)
    t.transpose(psS[3][0:1, 0:8], S8[:], idf[0:8, 0:8])
    ins = t.matmul(psS[2][:, 128:256], idb[:], idb[:], start=True, stop=True)
    sPE.inc(ins)
    v.wait_ge(sPE.sem, sPE.n)
    ins = v.tensor_scalar_mul(S5r[:], psS[3][0:1, 0:8], 512.0)
    s5_done = sDV.inc(ins)
    v.wait_ge(sDV.sem, sDV.n)
    t.wait_ge(sDV.sem, s5_done)
    t.matmul(psS[0][0:32, 0:8], onesf[:], S5r[:], start=False, stop=True)
    ins = t.matmul(psS[3][:, 128:256], idb[:], idb[:], start=True, stop=True)
    c2mm = sPE.inc(ins)
    s.wait_ge(sPE.sem, c2mm)
    ins = s.copy(csum[:], psS[0][0:32, 0:8])
    c2_ev = sAC.inc(ins)
    q.wait_ge(sDV.sem, S8_done)
    qdma(AP(sdat_d.tensor, 0, [[1, 8]]), S8[:])
    qdma(AP(sdat_d.tensor, 8, [[1, 8]]), SM8[:])
    q.wait_ge(sAC.sem, c2_ev)
    qdma(c2_d[:], csum[:])
    q.wait_ge(sQX.sem, sQX.n)
    qdma(Stile[:], AP(sdat_d.tensor, 0, [[0, 15], [1, 8]]))
    qdma(SMtile[:], AP(sdat_d.tensor, 8, [[0, 15], [1, 8]]))
    stile_ld = sQX.n
    v.wait_ge(sQX.sem, stile_ld)
    ins = v.tensor_scalar(Stile[:], Stile[:], iota15[:], None, ALU.is_le)
    sDV.inc(ins)
    v.wait_ge(sDV.sem, sDV.n)
    ins = v.tensor_scalar(SMtile[:], SMtile[:], iota15[:], None, ALU.is_gt)
    sDV.inc(ins)
    v.wait_ge(sDV.sem, sDV.n)
    ins = v.tensor_tensor(both[:], Stile[:], SMtile[:], ALU.mult)
    both_done = sDV.inc(ins)
    v.wait_ge(sDV.sem, sDV.n)
    ins = v.tensor_scalar(geT[:], i15r[:], S8[:], None, ALU.is_ge)
    sDV.inc(ins)
    v.wait_ge(sDV.sem, sDV.n)
    ins = v.tensor_scalar(ltT[:], i15r[:], SM8[:], None, ALU.is_lt)
    sDV.inc(ins)
    v.wait_ge(sDV.sem, sDV.n)
    ins = v.tensor_tensor(bothT[:], geT[:], ltT[:], ALU.mult)
    bothT_ev = sDV.inc(ins)
    v.wait_ge(sDV.sem, sDV.n)
    t.wait_ge(sDV.sem, bothT_ev)
    qdma(evrow[:], AP(evr_d.tensor, 0, [[0, 15], [1, 8]]))
    q.wait_ge(sQX.sem, sQX.n)
    v.wait_ge(sQX.sem, sQX.n)
    ins = v.tensor_tensor(evrow[:], evrow[:], both[:], ALU.mult)
    sDV.inc(ins)
    v.wait_ge(sDV.sem, sDV.n)
    ins = v.reduce_sum(slotm[0:15, 0:1], evrow[:], axis=AX.X)
    slotm_ev = sDV.inc(ins)
    v.wait_ge(sDV.sem, sDV.n)
    if debug:
        v.wait_ge(sDV.sem, slotm_ev)
        ins = v.tensor_copy(both[:], both[:])
        sDV.inc(ins)
        q.wait_ge(sDV.sem, sDV.n)
        qdma(dbg_d["slot"][:, 0:8], Stile[:])
        qdma(dbg_d["slot"][:, 8:16], SMtile[:])

        q.dma_start(out=dbg_d["slot"][0:8, 16:32], in_=bothT[:], allow_slow_non_contiguous=True).then_inc(sQX.sem, 16); sQX.n += 16
        q.dma_start(out=dbg_d["stat"][0:15, 3:5], in_=slotm[0:15, :], allow_slow_non_contiguous=True).then_inc(sQX.sem, 16); sQX.n += 16
        q.dma_start(out=dbg_d["stat"][0:8, 0:1], in_=S8[:], allow_slow_non_contiguous=True).then_inc(sQX.sem, 16); sQX.n += 16
        q.dma_start(out=dbg_d["stat"][0:8, 1:2], in_=SM8[:], allow_slow_non_contiguous=True).then_inc(sQX.sem, 16); sQX.n += 16
        q.dma_start(out=dbg_d["stat"][0:8, 2:3], in_=m8[:], allow_slow_non_contiguous=True).then_inc(sQX.sem, 16); sQX.n += 16
    q.wait_ge(sDV.sem, slotm_ev)
    qdma(AP(sdat_d.tensor, 16, [[1, 15]]), slotm[0:15, 0:1])
    q.wait_ge(sQX.sem, sQX.n)
    for s_ in range(NSLOT):
        qdma(escols[:, s_:s_ + 1], AP(sdat_d.tensor, 16 + s_, [[0, 128], [1, 1]]))
    escol_ld = sQX.n
    v.wait_ge(sQX.sem, escol_ld)
    for s_ in range(NSLOT):
        ins = v.tensor_scalar_mul(tmpe[:], escols[:, s_:s_ + 1], 1024.0)
        sDV.inc(ins)
        v.wait_ge(sDV.sem, sDV.n)
        ins = v.tensor_scalar(tmpc8[:], iotapf[:, 0:8], tmpe[:], None, ALU.add)
        sDV.inc(ins)
        v.wait_ge(sDV.sem, sDV.n)
        ins = v.tensor_copy(iw1a[:, s_ * 8:(s_ + 1) * 8], tmpc8[:])
        sDV.inc(ins)
        v.wait_ge(sDV.sem, sDV.n)
        ins = v.tensor_scalar_mul(tmpe[:], escols[:, s_:s_ + 1], 4096.0)
        sDV.inc(ins)
        v.wait_ge(sDV.sem, sDV.n)
        for qq in range(4):
            ins = v.tensor_scalar(tmpc8[:], iotapf[:, qq * 8:(qq + 1) * 8], tmpe[:], None, ALU.add)
            sDV.inc(ins)
            v.wait_ge(sDV.sem, sDV.n)
            ins = v.tensor_copy(iw2a[:, s_ * 32 + qq * 8: s_ * 32 + (qq + 1) * 8], tmpc8[:])
            sDV.inc(ins)
            v.wait_ge(sDV.sem, sDV.n)
        ins = v.tensor_scalar_mul(tmpe[:], escols[:, s_:s_ + 1], 128.0)
        sDV.inc(ins)
        v.wait_ge(sDV.sem, sDV.n)
        ins = v.tensor_scalar(selc[:], iotapf[:, 0:1], tmpe[:], None, ALU.add)
        sDV.inc(ins)
        v.wait_ge(sDV.sem, sDV.n)
        ins = v.tensor_copy(ib1a[:, s_:s_ + 1], selc[:])
        sDV.inc(ins)
        v.wait_ge(sDV.sem, sDV.n)
        ins = v.tensor_copy(iesa[:, s_:s_ + 1], escols[:, s_:s_ + 1])
        sDV.inc(ins)
    idx_done = sDV.n

    if debug:
        qdma(dbg_d["oh"][:], ohsb[:])
    for rb in range(32):
        qdma(c2all[:, rb * 8:(rb + 1) * 8], AP(c2_d.tensor, rb * 8, [[0, 128], [1, 8]]))
    c2all_ld = sQX.n
    ins = v.memset(xgs[:], 0.0)
    sDV.inc(ins)
    q.wait_ge(sDV.sem, sDV.n)
    qdma(AP(xls_d.tensor, 0, [[60 * 1024, 128], [1024, 60], [1, 1024]]),
         AP(xgs[:].tensor, 0, [[4096, 128], [0, 60], [1, 1024]]))
    xz_done = sQX.n
    g.wait_ge(sQX.sem, xz_done)
    rkv_ev = [0, 0]
    rk_evs = [0, 0]
    scat_by = [0, 0, 0, 0]
    pos_by = [0, 0, 0, 0]
    v.wait_ge(sQX.sem, c2all_ld)
    for rb in range(32):
        pb = rb % 2
        fb = rb % 4
        t.wait_ge(sAC.sem, rk_evs[pb])
        ins = t.matmul(psS[pb][:, 0:8], sl128[:], ohsb16[:, rb * 8:(rb + 1) * 8],
                       start=True, stop=True)
        ins = t.matmul(psS[2][:, 128:256], idb[:], idb[:], start=True, stop=True)
        mm = sPE.inc(ins)
        s.wait_ge(sPE.sem, mm)
        s.wait_ge(sDV.sem, rkv_ev[pb])
        ins = s.copy(rkev[:, pb * 128:(pb + 1) * 128], psS[pb][:, 0:128])
        rk_evs[pb] = sAC.inc(ins)
        v.wait_ge(sAC.sem, rk_evs[pb])
        v.wait_ge(sQW.sem, scat_by[fb])
        ins = v.tensor_tensor(rk8[:], rkev[:, pb * 128: pb * 128 + 8],
                              ohsb[:, rb * 8:(rb + 1) * 8], ALU.mult)
        rkv_ev[pb] = sDV.inc(ins)
        v.wait_ge(sDV.sem, sDV.n)
        ins = v.tensor_tensor(c2t[:], c2all[:, rb * 8:(rb + 1) * 8],
                              ohsb[:, rb * 8:(rb + 1) * 8], ALU.mult)
        sDV.inc(ins)
        v.wait_ge(sDV.sem, sDV.n)
        ins = v.reduce_sum(posf[:], rk8[:], axis=AX.X)
        sDV.inc(ins)
        v.wait_ge(sDV.sem, sDV.n)
        ins = v.reduce_sum(selc[:], c2t[:], axis=AX.X)
        sDV.inc(ins)
        v.wait_ge(sDV.sem, sDV.n)
        ins = v.tensor_tensor(posf[:], posf[:], selc[:], ALU.add)
        sDV.inc(ins)
        v.wait_ge(sDV.sem, sDV.n)
        ins = v.tensor_copy(posi[:, fb:fb + 1], posf[:])
        pos_done = sDV.inc(ins)
        v.wait_ge(sDV.sem, sDV.n)
        pos_by[fb] = pos_done
        q.wait_ge(sDV.sem, pos_done)
        qdma(AP(pos_d.tensor, rb * 128, [[1, 128]]), posi[:, fb:fb + 1])
        qdma(xgs[:, fb * 1024:(fb + 1) * 1024], xln_d[rb * 128:(rb + 1) * 128, :])
        xg_ld = sQX.n
        g.wait_ge(sQX.sem, xg_ld)
        ins = g.indirect_dma_start(xls_d[:], IndirectOffsetOnAxis(posi[:, fb:fb + 1], 0),
                                   xgs[:, fb * 1024:(fb + 1) * 1024], None,
                                   bounds_check=NSLOT * JT - 1, oob_is_err=False)
        ins.then_inc(sQW.sem, 16)
        sQW.n += 16
        scat_by[fb] = sQW.n
    phaseS_stores = sQX.n
    phaseS_scat = sQW.n
    del c2t

    esS.close()

    esB = ExitStack()

    def sbb(name, shape, dt):
        return esB.enter_context(nc.sbuf_tensor(name, shape, dt))

    w1g = sbb("w1g", [128, 8 * FF], BF16)
    w2g = sbb("w2g", [128, 8 * 1024], BF16)
    xg = sbb("xg", [128, 2 * 4096], BF16)
    xT = sbb("xT", [128, 8 * 512], BF16)
    h1sb = sbb("h1sb", [128, 32 * 512], BF16)
    outs = sbb("outs", [128, 4 * 1024], BF16)
    fb1g = sbb("fb1g", [128, 2 * 32], F32)
    fb2b = sbb("fb2b", [2, 2 * 1024], BF16)
    x1sl = sbb("x1sl", [128, 2 * 1024], F32)
    outb = sbb("outb", [128, 2 * 1024], F32)
    ln1c = sbb("ln1c", [128, 1024], F32)
    posb = sbb("posb", [128, 2], mybir.dt.int32)
    ffr = sbb("ffr", [128, 2 * 1024], BF16)
    acj = sbb("acj", [128, 512], F32)

    psB = [esB.enter_context(nc.psum_tensor(f"psB{i2}", [128, 512], F32)) for i2 in range(8)]

    blkB = esB.enter_context(nc.Block())
    barrier()
    g.wait_ge(sQW.sem, max(phaseS_scat, precast_done))

    w1_rd = 0
    w2_rd = [0] * 8
    h1_ev = [0, 0]
    tr_ev = [0, 0]
    out_st = 0
    xg_rd = 0
    w1_ready = 0
    fb_ready = 0
    w2_ready = [0] * 8

    def slot_pref(s_, buf):
        nonlocal w1_ready, fb_ready
        g.wait_ge(sPE.sem, w1_rd)
        for k in range(8):
            ins2 = g.indirect_dma_start(w1g[:, k * FF:(k + 1) * FF], None,
                                        w1b_d[:],
                                        IndirectOffsetOnAxis(iw1a[:, s_ * 8 + k: s_ * 8 + k + 1], 0))
            ins2.then_inc(sQW.sem, 16)
            sQW.n += 16
        ins2 = g.indirect_dma_start(fb1g[:, buf * 32:(buf + 1) * 32], None,
                                    fb1_d[:],
                                    IndirectOffsetOnAxis(ib1a[:, s_:s_ + 1], 0))
        ins2.then_inc(sQW.sem, 16)
        sQW.n += 16
        ins2 = g.indirect_dma_start(fb2b[0:2, buf * 1024:(buf + 1) * 1024], None,
                                    fb2b_d[:],
                                    IndirectOffsetOnAxis(iesa[0:2, s_:s_ + 1], 0))
        ins2.then_inc(sQW.sem, 16)
        sQW.n += 16
        w1_ready = sQW.n
        fb_ready = sQW.n

    slot_pref(0, 0)
    for s_ in range(NSLOT):
        buf = s_ % 2
        q.wait_ge(sPE.sem, xg_rd)
        q.wait_ge(sQW.sem, phaseS_scat)
        for sb_ in range(4):
            qdma(xg[:, buf * 4096 + sb_ * 1024: buf * 4096 + (sb_ + 1) * 1024],
                 xls_d[s_ * JT + sb_ * 128: s_ * JT + (sb_ + 1) * 128, :])
        xld = sQX.n
        for fc in range(8):
            g.wait_ge(sPE.sem, w2_rd[fc])
            ins = g.indirect_dma_start(w2g[:, fc * 1024:(fc + 1) * 1024], None,
                                       w2b_d[:],
                                       IndirectOffsetOnAxis(iw2a[:, s_ * 32 + fc: s_ * 32 + fc + 1], 0))
            ins.then_inc(sQW.sem, 16)
            sQW.n += 16
            w2_ready[fc] = sQW.n
        t.wait_ge(sQX.sem, xld)
        for sb_ in range(4):
            for k in range(8):
                tb = (sb_ * 8 + k) % 2
                t.wait_ge(sAC.sem, tr_ev[tb])
                ins = t.matmul(psB[tb][:, 0:128],
                               xg[:, buf * 4096 + sb_ * 1024 + k * 128: buf * 4096 + sb_ * 1024 + (k + 1) * 128],
                               idb[:], start=True, stop=True)
                mm = sPE.inc(ins)
                s.wait_ge(sPE.sem, mm)
                ins = s.copy(xT[:, k * 512 + sb_ * 128: k * 512 + (sb_ + 1) * 128],
                             psB[tb][:, 0:128])
                tr_ev[tb] = sAC.inc(ins)
        ins = s.copy(acj[:], acj[:])
        xT_done = sAC.inc(ins)
        xg_rd = sPE.n
        t.wait_ge(sQW.sem, w1_ready)
        t.wait_ge(sQW.sem, fb_ready)
        for m in range(32):
            pb = m % 2
            t.wait_ge(sAC.sem, h1_ev[pb] if m >= 2 else max(h1_ev[pb], xT_done))
            for k in range(8):
                ins = t.matmul(psB[pb][:, 0:512],
                               w1g[:, k * FF + m * 128: k * FF + (m + 1) * 128],
                               xT[:, k * 512:(k + 1) * 512],
                               start=(k == 0), stop=(k == 7))
            mm = sPE.inc(ins)
            s.wait_ge(sPE.sem, mm)
            s.wait_ge(sQW.sem, fb_ready)
            ins = s.activation(h1sb[:, m * 512:(m + 1) * 512], psB[pb][:, 0:512],
                               AF.Relu, bias=fb1g[:, buf * 32 + m: buf * 32 + m + 1])
            h1_ev[pb] = sAC.inc(ins)
        ins = s.copy(acj[:], acj[:])
        sAC.inc(ins)
        w1_rd = sPE.n
        h1_done = sAC.n
        if debug and s_ == 0:
            q.wait_ge(sPE.sem, w1_rd)
            qdma(dbg_d["w1g"][:], w1g[:, 0:FF])
            q.wait_ge(sAC.sem, xT_done)
            qdma(dbg_d["xT0"][:], xT[:, 0:512])
            q.wait_ge(sAC.sem, h1_done)
            qdma(dbg_d["h10"][:], h1sb[:])
        fb2b_done = 0
        if s_ + 1 < NSLOT:
            slot_pref(s_ + 1, 1 - buf)
        if debug and s_ == 0:
            q.wait_ge(sQW.sem, w2_ready[7])
            qdma(dbg_d["w2g0"][:], w2g[:])
            dbg_w2ld = sQX.n
            t.wait_ge(sQX.sem, dbg_w2ld)
        t.wait_ge(sAC.sem, h1_done)
        t.wait_ge(sQW.sem, fb_ready)
        w2_next = [0] * 8
        for fc in range(32):
            slot = fc % 8
            t.wait_ge(sQW.sem, w2_ready[slot] if fc < 8 else w2_next[slot])
            for sb_ in range(4):
                for grp in range(2):
                    ins = t.matmul(psB[sb_ * 2 + grp][:, 0:512],
                                   h1sb[:, fc * 512 + sb_ * 128: fc * 512 + (sb_ + 1) * 128],
                                   w2g[:, slot * 1024 + grp * 512: slot * 1024 + (grp + 1) * 512],
                                   start=(fc == 0), stop=False)
            fc_done = sPE.inc(ins)
            w2_rd[slot] = fc_done
            if fc + 8 < 32:
                g.wait_ge(sPE.sem, fc_done)
                ins = g.indirect_dma_start(w2g[:, slot * 1024:(slot + 1) * 1024], None,
                                           w2b_d[:],
                                           IndirectOffsetOnAxis(iw2a[:, s_ * 32 + fc + 8: s_ * 32 + fc + 9], 0))
                ins.then_inc(sQW.sem, 16)
                sQW.n += 16
                w2_next[slot] = sQW.n
        if debug and s_ == 0:
            ins = t.nop()
            sPE.inc(ins)
            s.wait_ge(sPE.sem, sPE.n)
            for _ in range(5):
                ins = s.copy(acj[:], acj[:])
            sAC.inc(ins)
            s.wait_ge(sAC.sem, sAC.n)
            ins = s.copy(acj[:], psB[0][:, 0:512])
            sAC.inc(ins)
            q.wait_ge(sAC.sem, sAC.n)
            qdma(dbg_d["ps0"][:], acj[:])
            t.wait_ge(sQX.sem, sQX.n)
        for sb_ in range(4):
            for grp in range(2):
                ins = t.matmul(psB[sb_ * 2 + grp][:, 0:512], onesk[:],
                               fb2b[0:1, buf * 1024 + grp * 512: buf * 1024 + (grp + 1) * 512],
                               start=False, stop=True)
        w2_done = sPE.inc(ins)
        s.wait_ge(sPE.sem, w2_done)
        s.wait_ge(sQX.sem, out_st)
        for sb_ in range(4):
            for grp in range(2):
                ins = s.copy(outs[:, sb_ * 1024 + grp * 512: sb_ * 1024 + (grp + 1) * 512],
                             psB[sb_ * 2 + grp][:, 0:512])
        sAC.inc(ins)
        ins = s.copy(acj[:], acj[:])
        out_ev = sAC.inc(ins)
        q.wait_ge(sAC.sem, out_ev)
        for sb_ in range(4):
            qdma(ffs_d[s_ * JT + sb_ * 128: s_ * JT + (sb_ + 1) * 128, :],
                 outs[:, sb_ * 1024:(sb_ + 1) * 1024])
        if debug and s_ == 0:
            qdma(dbg_d["o0"][:], outs[:, 0:1024])
        out_st = sQX.n
        tr_ev = [out_ev, out_ev]
        h1_ev = [out_ev, out_ev]
    ffs_done = sQX.n

    x1_free = [0, 0]
    out_free = [0, 0]
    ff_rd = [0, 0]
    for rb in range(32):
        cb = rb % 2
        q.wait_ge(sDV.sem, x1_free[cb])
        q.wait_ge(sQX.sem, ffs_done)
        qdma(x1sl[:, cb * 1024:(cb + 1) * 1024], x1_d[rb * 128:(rb + 1) * 128, :])
        qdma(posb[:, cb:cb + 1], AP(pos_d.tensor, rb * 128, [[1, 128]]))
        x1ld = sQX.n
        g.wait_ge(sQX.sem, x1ld)
        g.wait_ge(sDV.sem, ff_rd[cb])
        ins = g.indirect_dma_start(ffr[:, cb * 1024:(cb + 1) * 1024], None,
                                   ffs_d[:], IndirectOffsetOnAxis(posb[:, cb:cb + 1], 0))
        ins.then_inc(sQW.sem, 16)
        sQW.n += 16
        ffld = sQW.n
        v.wait_ge(sQX.sem, x1ld)
        v.wait_ge(sQW.sem, ffld)
        xs_ = x1sl[:, cb * 1024:(cb + 1) * 1024]
        v.tensor_scalar(ln1c[:], xs_, m1s[:, rb:rb + 1], r1s[:, rb:rb + 1],
                        ALU.subtract, ALU.mult)
        ins = v.tensor_tensor(ln1c[:], ln1c[:], xs_, ALU.add)
        x1_free[cb] = sDV.inc(ins)
        v.wait_ge(sST.sem, out_free[cb])
        ins = v.scalar_tensor_tensor(outb[:, cb * 1024:(cb + 1) * 1024],
                                     ffr[:, cb * 1024:(cb + 1) * 1024],
                                     alps[:, rb:rb + 1], ln1c[:], ALU.mult, ALU.add)
        o_done = sDV.inc(ins)
        ff_rd[cb] = o_done
        q.wait_ge(sDV.sem, o_done)
        q.dma_start(out=y_d[rb * 128:(rb + 1) * 128, :],
                    in_=outb[:, cb * 1024:(cb + 1) * 1024]).then_inc(sST.sem, 16)
        sST.n += 16
        out_free[cb] = sST.n

    q.wait_ge(sST.sem, sST.n)

    esB.close()
    es.close()
    return nc


def _host_inputs(inputs):
    x = np.ascontiguousarray(np.asarray(inputs["L0_attn_x"], dtype=np.float32))
    w_in = np.asarray(inputs["L0_attn_w_in"], dtype=np.float32)
    b_in = np.asarray(inputs["L0_attn_b_in"], dtype=np.float32)
    w_conv = np.asarray(inputs["L0_attn_w_conv"], dtype=np.float32)
    b_conv = np.asarray(inputs["L0_attn_b_conv"], dtype=np.float32)
    b1 = np.asarray(inputs["L0_attn_b1"], dtype=np.float32)
    w2 = np.asarray(inputs["L0_attn_w2"], dtype=np.float32)
    b2 = np.asarray(inputs["L0_attn_b2"], dtype=np.float32)
    w3 = np.asarray(inputs["L0_attn_w3"], dtype=np.float32)
    filt_bias = np.asarray(inputs["L0_attn_filt_bias"], dtype=np.float32)
    w_out = np.asarray(inputs["L0_attn_w_out"], dtype=np.float32)
    b_out = np.asarray(inputs["L0_attn_b_out"], dtype=np.float32)
    cent = np.asarray(inputs["L0_ffn_centroids"], dtype=np.float32)
    ln_g = np.asarray(inputs["L0_ffn_ln_g"], dtype=np.float32)
    ln_b = np.asarray(inputs["L0_ffn_ln_b"], dtype=np.float32)
    fw1 = np.asarray(inputs["L0_ffn_w1"], dtype=np.float32)
    fb1 = np.asarray(inputs["L0_ffn_b1"], dtype=np.float32)
    fw2 = np.asarray(inputs["L0_ffn_w2"], dtype=np.float32)
    fb2 = np.asarray(inputs["L0_ffn_b2"], dtype=np.float32)

    sel = np.r_[0:1024, 2048:3072]
    com = {
        "w_in_s": np.ascontiguousarray(w_in[:, sel]),
        "b_in_cm": np.ascontiguousarray(b_in[sel].reshape(16, 128).T),
        "bcv_cm": np.ascontiguousarray(b_conv[sel].reshape(16, 128).T),
        "fb_cm": np.ascontiguousarray(filt_bias.reshape(8, 128).T),
        "b1c": np.ascontiguousarray(b1.reshape(64, 1)),
        "w1fT": np.ascontiguousarray(np.asarray(inputs["L0_attn_w1"], dtype=np.float32).T),
        "w2f": np.ascontiguousarray(w2),
        "b2f": np.ascontiguousarray(b2.reshape(64, 1)),
        "w3f": np.ascontiguousarray(w3),
        "w_out": np.ascontiguousarray(w_out),
        "b_out_r": np.ascontiguousarray(b_out.reshape(1, D)),
        "cT": np.ascontiguousarray(cent.T),
        "ln_g": np.ascontiguousarray(ln_g),
        "ln_b": np.ascontiguousarray(ln_b),
        "fw1": np.ascontiguousarray(fw1),
        "fb1_cm": np.ascontiguousarray(fb1.reshape(NE, 32, 128).transpose(0, 2, 1).reshape(NE * 128, 32)),
        "fw2": np.ascontiguousarray(fw2),
        "fb2": np.ascontiguousarray(fb2),
        "id_f32": np.eye(128, dtype=np.float32),
        "id_b16": np.eye(128, dtype=np.float32),
        "sl128": np.triu(np.ones((128, 128), np.float32), 1),
        "su8": np.triu(np.ones((8, 8), np.float32), 1),
        "su32": np.triu(np.ones((32, 32), np.float32), 1),
        "ev8": np.arange(8, dtype=np.float32).reshape(8, 1),
        "ones32c": np.ones((32, 1), np.float32),
        "iota15": np.arange(15, dtype=np.float32).reshape(15, 1),
        "iota15r": np.arange(16, dtype=np.float32).reshape(1, 16),
        "evrow_in": np.arange(8, dtype=np.float32).reshape(1, 8),
        "iotapf": (np.arange(128, dtype=np.float32)[:, None]
                   + 128.0 * np.arange(32, dtype=np.float32)[None, :]),
    }
    for j in range(3):
        com[f"kc{j}"] = np.ascontiguousarray(w_conv[sel, 0, j].reshape(16, 128).T)

    in_maps = []
    for c in range(8):
        b, h = c // 2, c % 2
        start = h * NT
        if h == 0:
            halox = np.zeros((HALO, D), np.float32)
            mk = np.zeros((1, 1), np.float32)
        else:
            halox = x[b, start - HALO:start]
            mk = np.ones((1, 1), np.float32)
        m = dict(com)
        m["x0s"] = np.ascontiguousarray(np.concatenate([halox, x[b, start:start + NT]], 0))
        m["mask1"] = mk
        in_maps.append(m)
    return in_maps


def kernel(**inputs):
    in_maps = _host_inputs(inputs)
    nc = build(debug=False)
    res = run_bass_kernel_spmd(nc, in_maps, core_ids=list(range(8)))
    out = np.empty((4, 8192, D), np.float32)
    for c in range(8):
        b, h = c // 2, c % 2
        out[b, h * NT:(h + 1) * NT] = res.results[c]["y"]
    return out


# version 07
# speedup vs baseline: 1.1814x; 1.1814x over previous
import numpy as np
import ml_dtypes
from contextlib import ExitStack

import concourse.bass as bass
import concourse.mybir as mybir
from concourse.bass_utils import run_bass_kernel_spmd
from concourse.ap import AP
from concourse.bass_core_rust import IndirectOffsetOnAxis

F32 = mybir.dt.float32
BF16 = mybir.dt.bfloat16
AF = mybir.ActivationFunctionType
ALU = mybir.AluOpType
AX = mybir.AxisListType

D = 1024
NT = 4096
HALO = 1536
TOT = NT + HALO
BLK = 128
STT = 256
NST = TOT // STT
NHALO = HALO // STT
NE = 8
FF = 4096
NSLOT = 15
JT = 512
NJ = NT // JT
EPS = 1e-5


class Ctr:
    def __init__(self, sem):
        self.sem = sem
        self.n = 0

    def inc(self, ins, k=1):
        ins.then_inc(self.sem, k)
        self.n += k
        return self.n


def build(debug=False):
    nc = bass.Bass("TRN2", target_bir_lowering=False)

    x0_d = nc.dram_tensor("x0s", [TOT, D], F32, kind="ExternalInput").ap()
    msk_d = nc.dram_tensor("mask1", [1, 1], F32, kind="ExternalInput").ap()
    win_d = nc.dram_tensor("w_in_s", [D, 2048], F32, kind="ExternalInput").ap()
    bcin_d = nc.dram_tensor("b_in_cm", [128, 16], F32, kind="ExternalInput").ap()
    kc_d = [nc.dram_tensor(f"kc{j}", [128, 16], F32, kind="ExternalInput").ap() for j in range(3)]
    bcv_d = nc.dram_tensor("bcv_cm", [128, 16], F32, kind="ExternalInput").ap()
    fbc_d = nc.dram_tensor("fb_cm", [128, 8], F32, kind="ExternalInput").ap()
    b1_d = nc.dram_tensor("b1c", [64, 1], F32, kind="ExternalInput").ap()
    w1f_d = nc.dram_tensor("w1fT", [64, 3], F32, kind="ExternalInput").ap()
    w2f_d = nc.dram_tensor("w2f", [64, 64], F32, kind="ExternalInput").ap()
    b2f_d = nc.dram_tensor("b2f", [64, 1], F32, kind="ExternalInput").ap()
    w3f_d = nc.dram_tensor("w3f", [64, D], F32, kind="ExternalInput").ap()
    wout_d = nc.dram_tensor("w_out", [D, D], F32, kind="ExternalInput").ap()
    bout_d = nc.dram_tensor("b_out_r", [1, D], F32, kind="ExternalInput").ap()
    cT_d = nc.dram_tensor("cT", [D, 8], F32, kind="ExternalInput").ap()
    lng_d = nc.dram_tensor("ln_g", [8, D], F32, kind="ExternalInput").ap()
    lnb_d = nc.dram_tensor("ln_b", [8, D], F32, kind="ExternalInput").ap()
    w1_d = nc.dram_tensor("fw1", [NE, D, FF], F32, kind="ExternalInput").ap()
    fb1_d = nc.dram_tensor("fb1_cm", [NE * 128, 32], F32, kind="ExternalInput").ap()
    w2e_d = nc.dram_tensor("fw2", [NE, FF, D], F32, kind="ExternalInput").ap()
    fb2_d = nc.dram_tensor("fb2", [8, D], F32, kind="ExternalInput").ap()
    idf_d = nc.dram_tensor("id_f32", [128, 128], F32, kind="ExternalInput").ap()
    sl128_d = nc.dram_tensor("sl128", [128, 128], F32, kind="ExternalInput").ap()
    su8_d = nc.dram_tensor("su8", [8, 8], F32, kind="ExternalInput").ap()
    su32_d = nc.dram_tensor("su32", [32, 32], F32, kind="ExternalInput").ap()
    ev8_d = nc.dram_tensor("ev8", [8, 1], F32, kind="ExternalInput").ap()
    ones32_d = nc.dram_tensor("ones32c", [32, 1], F32, kind="ExternalInput").ap()
    iota15_d = nc.dram_tensor("iota15", [15, 1], F32, kind="ExternalInput").ap()
    iota15r_d = nc.dram_tensor("iota15r", [1, 16], F32, kind="ExternalInput").ap()
    evr_d = nc.dram_tensor("evrow_in", [1, 8], F32, kind="ExternalInput").ap()
    iotapf_d = nc.dram_tensor("iotapf", [128, 32], F32, kind="ExternalInput").ap()
    idb_d = nc.dram_tensor("id_b16", [128, 128], F32, kind="ExternalInput").ap()
    y_d = nc.dram_tensor("y", [NT, D], F32, kind="ExternalOutput").ap()

    dbg = "ExternalOutput" if debug else "Internal"
    hbuf_d = nc.dram_tensor("hbuf", [2048], BF16, kind=dbg).ap()
    x1_d = nc.dram_tensor("x1scr", [NT, D], F32, kind=dbg).ap()
    xln_d = nc.dram_tensor("xlnscr", [NT, D], BF16, kind=dbg).ap()
    xls_d = nc.dram_tensor("xlsort", [NSLOT * JT, D], BF16, kind=dbg).ap()
    ffs_d = nc.dram_tensor("ffsort", [NSLOT * JT, D], BF16, kind=dbg).ap()
    pos_d = nc.dram_tensor("posscr", [NT], mybir.dt.int32, kind=dbg).ap()
    sdat_d = nc.dram_tensor("sdat", [64], F32, kind=dbg).ap()
    c2_d = nc.dram_tensor("c2scr", [32, 8], F32, kind="Internal").ap()
    w1b_d = nc.dram_tensor("w1bscr", [NE * D, FF], BF16, kind="Internal").ap()
    fb2b_d = nc.dram_tensor("fb2bscr", [8, D], BF16, kind="Internal").ap()
    w2b_d = nc.dram_tensor("w2bscr", [NE * FF, D], BF16, kind="Internal").ap()
    if debug:
        dbg_d = {nm: nc.dram_tensor("dbg_" + nm, shp, dt_, kind="ExternalOutput").ap()
                 for nm, shp, dt_ in [
                     ("u2", [128, 16 * 258], F32), ("s0c", [128, 2048], F32),
                     ("vch", [128, 2048], F32), ("gb16", [128, 2048], BF16),
                     ("lnT", [128, 2048], BF16), ("ring", [128, 12 * 1024], BF16),
                     ("ydbg", [128, 2048], F32), ("x0", [128, 2048], F32),
                     ("lnb16", [128, 2048], BF16), ("stat", [128, 16], F32),
                     ("slot", [15, 32], F32), ("rk", [NT], F32), ("cs", [NT], F32),
                     ("oh", [128, 256], F32), ("w1g", [128, FF], BF16),
                     ("xT0", [128, 512], BF16), ("h10", [128, 32 * 512], BF16),
                     ("o0", [128, 1024], BF16), ("w2g0", [128, 8192], BF16),
                     ("ps0", [128, 512], F32)]}

    es = ExitStack()

    def sb(name, shape, dt):
        return es.enter_context(nc.sbuf_tensor(name, shape, dt))

    sQX = Ctr(es.enter_context(nc.semaphore("sQX")))
    sQW = Ctr(es.enter_context(nc.semaphore("sQW")))
    sPE = Ctr(es.enter_context(nc.semaphore("sPE")))
    sDV = Ctr(es.enter_context(nc.semaphore("sDV")))
    sAC = Ctr(es.enter_context(nc.semaphore("sAC")))
    sGP = Ctr(es.enter_context(nc.semaphore("sGP")))
    sST = Ctr(es.enter_context(nc.semaphore("sST")))

    idf = sb("idf", [128, 128], F32)
    idb = sb("idb", [128, 128], BF16)
    cTs = sb("cTs", [128, 64], F32)
    lngb = sb("lngb", [8, D], BF16)
    lnbb = sb("lnbb", [8, D], BF16)
    ohsb = sb("ohsb", [128, 256], F32)
    ohsb16 = sb("ohsb16", [128, 256], BF16)
    alps = sb("alps", [128, 32], F32)
    m1s = sb("m1s", [128, 32], F32)
    r1s = sb("r1s", [128, 32], F32)
    mskc = sb("mskc", [128, 1], F32)
    onesk = sb("onesk", [1, 128], BF16)
    boutb = sb("boutb", [1, D], BF16)
    mcol = sb("mcol", [128, 1], F32)
    sscol = sb("sscol", [128, 1], F32)
    ecol = sb("ecol", [128, 1], F32)
    tcol = sb("tcol", [128, 1], F32)
    vcol = sb("vcol", [128, 1], F32)
    rcol = sb("rcol", [128, 1], F32)
    m2col = sb("m2col", [128, 1], F32)
    r2col = sb("r2col", [128, 1], F32)
    affs = sb("affs", [128, 256], F32)
    amax = sb("amax", [128, 1], F32)
    ohblk = sb("ohblk", [128, 8], F32)
    tm4 = sb("tm4", [128, 6], F32)
    ohTf = sb("ohTf", [8, 128], F32)
    ohTb = sb("ohTb", [8, 128], BF16)
    iw1a = sb("iw1a", [128, 15 * 8], mybir.dt.int32)
    iw2a = sb("iw2a", [128, 15 * 32], mybir.dt.int32)
    ib1a = sb("ib1a", [128, 15], mybir.dt.int32)
    iesa = sb("iesa", [128, 15], mybir.dt.int32)
    statd = sb("statd", [128, 16], F32)
    bnst = sb("bnst", [128, 12], F32)
    bnagg = sb("bnagg", [128, 2], F32)

    t, v, s, g, q = nc.tensor, nc.vector, nc.scalar, nc.gpsimd, nc.sync

    def qdma(out, in_):
        q.dma_start(out=out, in_=in_).then_inc(sQX.sem, 16)
        sQX.n += 16
        return sQX.n

    def gdma(out, in_):
        g.dma_start(out=out, in_=in_).then_inc(sQW.sem, 16)
        sQW.n += 16
        return sQW.n

    def barrier():
        marks = [(sPE, t), (sDV, v), (sAC, s)]
        for c, eng in marks:
            c.inc(eng.nop())
        for c, eng in marks:
            for c2, _ in marks:
                if c2 is not c:
                    eng.wait_ge(c2.sem, c2.n)

    def ln_stats(src_ap, sq_sink, mean_col, rstd_col, dv_ready=0):
        v.bn_stats(bnst[:, 0:6], src_ap[:, 0:512])
        ins = v.bn_stats(bnst[:, 6:12], src_ap[:, 512:1024])
        sDV.inc(ins)
        v.wait_ge(sDV.sem, sDV.n)
        ins = v.bn_aggr(bnagg[:], bnst[:])
        sDV.inc(ins)
        v.wait_ge(sDV.sem, sDV.n)
        ins = v.tensor_copy(mean_col, bnagg[:, 0:1])
        sDV.inc(ins)
        v.wait_ge(sDV.sem, sDV.n)
        ins = v.tensor_scalar_add(vcol[:], bnagg[:, 1:2], EPS)
        sDV.inc(ins)
        s.wait_ge(sDV.sem, sDV.n)
        ins = s.activation(rcol[:], vcol[:], AF.Ln)
        sAC.inc(ins)
        s.wait_ge(sAC.sem, sAC.n)
        ins = s.activation(rstd_col, rcol[:], AF.Exp, scale=-0.5)
        sAC.inc(ins)
        v.wait_ge(sAC.sem, sAC.n)

    esA = ExitStack()

    def sba(name, shape, dt):
        return esA.enter_context(nc.sbuf_tensor(name, shape, dt))

    winsb = sba("winsb", [128, 8 * 2048], BF16)
    woutsb = sba("woutsb", [128, 8 * 1024], BF16)
    Rsb = sba("Rsb", [128, 10 * 256], BF16)
    ring = sba("ring", [128, 12 * 1024], BF16)
    x0t = sba("x0t", [128, 2 * 2048], F32)
    lnb16 = sba("lnb16", [128, 2048], BF16)
    lnT = sba("lnT", [128, 8 * 256], BF16)
    u2 = sba("u2", [128, 16 * 258], F32)
    s0c = sba("s0c", [128, 2 * 8 * 256], F32)
    vch = sba("vch", [128, 2 * 8 * 256], F32)
    vrev = sba("vrev", [128, 16 * 128], BF16)
    gb16 = sba("gb16", [128, 8 * 256], BF16)
    tmpA = sba("tmpA", [128, 256], F32)
    tmpB = sba("tmpB", [128, 256], F32)
    tmpC = sba("tmpC", [128, 256], F32)
    x1t = sba("x1t", [128, 1024], F32)
    ln1t = sba("ln1t", [128, 1024], F32)
    xlnb = sba("xlnb", [128, 1024], BF16)
    wcb = sba("wcb", [128, 4096], BF16)
    kcs = [sba(f"kcs{j}", [128, 16], F32) for j in range(3)]
    bcvs = sba("bcvs", [128, 16], F32)
    bcins = sba("bcins", [128, 16], F32)
    bcinm = sba("bcinm", [128, 16], F32)
    fbcs = sba("fbcs", [128, 8], F32)
    b1s = sba("b1s", [64, 1], F32)
    w1fs = sba("w1fs", [64, 3], F32)
    w1ss = sba("w1ss", [64, 1], F32)
    w2fs = sba("w2fs", [64, 64], F32)
    b2fs = sba("b2fs", [64, 1], F32)
    w3fs = sba("w3fs", [64, D], F32)
    r1f = sba("r1f", [64, 1], F32)
    r2f = sba("r2f", [64, 1], F32)
    hcol = sba("hcol", [128, 8], F32)
    h8 = sba("h8", [8, 128], F32)
    h8b = sba("h8b", [8, 128], BF16)
    z8 = sba("z8", [8, 256], BF16)
    fb2bb = sba("fb2bb", [8, D], BF16)

    ln1T8 = sba("ln1T8", [128, 1024], F32)

    psA = [esA.enter_context(nc.psum_tensor(f"psA{i}", [128, 512], F32)) for i in range(6)]
    ptA = [esA.enter_context(nc.psum_tensor(f"ptA{i}", [128, 1024], BF16)) for i in range(2)]

    blkA = esA.enter_context(nc.Block())

    qdma(idf[:], idf_d[:])
    for k in range(8):
        qdma(cTs[:, k * 8:(k + 1) * 8], cT_d[k * 128:(k + 1) * 128, :])
    for j in range(3):
        qdma(kcs[j][:], kc_d[j][:])
    qdma(bcvs[:], bcv_d[:])
    qdma(bcins[:], bcin_d[:])
    qdma(fbcs[:], fbc_d[:])
    qdma(b1s[:], b1_d[:])
    qdma(w1fs[:], w1f_d[:])
    qdma(w2fs[:], w2f_d[:])
    qdma(b2fs[:], b2f_d[:])
    qdma(w3fs[:], w3f_d[:])
    qdma(mskc[:], AP(msk_d.tensor, 0, [[0, 128], [1, 1]]))
    gdma(idb[:], idb_d[:])
    gdma(fb2bb[:], fb2_d[:])
    gdma(lngb[:], lng_d[:])
    gdma(lnbb[:], lnb_d[:])
    gdma(boutb[:], bout_d[:])
    for k in range(8):
        gdma(winsb[:, k * 2048:(k + 1) * 2048], win_d[k * 128:(k + 1) * 128, :])
        gdma(woutsb[:, k * 1024:(k + 1) * 1024], wout_d[k * 128:(k + 1) * 128, :])
    setup_qw = sQW.n
    setup_qx = sQX.n

    v.memset(u2[:], 0.0)
    v.memset(onesk[:], 1.0)
    ins = v.memset(z8[:], 0.0)
    sDV.inc(ins)
    v.wait_ge(sQX.sem, setup_qx)
    ins = v.tensor_scalar(bcinm[:], bcins[:], mskc[:], None, ALU.mult)
    sDV.inc(ins)

    v.wait_ge(sQX.sem, setup_qx)
    ins = v.reduce_sum(w1ss[:], w1fs[:], axis=AX.X)
    sDV.inc(ins)
    v.wait_ge(sDV.sem, sDV.n)
    ins = v.scalar_tensor_tensor(w1ss[:], w1ss[:], 0.0, b1s[:], ALU.mult, ALU.add)
    sDV.inc(ins)
    s.wait_ge(sDV.sem, sDV.n)
    ins = s.activation(r1f[:], w1ss[:], AF.Silu)
    sAC.inc(ins)
    t.wait_ge(sAC.sem, sAC.n)
    t.matmul(psA[0][0:64, 0:1], w2fs[:], r1f[:], start=True, stop=True)
    ins = t.matmul(psA[3][:, 0:128], idb[:], idb[:], start=True, stop=True)
    sPE.inc(ins)
    s.wait_ge(sPE.sem, sPE.n)
    ins = s.activation(r2f[:], psA[0][0:64, 0:1], AF.Silu, bias=b2fs[:])
    sAC.inc(ins)
    t.wait_ge(sAC.sem, sAC.n)
    for c in range(8):
        ins = t.matmul(psA[1][:, c:c + 1], w3fs[:, c * 128:(c + 1) * 128], r2f[:],
                       start=True, stop=True)
    ins = t.matmul(psA[3][:, 0:128], idb[:], idb[:], start=True, stop=True)
    sPE.inc(ins)
    v.wait_ge(sPE.sem, sPE.n)
    ins = v.tensor_copy(hcol[:], psA[1][:, 0:8])
    sDV.inc(ins)
    t.wait_ge(sDV.sem, sDV.n)
    t.transpose(psA[2][0:8, 0:128], hcol[:], idf[:])
    ins = t.matmul(psA[3][:, 0:128], idb[:], idb[:], start=True, stop=True)
    sPE.inc(ins)
    v.wait_ge(sPE.sem, sPE.n)
    v.tensor_copy(h8[:], psA[2][0:8, 0:128])
    ins = v.tensor_copy(h8b[:], h8[:])
    sDV.inc(ins)
    q.wait_ge(sDV.sem, sDV.n)
    q.wait_ge(sQW.sem, setup_qw)
    qdma(fb2b_d[:], fb2bb[:])
    qdma(AP(hbuf_d.tensor, 0, [[256, 8], [1, 256]]), z8[:])
    q.wait_ge(sQX.sem, sQX.n)
    qdma(AP(hbuf_d.tensor, 512, [[128, 8], [1, 128]]), h8b[:])
    q.wait_ge(sQX.sem, sQX.n)
    for ip, p in enumerate(range(-1, 9)):
        qdma(Rsb[:, ip * 256:(ip + 1) * 256],
             AP(hbuf_d.tensor, 512 + 128 * p - 127, [[1, 128], [1, 256]]))
    setup_qx2 = sQX.n

    x0buf = [x0t[:, 0:2048], x0t[:, 2048:4096]]
    for b in range(2):
        qdma(x0buf[0][:, b * 1024:(b + 1) * 1024], x0_d[b * 128:(b + 1) * 128, :])
    x0_ready = [sQX.n]

    for eng in (t, v, s, g):
        eng.wait_ge(sQX.sem, setup_qx2)
        eng.wait_ge(sQW.sem, setup_qw)

    evac_u = [0, 0]
    evac_pt = [0, 0]
    evac_y = [0, 0]
    evac_w = [0, 0]
    x0_free = [0, 0]
    ring_rd = 0
    tr_i = 0
    prev_stores = 0

    def transpose_b16(src_ap, dst_ap):
        nonlocal tr_i
        i = tr_i % 2
        tr_i += 1
        t.wait_ge(sAC.sem, evac_pt[i])
        ins = t.transpose(ptA[i][:, 0:128], src_ap, idb[:])
        sPE.inc(ins)
        s.wait_ge(sPE.sem, sPE.n)
        ins = s.copy(dst_ap, ptA[i][:, 0:128])
        evac_pt[i] = sAC.inc(ins)

    def transpose_f32(src_ap, dst_ap, bank, width=128):
        if bank < 2:
            t.wait_ge(sAC.sem, evac_u[bank])
        ins = t.transpose(psA[bank][0:width, 0:128], src_ap, idf[:])
        sPE.inc(ins)
        s.wait_ge(sPE.sem, sPE.n)
        ins = s.copy(dst_ap, psA[bank][0:width, 0:128])
        if bank < 2:
            evac_u[bank] = sAC.inc(ins)
        else:
            sAC.inc(ins)

    dv_chunks_prev = 0

    def prefetch_x0(tgt):
        nxt = tgt % 2
        q.wait_ge(sDV.sem, x0_free[nxt])
        for b in range(2):
            r0 = tgt * STT + b * BLK
            qdma(x0buf[nxt][:, b * 1024:(b + 1) * 1024], x0_d[r0:r0 + 128, :])
        x0_ready.append(sQX.n)

    def front(st_i):
        nonlocal dv_chunks_prev
        halo = st_i < NHALO
        cur = st_i % 2
        if st_i + 1 <= NHALO + 1 and st_i + 1 < NST:
            prefetch_x0(st_i + 1)

        v.wait_ge(sQX.sem, x0_ready[st_i])
        s.wait_ge(sQX.sem, x0_ready[st_i])
        for b in range(2):
            xb = x0buf[cur][:, b * 1024:(b + 1) * 1024]
            ln_stats(xb, None, mcol[:], rcol[:])
            ins = v.tensor_scalar(lnb16[:, b * 1024:(b + 1) * 1024], xb, mcol[:], rcol[:],
                                  ALU.subtract, ALU.mult)
            sDV.inc(ins)
        lnb16_done = sDV.n
        t.wait_ge(sDV.sem, lnb16_done)
        for b in range(2):
            for k in range(8):
                transpose_b16(lnb16[:, b * 1024 + k * 128: b * 1024 + (k + 1) * 128],
                              lnT[:, k * 256 + b * 128: k * 256 + (b + 1) * 128])
        lnT_done = sAC.n

        s0cb = s0c[:, cur * 2048:(cur + 1) * 2048]
        vchb = vch[:, cur * 2048:(cur + 1) * 2048]
        for c in range(16):
            pi = c % 2
            t.wait_ge(sAC.sem, max(lnT_done, evac_u[pi]))
            for k in range(8):
                ins = t.matmul(psA[pi][:, 0:256],
                               winsb[:, k * 2048 + c * 128: k * 2048 + (c + 1) * 128],
                               lnT[:, k * 256:(k + 1) * 256],
                               start=(k == 0), stop=(k == 7))
            mm_done = sPE.inc(ins)
            s.wait_ge(sPE.sem, mm_done)
            if c == 0:
                s.wait_ge(sDV.sem, dv_chunks_prev)
            uc = u2[:, c * 258:(c + 1) * 258]
            if halo:
                ins = s.activation(uc[:, 2:258], psA[pi][:, 0:256], AF.Identity,
                                   scale=mskc[:], bias=bcinm[:, c:c + 1])
            else:
                ins = s.activation(uc[:, 2:258], psA[pi][:, 0:256], AF.Identity,
                                   bias=bcins[:, c:c + 1])
            evac_u[pi] = sAC.inc(ins)
            v.wait_ge(sAC.sem, evac_u[pi])
            v.tensor_scalar(tmpA[:], uc[:, 2:258], kcs[2][:, c:c + 1], bcvs[:, c:c + 1],
                            ALU.mult, ALU.add)
            ins = v.scalar_tensor_tensor(tmpB[:], uc[:, 1:257], kcs[1][:, c:c + 1], tmpA[:],
                                         ALU.mult, ALU.add)
            sDV.inc(ins)
            if c < 8:
                ins = v.scalar_tensor_tensor(s0cb[:, c * 256:(c + 1) * 256],
                                             uc[:, 0:256], kcs[0][:, c:c + 1], tmpB[:],
                                             ALU.mult, ALU.add)
                sDV.inc(ins)
            else:
                cc = c - 8
                vd = vchb[:, cc * 256:(cc + 1) * 256]
                ins = v.scalar_tensor_tensor(tmpA[:], uc[:, 0:256], kcs[0][:, c:c + 1], tmpB[:],
                                             ALU.mult, ALU.add)
                sDV.inc(ins)
                ins = v.tensor_tensor(vd, tmpA[:], s0cb[:, cc * 256:(cc + 1) * 256], ALU.mult)
                sDV.inc(ins)
                for b in range(2):
                    seg = vd[:, b * 128:(b + 1) * 128]
                    vcol_ = vrev[:, (cc * 2 + b) * 128:(cc * 2 + b + 1) * 128]
                    if halo:
                        ins = v.tensor_scalar(vcol_, seg[:, ::-1], mskc[:], None, ALU.mult)
                    else:
                        ins = v.tensor_copy(vcol_, seg[:, ::-1])
                    vrev_done = sDV.inc(ins)
                    slot = (2 * st_i + b) % 12
                    t.wait_ge(sDV.sem, vrev_done)
                    transpose_b16(vcol_,
                                  ring[:, slot * 1024 + cc * 128: slot * 1024 + (cc + 1) * 128])
            ins = v.tensor_copy(uc[:, 0:2], uc[:, 256:258])
            sDV.inc(ins)
        dv_chunks_prev = sDV.n
        if halo:
            x0_free[cur] = sDV.n
        return sAC.n

    def back(st_i, ring_wr):
        nonlocal prev_stores
        cur = st_i % 2
        s0cb = s0c[:, cur * 2048:(cur + 1) * 2048]
        vchb = vch[:, cur * 2048:(cur + 1) * 2048]
        t.wait_ge(sAC.sem, ring_wr)
        for cc in range(8):
            yi = 2 + (cc % 2)
            t.wait_ge(sDV.sem, evac_y[cc % 2])
            for ip, p in enumerate(range(-1, 9)):
                slot = (2 * st_i - p) % 12
                ins = t.matmul(psA[yi][:, 0:256],
                               ring[:, slot * 1024 + cc * 128: slot * 1024 + (cc + 1) * 128],
                               Rsb[:, ip * 256:(ip + 1) * 256],
                               start=(ip == 0), stop=(ip == 9))
            y_done = sPE.inc(ins)
            v.wait_ge(sPE.sem, y_done)
            ins = v.scalar_tensor_tensor(tmpC[:], vchb[:, cc * 256:(cc + 1) * 256],
                                         fbcs[:, cc:cc + 1], psA[yi][:, 0:256],
                                         ALU.mult, ALU.add)
            sDV.inc(ins)
            ins = v.tensor_tensor(gb16[:, cc * 256:(cc + 1) * 256], tmpC[:],
                                  s0cb[:, cc * 256:(cc + 1) * 256], ALU.mult)
            evac_y[cc % 2] = sDV.inc(ins)
        g_done = sDV.n

        for b in range(2):
            rb = 2 * st_i + b - 12
            v.wait_ge(sQX.sem, prev_stores)
            s.wait_ge(sQX.sem, prev_stores)
            t.wait_ge(sDV.sem, g_done)
            for grp in range(2):
                t.wait_ge(sDV.sem, evac_w[grp])
                for cc in range(8):
                    ins = t.matmul(psA[4 + grp][:, 0:512],
                                   gb16[:, cc * 256 + b * 128: cc * 256 + (b + 1) * 128],
                                   woutsb[:, cc * 1024 + grp * 512: cc * 1024 + (grp + 1) * 512],
                                   start=(cc == 0), stop=False)
                ins = t.matmul(psA[4 + grp][:, 0:512], onesk[:],
                               boutb[0:1, grp * 512:(grp + 1) * 512], start=False, stop=True)
                w_done = sPE.inc(ins)
                v.wait_ge(sPE.sem, w_done)
                ins = v.tensor_tensor(x1t[:, grp * 512:(grp + 1) * 512],
                                      x0buf[cur][:, b * 1024 + grp * 512: b * 1024 + (grp + 1) * 512],
                                      psA[4 + grp][:, 0:512], ALU.add)
                evac_w[grp] = sDV.inc(ins)
            x1_done = sDV.n
            if b == 1:
                x0_free[cur] = x1_done
            q.wait_ge(sDV.sem, x1_done)
            qdma(x1_d[rb * 128:(rb + 1) * 128, :], x1t[:])

            ln_stats(x1t[:], None, m2col[:], r2col[:], dv_ready=x1_done)
            v.tensor_copy(m1s[:, rb:rb + 1], m2col[:])
            v.tensor_copy(r1s[:, rb:rb + 1], r2col[:])
            ins = v.tensor_scalar(ln1t[:], x1t[:], m2col[:], r2col[:], ALU.subtract, ALU.mult)
            ln1_done = sDV.inc(ins)

            t.wait_ge(sDV.sem, ln1_done)
            for k in range(8):
                transpose_f32(ln1t[:, k * 128:(k + 1) * 128],
                              ln1T8[:, k * 128:(k + 1) * 128], k % 2)
            lt_done = sAC.n
            t.wait_ge(sAC.sem, lt_done)
            t.wait_ge(sDV.sem, evac_w[0])
            for k in range(8):
                ins = t.matmul(psA[4][:, 0:8], ln1T8[:, k * 128:(k + 1) * 128],
                               cTs[:, k * 8:(k + 1) * 8], start=(k == 0), stop=(k == 7))
            t.wait_ge(sDV.sem, evac_w[1])
            ins = t.matmul(psA[5][:, 0:128], idb[:], idb[:], start=True, stop=True)
            aff_done = sPE.inc(ins)
            s.wait_ge(sPE.sem, aff_done)
            ins = s.copy(affs[:], psA[4][:, 0:256])
            aff_ev = sAC.inc(ins)
            v.wait_ge(sAC.sem, aff_ev)
            ins = v.tensor_tensor(tm4[:, 0:4], affs[:, 0:4], affs[:, 4:8], ALU.max)
            sDV.inc(ins)
            v.wait_ge(sDV.sem, sDV.n)
            ins = v.tensor_tensor(tm4[:, 4:6], tm4[:, 0:2], tm4[:, 2:4], ALU.max)
            sDV.inc(ins)
            v.wait_ge(sDV.sem, sDV.n)
            ins = v.tensor_tensor(amax[:], tm4[:, 4:5], tm4[:, 5:6], ALU.max)
            sDV.inc(ins)
            v.wait_ge(sDV.sem, sDV.n)
            ins = v.tensor_scalar(ohblk[:], affs[:, 0:8], amax[:], None, ALU.is_equal)
            oh_done = sDV.inc(ins)
            v.wait_ge(sDV.sem, sDV.n)
            v.tensor_copy(alps[:, rb:rb + 1], amax[:])
            v.tensor_copy(ohsb[:, rb * 8:(rb + 1) * 8], ohblk[:])
            ins = v.tensor_copy(ohsb16[:, rb * 8:(rb + 1) * 8], ohblk[:])
            sDV.inc(ins)
            t.wait_ge(sDV.sem, oh_done)
            t.wait_ge(sDV.sem, evac_y[1])
            transpose_f32(ohblk[:], ohTf[:], 3, width=8)
            ohT_ev = sAC.n
            v.wait_ge(sAC.sem, ohT_ev)
            ins = v.tensor_copy(ohTb[:], ohTf[:])
            ohtb_done = sDV.inc(ins)

            ln_stats(ln1t[:], None, m2col[:], r2col[:], dv_ready=ln1_done)
            ins = v.tensor_scalar(ln1t[:], ln1t[:], m2col[:], r2col[:], ALU.subtract, ALU.mult)
            xn_done = sDV.inc(ins)

            t.wait_ge(sDV.sem, ohtb_done)
            for grp in range(2):
                t.wait_ge(sDV.sem, evac_w[grp])
                ins = t.matmul(psA[4 + grp][:, 0:512], ohTb[:],
                               lngb[0:8, grp * 512:(grp + 1) * 512], start=True, stop=True)
                sPE.inc(ins)
            for grp in range(2):
                t.wait_ge(sDV.sem, evac_y[grp])
                ins = t.matmul(psA[2 + grp][:, 0:512], ohTb[:],
                               lnbb[0:8, grp * 512:(grp + 1) * 512], start=True, stop=True)
                sPE.inc(ins)
            gains_done = sPE.n
            v.wait_ge(sPE.sem, gains_done)
            v.wait_ge(sDV.sem, xn_done)
            for grp in range(2):
                sl = slice(grp * 512, (grp + 1) * 512)
                v.tensor_tensor(ln1t[:, sl], ln1t[:, sl], psA[4 + grp][:, 0:512], ALU.mult)
                ins = v.tensor_tensor(ln1t[:, sl], ln1t[:, sl], psA[2 + grp][:, 0:512], ALU.add)
                evac_w[grp] = sDV.inc(ins)
                evac_y[grp] = evac_w[grp]
            ins = v.tensor_copy(xlnb[:], ln1t[:])
            xlnb_done = sDV.inc(ins)
            q.wait_ge(sDV.sem, xlnb_done)
            qdma(xln_d[rb * 128:(rb + 1) * 128, :], xlnb[:])
            prev_stores = sQX.n
        if st_i + 2 < NST:
            prefetch_x0(st_i + 2)

    ring_wrs = {}
    ring_wrs[0] = front(0)
    for st_i in range(NST):
        barrier()
        if st_i + 1 < NST:
            ring_wrs[st_i + 1] = front(st_i + 1)
        if st_i >= NHALO:
            back(st_i, ring_wrs[st_i])

    pc_jobs = []
    for e in range(NE):
        for k in range(8):
            off = (e * 1024 + k * 128) * FF
            pc_jobs.append((AP(w1_d.tensor, off, [[FF, 128], [1, FF]]),
                            AP(w1b_d.tensor, off, [[FF, 128], [1, FF]])))
    for e in range(NE):
        for j in range(8):
            off = (e * FF + j * 512) * D
            pc_jobs.append((AP(w2e_d.tensor, off, [[4096, 128], [1, 4096]]),
                            AP(w2b_d.tensor, off, [[4096, 128], [1, 4096]])))
    pc_out = 0
    for pi_, (srcap, dstap) in enumerate(pc_jobs):
        g.wait_ge(sQW.sem, pc_out)
        lin = gdma(wcb[:], srcap)
        g.wait_ge(sQW.sem, lin)
        pc_out = gdma(dstap, wcb[:])
    precast_done = sQW.n

    phaseA_stores = sQX.n
    esA.close()

    esS = ExitStack()

    def sbs(name, shape, dt):
        return esS.enter_context(nc.sbuf_tensor(name, shape, dt))

    sl128 = sbs("sl128s", [128, 128], BF16)
    su8 = sbs("su8s", [8, 8], F32)
    su32 = sbs("su32s", [32, 32], F32)
    ev8 = sbs("ev8s", [8, 1], F32)
    ones32 = sbs("ones32s", [32, 1], F32)
    iota15 = sbs("iota15s", [15, 1], F32)
    iotapf = sbs("iotapfs", [128, 32], F32)
    onec = sbs("onec", [128, 1], BF16)
    bcT = sbs("bcT", [8, 32], F32)
    bc32 = sbs("bc32", [32, 8], F32)
    csum = sbs("csumS", [32, 8], F32)
    n8 = sbs("n8", [8, 1], F32)
    m8 = sbs("m8", [8, 1], F32)
    t8a = sbs("t8a", [8, 1], F32)
    S8 = sbs("S8", [8, 1], F32)
    SM8 = sbs("SM8", [8, 1], F32)
    S5r = sbs("S5r", [1, 8], F32)
    r83 = sbs("r83", [8, 2], F32)
    Stile = sbs("Stile", [15, 8], F32)
    SMtile = sbs("SMtile", [15, 8], F32)
    both = sbs("both", [15, 8], F32)
    bothT = sbs("bothTs", [8, 16], F32)
    i15r = sbs("i15r", [8, 16], F32)
    evrow = sbs("evrow", [15, 8], F32)
    geT = sbs("geT", [8, 16], F32)
    ltT = sbs("ltT", [8, 16], F32)
    slotm = sbs("slotm", [16, 2], F32)
    rk8 = sbs("rk8", [128, 8], F32)
    rkev = sbs("rkev", [128, 256], F32)
    c2t = sbs("c2t", [128, 8], F32)
    selc = sbs("selc", [128, 1], F32)
    posf = sbs("posf", [128, 1], F32)
    posi = sbs("posi", [128, 4], mybir.dt.int32)
    xgs = sbs("xgs", [128, 4 * 1024], BF16)
    c2all = sbs("c2all", [128, 256], F32)
    escols = sbs("escols", [128, 15], F32)
    tmpe = sbs("tmpe", [128, 1], F32)
    tmpc8 = sbs("tmpc8", [128, 8], F32)
    onesf = sbs("onesf", [1, 32], F32)

    psS = [esS.enter_context(nc.psum_tensor(f"psS{i2}", [128, 512], F32)) for i2 in range(4)]

    blkS = esS.enter_context(nc.Block())
    barrier()
    ins = s.activation(alps[:], alps[:], AF.Sigmoid)
    sAC.inc(ins)

    qdma(su8[:], su8_d[:])
    qdma(su32[:], su32_d[:])
    qdma(ev8[:], ev8_d[:])
    qdma(ones32[:], ones32_d[:])
    qdma(iota15[:], iota15_d[:])
    qdma(i15r[:], AP(iota15r_d.tensor, 0, [[0, 8], [1, 16]]))
    qdma(iotapf[:], iotapf_d[:])
    gdma(sl128[:], sl128_d[:])
    sload = sQX.n
    sloadw = sQW.n
    ins = v.memset(onec[:], 1.0)
    sDV.inc(ins)
    v.wait_ge(sDV.sem, sDV.n)
    ins = v.memset(onesf[:], 1.0)
    sDV.inc(ins)
    for eng in (t, v, s):
        eng.wait_ge(sQX.sem, sload)
        eng.wait_ge(sQW.sem, sloadw)
        eng.wait_ge(sDV.sem, sDV.n)

    bc_ev = [0, 0]
    for rb in range(32):
        pb = rb % 2
        t.wait_ge(sAC.sem, bc_ev[pb])
        ins = t.matmul(psS[pb][0:8, 0:1], ohsb16[:, rb * 8:(rb + 1) * 8], onec[:],
                       start=True, stop=True)
        ins = t.matmul(psS[3][:, 128:256], idb[:], idb[:], start=True, stop=True)
        mm = sPE.inc(ins)
        s.wait_ge(sPE.sem, mm)
        ins = s.copy(bcT[:, rb:rb + 1], psS[pb][0:8, 0:1])
        bc_ev[pb] = sAC.inc(ins)
    t.wait_ge(sAC.sem, sAC.n)
    t.transpose(psS[2][0:32, 0:8], bcT[:], idf[0:8, 0:8])
    ins = t.matmul(psS[3][:, 128:256], idb[:], idb[:], start=True, stop=True)
    sPE.inc(ins)
    s.wait_ge(sPE.sem, sPE.n)
    ins = s.copy(bc32[:], psS[2][0:32, 0:8])
    sAC.inc(ins)
    t.wait_ge(sAC.sem, sAC.n)
    ins = t.matmul(psS[0][0:32, 0:8], su32[:], bc32[:], start=True, stop=False)
    sPE.inc(ins)
    t.matmul(psS[1][0:8, 0:1], bc32[:], ones32[:], start=True, stop=True)
    ins = t.matmul(psS[3][:, 128:256], idb[:], idb[:], start=True, stop=True)
    sPE.inc(ins)
    s.wait_ge(sPE.sem, sPE.n)
    ins = s.copy(n8[:], psS[1][0:8, 0:1])
    n8_ev = sAC.inc(ins)
    v.wait_ge(sAC.sem, n8_ev)
    ins = v.memset(m8[:], 0.0)
    sDV.inc(ins)
    v.wait_ge(sDV.sem, sDV.n)
    for jj in range(8):
        ins = v.tensor_scalar(t8a[:], n8[:], float(512 * jj), None, ALU.is_gt)
        sDV.inc(ins)
        v.wait_ge(sDV.sem, sDV.n)
        ins = v.tensor_tensor(m8[:], m8[:], t8a[:], ALU.add)
    m8_done = sDV.inc(ins)
    v.wait_ge(sDV.sem, sDV.n)
    t.wait_ge(sDV.sem, m8_done)
    ins = t.matmul(psS[1][0:8, 1:2], su8[:], m8[:], start=True, stop=True)
    sPE.inc(ins)
    v.wait_ge(sPE.sem, sPE.n)
    ins = v.tensor_copy(S8[:], psS[1][0:8, 1:2])
    sDV.inc(ins)
    v.wait_ge(sDV.sem, sDV.n)
    ins = v.tensor_tensor(SM8[:], psS[1][0:8, 1:2], m8[:], ALU.add)
    S8_done = sDV.inc(ins)
    v.wait_ge(sDV.sem, sDV.n)
    t.wait_ge(sDV.sem, S8_done)
    t.transpose(psS[3][0:1, 0:8], S8[:], idf[0:8, 0:8])
    ins = t.matmul(psS[2][:, 128:256], idb[:], idb[:], start=True, stop=True)
    sPE.inc(ins)
    v.wait_ge(sPE.sem, sPE.n)
    ins = v.tensor_scalar_mul(S5r[:], psS[3][0:1, 0:8], 512.0)
    s5_done = sDV.inc(ins)
    v.wait_ge(sDV.sem, sDV.n)
    t.wait_ge(sDV.sem, s5_done)
    t.matmul(psS[0][0:32, 0:8], onesf[:], S5r[:], start=False, stop=True)
    ins = t.matmul(psS[3][:, 128:256], idb[:], idb[:], start=True, stop=True)
    c2mm = sPE.inc(ins)
    s.wait_ge(sPE.sem, c2mm)
    ins = s.copy(csum[:], psS[0][0:32, 0:8])
    c2_ev = sAC.inc(ins)
    q.wait_ge(sDV.sem, S8_done)
    qdma(AP(sdat_d.tensor, 0, [[1, 8]]), S8[:])
    qdma(AP(sdat_d.tensor, 8, [[1, 8]]), SM8[:])
    q.wait_ge(sAC.sem, c2_ev)
    qdma(c2_d[:], csum[:])
    q.wait_ge(sQX.sem, sQX.n)
    qdma(Stile[:], AP(sdat_d.tensor, 0, [[0, 15], [1, 8]]))
    qdma(SMtile[:], AP(sdat_d.tensor, 8, [[0, 15], [1, 8]]))
    stile_ld = sQX.n
    v.wait_ge(sQX.sem, stile_ld)
    ins = v.tensor_scalar(Stile[:], Stile[:], iota15[:], None, ALU.is_le)
    sDV.inc(ins)
    v.wait_ge(sDV.sem, sDV.n)
    ins = v.tensor_scalar(SMtile[:], SMtile[:], iota15[:], None, ALU.is_gt)
    sDV.inc(ins)
    v.wait_ge(sDV.sem, sDV.n)
    ins = v.tensor_tensor(both[:], Stile[:], SMtile[:], ALU.mult)
    both_done = sDV.inc(ins)
    v.wait_ge(sDV.sem, sDV.n)
    ins = v.tensor_scalar(geT[:], i15r[:], S8[:], None, ALU.is_ge)
    sDV.inc(ins)
    v.wait_ge(sDV.sem, sDV.n)
    ins = v.tensor_scalar(ltT[:], i15r[:], SM8[:], None, ALU.is_lt)
    sDV.inc(ins)
    v.wait_ge(sDV.sem, sDV.n)
    ins = v.tensor_tensor(bothT[:], geT[:], ltT[:], ALU.mult)
    bothT_ev = sDV.inc(ins)
    v.wait_ge(sDV.sem, sDV.n)
    t.wait_ge(sDV.sem, bothT_ev)
    qdma(evrow[:], AP(evr_d.tensor, 0, [[0, 15], [1, 8]]))
    q.wait_ge(sQX.sem, sQX.n)
    v.wait_ge(sQX.sem, sQX.n)
    ins = v.tensor_tensor(evrow[:], evrow[:], both[:], ALU.mult)
    sDV.inc(ins)
    v.wait_ge(sDV.sem, sDV.n)
    ins = v.reduce_sum(slotm[0:15, 0:1], evrow[:], axis=AX.X)
    slotm_ev = sDV.inc(ins)
    v.wait_ge(sDV.sem, sDV.n)
    if debug:
        v.wait_ge(sDV.sem, slotm_ev)
        ins = v.tensor_copy(both[:], both[:])
        sDV.inc(ins)
        q.wait_ge(sDV.sem, sDV.n)
        qdma(dbg_d["slot"][:, 0:8], Stile[:])
        qdma(dbg_d["slot"][:, 8:16], SMtile[:])

        q.dma_start(out=dbg_d["slot"][0:8, 16:32], in_=bothT[:], allow_slow_non_contiguous=True).then_inc(sQX.sem, 16); sQX.n += 16
        q.dma_start(out=dbg_d["stat"][0:15, 3:5], in_=slotm[0:15, :], allow_slow_non_contiguous=True).then_inc(sQX.sem, 16); sQX.n += 16
        q.dma_start(out=dbg_d["stat"][0:8, 0:1], in_=S8[:], allow_slow_non_contiguous=True).then_inc(sQX.sem, 16); sQX.n += 16
        q.dma_start(out=dbg_d["stat"][0:8, 1:2], in_=SM8[:], allow_slow_non_contiguous=True).then_inc(sQX.sem, 16); sQX.n += 16
        q.dma_start(out=dbg_d["stat"][0:8, 2:3], in_=m8[:], allow_slow_non_contiguous=True).then_inc(sQX.sem, 16); sQX.n += 16
    q.wait_ge(sDV.sem, slotm_ev)
    qdma(AP(sdat_d.tensor, 16, [[1, 15]]), slotm[0:15, 0:1])
    q.wait_ge(sQX.sem, sQX.n)
    for s_ in range(NSLOT):
        qdma(escols[:, s_:s_ + 1], AP(sdat_d.tensor, 16 + s_, [[0, 128], [1, 1]]))
    escol_ld = sQX.n
    v.wait_ge(sQX.sem, escol_ld)
    for s_ in range(NSLOT):
        ins = v.tensor_scalar_mul(tmpe[:], escols[:, s_:s_ + 1], 1024.0)
        sDV.inc(ins)
        v.wait_ge(sDV.sem, sDV.n)
        ins = v.tensor_scalar(tmpc8[:], iotapf[:, 0:8], tmpe[:], None, ALU.add)
        sDV.inc(ins)
        v.wait_ge(sDV.sem, sDV.n)
        ins = v.tensor_copy(iw1a[:, s_ * 8:(s_ + 1) * 8], tmpc8[:])
        sDV.inc(ins)
        v.wait_ge(sDV.sem, sDV.n)
        ins = v.tensor_scalar_mul(tmpe[:], escols[:, s_:s_ + 1], 4096.0)
        sDV.inc(ins)
        v.wait_ge(sDV.sem, sDV.n)
        for qq in range(4):
            ins = v.tensor_scalar(tmpc8[:], iotapf[:, qq * 8:(qq + 1) * 8], tmpe[:], None, ALU.add)
            sDV.inc(ins)
            v.wait_ge(sDV.sem, sDV.n)
            ins = v.tensor_copy(iw2a[:, s_ * 32 + qq * 8: s_ * 32 + (qq + 1) * 8], tmpc8[:])
            sDV.inc(ins)
            v.wait_ge(sDV.sem, sDV.n)
        ins = v.tensor_scalar_mul(tmpe[:], escols[:, s_:s_ + 1], 128.0)
        sDV.inc(ins)
        v.wait_ge(sDV.sem, sDV.n)
        ins = v.tensor_scalar(selc[:], iotapf[:, 0:1], tmpe[:], None, ALU.add)
        sDV.inc(ins)
        v.wait_ge(sDV.sem, sDV.n)
        ins = v.tensor_copy(ib1a[:, s_:s_ + 1], selc[:])
        sDV.inc(ins)
        v.wait_ge(sDV.sem, sDV.n)
        ins = v.tensor_copy(iesa[:, s_:s_ + 1], escols[:, s_:s_ + 1])
        sDV.inc(ins)
    idx_done = sDV.n

    if debug:
        qdma(dbg_d["oh"][:], ohsb[:])
    for rb in range(32):
        qdma(c2all[:, rb * 8:(rb + 1) * 8], AP(c2_d.tensor, rb * 8, [[0, 128], [1, 8]]))
    c2all_ld = sQX.n
    ins = v.memset(xgs[:], 0.0)
    sDV.inc(ins)
    q.wait_ge(sDV.sem, sDV.n)
    qdma(AP(xls_d.tensor, 0, [[60 * 1024, 128], [1024, 60], [1, 1024]]),
         AP(xgs[:].tensor, 0, [[4096, 128], [0, 60], [1, 1024]]))
    xz_done = sQX.n
    g.wait_ge(sQX.sem, xz_done)
    rkv_ev = [0, 0]
    rk_evs = [0, 0]
    scat_by = [0, 0, 0, 0]
    pos_by = [0, 0, 0, 0]
    v.wait_ge(sQX.sem, c2all_ld)
    for rb in range(32):
        pb = rb % 2
        fb = rb % 4
        t.wait_ge(sAC.sem, rk_evs[pb])
        ins = t.matmul(psS[pb][:, 0:8], sl128[:], ohsb16[:, rb * 8:(rb + 1) * 8],
                       start=True, stop=True)
        ins = t.matmul(psS[2][:, 128:256], idb[:], idb[:], start=True, stop=True)
        mm = sPE.inc(ins)
        s.wait_ge(sPE.sem, mm)
        s.wait_ge(sDV.sem, rkv_ev[pb])
        ins = s.copy(rkev[:, pb * 128:(pb + 1) * 128], psS[pb][:, 0:128])
        rk_evs[pb] = sAC.inc(ins)
        v.wait_ge(sAC.sem, rk_evs[pb])
        v.wait_ge(sQW.sem, scat_by[fb])
        ins = v.tensor_tensor(rk8[:], rkev[:, pb * 128: pb * 128 + 8],
                              ohsb[:, rb * 8:(rb + 1) * 8], ALU.mult)
        rkv_ev[pb] = sDV.inc(ins)
        v.wait_ge(sDV.sem, sDV.n)
        ins = v.tensor_tensor(c2t[:], c2all[:, rb * 8:(rb + 1) * 8],
                              ohsb[:, rb * 8:(rb + 1) * 8], ALU.mult)
        sDV.inc(ins)
        v.wait_ge(sDV.sem, sDV.n)
        ins = v.reduce_sum(posf[:], rk8[:], axis=AX.X)
        sDV.inc(ins)
        v.wait_ge(sDV.sem, sDV.n)
        ins = v.reduce_sum(selc[:], c2t[:], axis=AX.X)
        sDV.inc(ins)
        v.wait_ge(sDV.sem, sDV.n)
        ins = v.tensor_tensor(posf[:], posf[:], selc[:], ALU.add)
        sDV.inc(ins)
        v.wait_ge(sDV.sem, sDV.n)
        ins = v.tensor_copy(posi[:, fb:fb + 1], posf[:])
        pos_done = sDV.inc(ins)
        v.wait_ge(sDV.sem, sDV.n)
        pos_by[fb] = pos_done
        q.wait_ge(sDV.sem, pos_done)
        qdma(AP(pos_d.tensor, rb * 128, [[1, 128]]), posi[:, fb:fb + 1])
        qdma(xgs[:, fb * 1024:(fb + 1) * 1024], xln_d[rb * 128:(rb + 1) * 128, :])
        xg_ld = sQX.n
        g.wait_ge(sQX.sem, xg_ld)
        ins = g.indirect_dma_start(xls_d[:], IndirectOffsetOnAxis(posi[:, fb:fb + 1], 0),
                                   xgs[:, fb * 1024:(fb + 1) * 1024], None,
                                   bounds_check=NSLOT * JT - 1, oob_is_err=False)
        ins.then_inc(sQW.sem, 16)
        sQW.n += 16
        scat_by[fb] = sQW.n
    phaseS_stores = sQX.n
    phaseS_scat = sQW.n
    del c2t

    esS.close()

    esB = ExitStack()

    def sbb(name, shape, dt):
        return esB.enter_context(nc.sbuf_tensor(name, shape, dt))

    w1g = sbb("w1g", [128, 8 * FF], BF16)
    w2g = sbb("w2g", [128, 8 * 1024], BF16)
    xg = sbb("xg", [128, 2 * 4096], BF16)
    xT = sbb("xT", [128, 8 * 512], BF16)
    h1sb = sbb("h1sb", [128, 32 * 512], BF16)
    outs = sbb("outs", [128, 4 * 1024], BF16)
    fb1g = sbb("fb1g", [128, 2 * 32], F32)
    fb2b = sbb("fb2b", [2, 2 * 1024], BF16)
    x1sl = sbb("x1sl", [128, 2 * 1024], F32)
    outb = sbb("outb", [128, 2 * 1024], F32)
    ln1c = sbb("ln1c", [128, 1024], F32)
    posb = sbb("posb", [128, 2], mybir.dt.int32)
    ffr = sbb("ffr", [128, 2 * 1024], BF16)
    acj = sbb("acj", [128, 512], F32)

    psB = [esB.enter_context(nc.psum_tensor(f"psB{i2}", [128, 512], F32)) for i2 in range(8)]

    blkB = esB.enter_context(nc.Block())
    barrier()
    g.wait_ge(sQW.sem, max(phaseS_scat, precast_done))

    w1_rd = 0
    w2_rd = [0] * 8
    h1_ev = [0, 0]
    tr_ev = [0, 0]
    out_st = 0
    xg_rd = 0
    w1_ready = 0
    fb_ready = 0
    w2_ready = [0] * 8

    def slot_pref(s_, buf):
        nonlocal w1_ready, fb_ready
        g.wait_ge(sPE.sem, w1_rd)
        for k in range(8):
            ins2 = g.indirect_dma_start(w1g[:, k * FF:(k + 1) * FF], None,
                                        w1b_d[:],
                                        IndirectOffsetOnAxis(iw1a[:, s_ * 8 + k: s_ * 8 + k + 1], 0))
            ins2.then_inc(sQW.sem, 16)
            sQW.n += 16
        ins2 = g.indirect_dma_start(fb1g[:, buf * 32:(buf + 1) * 32], None,
                                    fb1_d[:],
                                    IndirectOffsetOnAxis(ib1a[:, s_:s_ + 1], 0))
        ins2.then_inc(sQW.sem, 16)
        sQW.n += 16
        ins2 = g.indirect_dma_start(fb2b[0:2, buf * 1024:(buf + 1) * 1024], None,
                                    fb2b_d[:],
                                    IndirectOffsetOnAxis(iesa[0:2, s_:s_ + 1], 0))
        ins2.then_inc(sQW.sem, 16)
        sQW.n += 16
        w1_ready = sQW.n
        fb_ready = sQW.n

    slot_pref(0, 0)
    for s_ in range(NSLOT):
        buf = s_ % 2
        q.wait_ge(sPE.sem, xg_rd)
        q.wait_ge(sQW.sem, phaseS_scat)
        for sb_ in range(4):
            qdma(xg[:, buf * 4096 + sb_ * 1024: buf * 4096 + (sb_ + 1) * 1024],
                 xls_d[s_ * JT + sb_ * 128: s_ * JT + (sb_ + 1) * 128, :])
        xld = sQX.n
        for fc in range(8):
            g.wait_ge(sPE.sem, w2_rd[fc])
            ins = g.indirect_dma_start(w2g[:, fc * 1024:(fc + 1) * 1024], None,
                                       w2b_d[:],
                                       IndirectOffsetOnAxis(iw2a[:, s_ * 32 + fc: s_ * 32 + fc + 1], 0))
            ins.then_inc(sQW.sem, 16)
            sQW.n += 16
            w2_ready[fc] = sQW.n
        t.wait_ge(sQX.sem, xld)
        for sb_ in range(4):
            for k in range(8):
                tb = (sb_ * 8 + k) % 2
                t.wait_ge(sAC.sem, tr_ev[tb])
                ins = t.matmul(psB[tb][:, 0:128],
                               xg[:, buf * 4096 + sb_ * 1024 + k * 128: buf * 4096 + sb_ * 1024 + (k + 1) * 128],
                               idb[:], start=True, stop=True)
                mm = sPE.inc(ins)
                s.wait_ge(sPE.sem, mm)
                ins = s.copy(xT[:, k * 512 + sb_ * 128: k * 512 + (sb_ + 1) * 128],
                             psB[tb][:, 0:128])
                tr_ev[tb] = sAC.inc(ins)
        ins = s.copy(acj[:], acj[:])
        xT_done = sAC.inc(ins)
        xg_rd = sPE.n
        t.wait_ge(sQW.sem, w1_ready)
        t.wait_ge(sQW.sem, fb_ready)
        for m in range(32):
            pb = m % 2
            t.wait_ge(sAC.sem, h1_ev[pb] if m >= 2 else max(h1_ev[pb], xT_done))
            for k in range(8):
                ins = t.matmul(psB[pb][:, 0:512],
                               w1g[:, k * FF + m * 128: k * FF + (m + 1) * 128],
                               xT[:, k * 512:(k + 1) * 512],
                               start=(k == 0), stop=(k == 7))
            mm = sPE.inc(ins)
            s.wait_ge(sPE.sem, mm)
            s.wait_ge(sQW.sem, fb_ready)
            ins = s.activation(h1sb[:, m * 512:(m + 1) * 512], psB[pb][:, 0:512],
                               AF.Relu, bias=fb1g[:, buf * 32 + m: buf * 32 + m + 1])
            h1_ev[pb] = sAC.inc(ins)
        ins = s.copy(acj[:], acj[:])
        sAC.inc(ins)
        w1_rd = sPE.n
        h1_done = sAC.n
        if debug and s_ == 0:
            q.wait_ge(sPE.sem, w1_rd)
            qdma(dbg_d["w1g"][:], w1g[:, 0:FF])
            q.wait_ge(sAC.sem, xT_done)
            qdma(dbg_d["xT0"][:], xT[:, 0:512])
            q.wait_ge(sAC.sem, h1_done)
            qdma(dbg_d["h10"][:], h1sb[:])
        fb2b_done = 0
        if s_ + 1 < NSLOT:
            slot_pref(s_ + 1, 1 - buf)
        if debug and s_ == 0:
            q.wait_ge(sQW.sem, w2_ready[7])
            qdma(dbg_d["w2g0"][:], w2g[:])
            dbg_w2ld = sQX.n
            t.wait_ge(sQX.sem, dbg_w2ld)
        t.wait_ge(sAC.sem, h1_done)
        t.wait_ge(sQW.sem, fb_ready)
        w2_next = [0] * 8
        for fc in range(32):
            slot = fc % 8
            t.wait_ge(sQW.sem, w2_ready[slot] if fc < 8 else w2_next[slot])
            for sb_ in range(4):
                for grp in range(2):
                    ins = t.matmul(psB[sb_ * 2 + grp][:, 0:512],
                                   h1sb[:, fc * 512 + sb_ * 128: fc * 512 + (sb_ + 1) * 128],
                                   w2g[:, slot * 1024 + grp * 512: slot * 1024 + (grp + 1) * 512],
                                   start=(fc == 0), stop=False)
            fc_done = sPE.inc(ins)
            w2_rd[slot] = fc_done
            if fc + 8 < 32:
                g.wait_ge(sPE.sem, fc_done)
                ins = g.indirect_dma_start(w2g[:, slot * 1024:(slot + 1) * 1024], None,
                                           w2b_d[:],
                                           IndirectOffsetOnAxis(iw2a[:, s_ * 32 + fc + 8: s_ * 32 + fc + 9], 0))
                ins.then_inc(sQW.sem, 16)
                sQW.n += 16
                w2_next[slot] = sQW.n
        if debug and s_ == 0:
            ins = t.nop()
            sPE.inc(ins)
            s.wait_ge(sPE.sem, sPE.n)
            for _ in range(5):
                ins = s.copy(acj[:], acj[:])
            sAC.inc(ins)
            s.wait_ge(sAC.sem, sAC.n)
            ins = s.copy(acj[:], psB[0][:, 0:512])
            sAC.inc(ins)
            q.wait_ge(sAC.sem, sAC.n)
            qdma(dbg_d["ps0"][:], acj[:])
            t.wait_ge(sQX.sem, sQX.n)
        for sb_ in range(4):
            for grp in range(2):
                ins = t.matmul(psB[sb_ * 2 + grp][:, 0:512], onesk[:],
                               fb2b[0:1, buf * 1024 + grp * 512: buf * 1024 + (grp + 1) * 512],
                               start=False, stop=True)
        w2_done = sPE.inc(ins)
        s.wait_ge(sPE.sem, w2_done)
        s.wait_ge(sQX.sem, out_st)
        for sb_ in range(4):
            for grp in range(2):
                ins = s.copy(outs[:, sb_ * 1024 + grp * 512: sb_ * 1024 + (grp + 1) * 512],
                             psB[sb_ * 2 + grp][:, 0:512])
        sAC.inc(ins)
        ins = s.copy(acj[:], acj[:])
        out_ev = sAC.inc(ins)
        q.wait_ge(sAC.sem, out_ev)
        for sb_ in range(4):
            qdma(ffs_d[s_ * JT + sb_ * 128: s_ * JT + (sb_ + 1) * 128, :],
                 outs[:, sb_ * 1024:(sb_ + 1) * 1024])
        if debug and s_ == 0:
            qdma(dbg_d["o0"][:], outs[:, 0:1024])
        out_st = sQX.n
        tr_ev = [out_ev, out_ev]
        h1_ev = [out_ev, out_ev]
    ffs_done = sQX.n

    x1_free = [0, 0]
    out_free = [0, 0]
    ff_rd = [0, 0]
    for rb in range(32):
        cb = rb % 2
        q.wait_ge(sDV.sem, x1_free[cb])
        q.wait_ge(sQX.sem, ffs_done)
        qdma(x1sl[:, cb * 1024:(cb + 1) * 1024], x1_d[rb * 128:(rb + 1) * 128, :])
        qdma(posb[:, cb:cb + 1], AP(pos_d.tensor, rb * 128, [[1, 128]]))
        x1ld = sQX.n
        g.wait_ge(sQX.sem, x1ld)
        g.wait_ge(sDV.sem, ff_rd[cb])
        ins = g.indirect_dma_start(ffr[:, cb * 1024:(cb + 1) * 1024], None,
                                   ffs_d[:], IndirectOffsetOnAxis(posb[:, cb:cb + 1], 0))
        ins.then_inc(sQW.sem, 16)
        sQW.n += 16
        ffld = sQW.n
        v.wait_ge(sQX.sem, x1ld)
        v.wait_ge(sQW.sem, ffld)
        xs_ = x1sl[:, cb * 1024:(cb + 1) * 1024]
        v.tensor_scalar(ln1c[:], xs_, m1s[:, rb:rb + 1], r1s[:, rb:rb + 1],
                        ALU.subtract, ALU.mult)
        ins = v.tensor_tensor(ln1c[:], ln1c[:], xs_, ALU.add)
        x1_free[cb] = sDV.inc(ins)
        v.wait_ge(sST.sem, out_free[cb])
        ins = v.scalar_tensor_tensor(outb[:, cb * 1024:(cb + 1) * 1024],
                                     ffr[:, cb * 1024:(cb + 1) * 1024],
                                     alps[:, rb:rb + 1], ln1c[:], ALU.mult, ALU.add)
        o_done = sDV.inc(ins)
        ff_rd[cb] = o_done
        q.wait_ge(sDV.sem, o_done)
        q.dma_start(out=y_d[rb * 128:(rb + 1) * 128, :],
                    in_=outb[:, cb * 1024:(cb + 1) * 1024]).then_inc(sST.sem, 16)
        sST.n += 16
        out_free[cb] = sST.n

    q.wait_ge(sST.sem, sST.n)

    esB.close()
    es.close()
    return nc


def _host_inputs(inputs):
    x = np.ascontiguousarray(np.asarray(inputs["L0_attn_x"], dtype=np.float32))
    w_in = np.asarray(inputs["L0_attn_w_in"], dtype=np.float32)
    b_in = np.asarray(inputs["L0_attn_b_in"], dtype=np.float32)
    w_conv = np.asarray(inputs["L0_attn_w_conv"], dtype=np.float32)
    b_conv = np.asarray(inputs["L0_attn_b_conv"], dtype=np.float32)
    b1 = np.asarray(inputs["L0_attn_b1"], dtype=np.float32)
    w2 = np.asarray(inputs["L0_attn_w2"], dtype=np.float32)
    b2 = np.asarray(inputs["L0_attn_b2"], dtype=np.float32)
    w3 = np.asarray(inputs["L0_attn_w3"], dtype=np.float32)
    filt_bias = np.asarray(inputs["L0_attn_filt_bias"], dtype=np.float32)
    w_out = np.asarray(inputs["L0_attn_w_out"], dtype=np.float32)
    b_out = np.asarray(inputs["L0_attn_b_out"], dtype=np.float32)
    cent = np.asarray(inputs["L0_ffn_centroids"], dtype=np.float32)
    ln_g = np.asarray(inputs["L0_ffn_ln_g"], dtype=np.float32)
    ln_b = np.asarray(inputs["L0_ffn_ln_b"], dtype=np.float32)
    fw1 = np.asarray(inputs["L0_ffn_w1"], dtype=np.float32)
    fb1 = np.asarray(inputs["L0_ffn_b1"], dtype=np.float32)
    fw2 = np.asarray(inputs["L0_ffn_w2"], dtype=np.float32)
    fb2 = np.asarray(inputs["L0_ffn_b2"], dtype=np.float32)

    sel = np.r_[0:1024, 2048:3072]
    com = {
        "w_in_s": np.ascontiguousarray(w_in[:, sel]),
        "b_in_cm": np.ascontiguousarray(b_in[sel].reshape(16, 128).T),
        "bcv_cm": np.ascontiguousarray(b_conv[sel].reshape(16, 128).T),
        "fb_cm": np.ascontiguousarray(filt_bias.reshape(8, 128).T),
        "b1c": np.ascontiguousarray(b1.reshape(64, 1)),
        "w1fT": np.ascontiguousarray(np.asarray(inputs["L0_attn_w1"], dtype=np.float32).T),
        "w2f": np.ascontiguousarray(w2),
        "b2f": np.ascontiguousarray(b2.reshape(64, 1)),
        "w3f": np.ascontiguousarray(w3),
        "w_out": np.ascontiguousarray(w_out),
        "b_out_r": np.ascontiguousarray(b_out.reshape(1, D)),
        "cT": np.ascontiguousarray(cent.T),
        "ln_g": np.ascontiguousarray(ln_g),
        "ln_b": np.ascontiguousarray(ln_b),
        "fw1": np.ascontiguousarray(fw1),
        "fb1_cm": np.ascontiguousarray(fb1.reshape(NE, 32, 128).transpose(0, 2, 1).reshape(NE * 128, 32)),
        "fw2": np.ascontiguousarray(fw2),
        "fb2": np.ascontiguousarray(fb2),
        "id_f32": np.eye(128, dtype=np.float32),
        "id_b16": np.eye(128, dtype=np.float32),
        "sl128": np.triu(np.ones((128, 128), np.float32), 1),
        "su8": np.triu(np.ones((8, 8), np.float32), 1),
        "su32": np.triu(np.ones((32, 32), np.float32), 1),
        "ev8": np.arange(8, dtype=np.float32).reshape(8, 1),
        "ones32c": np.ones((32, 1), np.float32),
        "iota15": np.arange(15, dtype=np.float32).reshape(15, 1),
        "iota15r": np.arange(16, dtype=np.float32).reshape(1, 16),
        "evrow_in": np.arange(8, dtype=np.float32).reshape(1, 8),
        "iotapf": (np.arange(128, dtype=np.float32)[:, None]
                   + 128.0 * np.arange(32, dtype=np.float32)[None, :]),
    }
    for j in range(3):
        com[f"kc{j}"] = np.ascontiguousarray(w_conv[sel, 0, j].reshape(16, 128).T)

    in_maps = []
    for c in range(8):
        b, h = c // 2, c % 2
        start = h * NT
        if h == 0:
            halox = np.zeros((HALO, D), np.float32)
            mk = np.zeros((1, 1), np.float32)
        else:
            halox = x[b, start - HALO:start]
            mk = np.ones((1, 1), np.float32)
        m = dict(com)
        m["x0s"] = np.ascontiguousarray(np.concatenate([halox, x[b, start:start + NT]], 0))
        m["mask1"] = mk
        in_maps.append(m)
    return in_maps


def kernel(**inputs):
    in_maps = _host_inputs(inputs)
    nc = build(debug=False)
    res = run_bass_kernel_spmd(nc, in_maps, core_ids=list(range(8)))
    out = np.empty((4, 8192, D), np.float32)
    for c in range(8):
        b, h = c // 2, c % 2
        out[b, h * NT:(h + 1) * NT] = res.results[c]["y"]
    return out


# version 08 (final)
# speedup vs baseline: 1.1882x; 1.0058x over previous
import numpy as np
import ml_dtypes
from contextlib import ExitStack

import concourse.bass as bass
import concourse.mybir as mybir
from concourse.bass_utils import run_bass_kernel_spmd
from concourse.ap import AP
from concourse.bass_core_rust import IndirectOffsetOnAxis

F32 = mybir.dt.float32
BF16 = mybir.dt.bfloat16
AF = mybir.ActivationFunctionType
ALU = mybir.AluOpType
AX = mybir.AxisListType

D = 1024
NT = 4096
HALO = 1536
TOT = NT + HALO
BLK = 128
STT = 256
NST = TOT // STT
NHALO = HALO // STT
NE = 8
FF = 4096
NSLOT = 15
JT = 512
NJ = NT // JT
EPS = 1e-5


class Ctr:
    def __init__(self, sem):
        self.sem = sem
        self.n = 0

    def inc(self, ins, k=1):
        ins.then_inc(self.sem, k)
        self.n += k
        return self.n


def build(debug=False):
    nc = bass.Bass("TRN2", target_bir_lowering=False)

    x0_d = nc.dram_tensor("x0s", [TOT, D], F32, kind="ExternalInput").ap()
    msk_d = nc.dram_tensor("mask1", [1, 1], F32, kind="ExternalInput").ap()
    win_d = nc.dram_tensor("w_in_s", [D, 2048], F32, kind="ExternalInput").ap()
    bcin_d = nc.dram_tensor("b_in_cm", [128, 16], F32, kind="ExternalInput").ap()
    kc_d = [nc.dram_tensor(f"kc{j}", [128, 16], F32, kind="ExternalInput").ap() for j in range(3)]
    bcv_d = nc.dram_tensor("bcv_cm", [128, 16], F32, kind="ExternalInput").ap()
    fbc_d = nc.dram_tensor("fb_cm", [128, 8], F32, kind="ExternalInput").ap()
    b1_d = nc.dram_tensor("b1c", [64, 1], F32, kind="ExternalInput").ap()
    w1f_d = nc.dram_tensor("w1fT", [64, 3], F32, kind="ExternalInput").ap()
    w2f_d = nc.dram_tensor("w2f", [64, 64], F32, kind="ExternalInput").ap()
    b2f_d = nc.dram_tensor("b2f", [64, 1], F32, kind="ExternalInput").ap()
    w3f_d = nc.dram_tensor("w3f", [64, D], F32, kind="ExternalInput").ap()
    wout_d = nc.dram_tensor("w_out", [D, D], F32, kind="ExternalInput").ap()
    bout_d = nc.dram_tensor("b_out_r", [1, D], F32, kind="ExternalInput").ap()
    cT_d = nc.dram_tensor("cT", [D, 8], F32, kind="ExternalInput").ap()
    lng_d = nc.dram_tensor("ln_g", [8, D], F32, kind="ExternalInput").ap()
    lnb_d = nc.dram_tensor("ln_b", [8, D], F32, kind="ExternalInput").ap()
    w1_d = nc.dram_tensor("fw1", [NE, D, FF], F32, kind="ExternalInput").ap()
    fb1_d = nc.dram_tensor("fb1_cm", [NE * 128, 32], F32, kind="ExternalInput").ap()
    w2e_d = nc.dram_tensor("fw2", [NE, FF, D], F32, kind="ExternalInput").ap()
    fb2_d = nc.dram_tensor("fb2", [8, D], F32, kind="ExternalInput").ap()
    idf_d = nc.dram_tensor("id_f32", [128, 128], F32, kind="ExternalInput").ap()
    sl128_d = nc.dram_tensor("sl128", [128, 128], F32, kind="ExternalInput").ap()
    su8_d = nc.dram_tensor("su8", [8, 8], F32, kind="ExternalInput").ap()
    su32_d = nc.dram_tensor("su32", [32, 32], F32, kind="ExternalInput").ap()
    ev8_d = nc.dram_tensor("ev8", [8, 1], F32, kind="ExternalInput").ap()
    ones32_d = nc.dram_tensor("ones32c", [32, 1], F32, kind="ExternalInput").ap()
    iota15_d = nc.dram_tensor("iota15", [15, 1], F32, kind="ExternalInput").ap()
    iota15r_d = nc.dram_tensor("iota15r", [1, 16], F32, kind="ExternalInput").ap()
    evr_d = nc.dram_tensor("evrow_in", [1, 8], F32, kind="ExternalInput").ap()
    iotapf_d = nc.dram_tensor("iotapf", [128, 32], F32, kind="ExternalInput").ap()
    idb_d = nc.dram_tensor("id_b16", [128, 128], F32, kind="ExternalInput").ap()
    y_d = nc.dram_tensor("y", [NT, D], F32, kind="ExternalOutput").ap()

    dbg = "ExternalOutput" if debug else "Internal"
    hbuf_d = nc.dram_tensor("hbuf", [2048], BF16, kind=dbg).ap()
    x1_d = nc.dram_tensor("x1scr", [NT, D], F32, kind=dbg).ap()
    xln_d = nc.dram_tensor("xlnscr", [NT, D], BF16, kind=dbg).ap()
    xls_d = nc.dram_tensor("xlsort", [NSLOT * JT, D], BF16, kind=dbg).ap()
    ffs_d = nc.dram_tensor("ffsort", [NSLOT * JT, D], BF16, kind=dbg).ap()
    pos_d = nc.dram_tensor("posscr", [NT], mybir.dt.int32, kind=dbg).ap()
    sdat_d = nc.dram_tensor("sdat", [64], F32, kind=dbg).ap()
    c2_d = nc.dram_tensor("c2scr", [32, 8], F32, kind="Internal").ap()
    w1b_d = nc.dram_tensor("w1bscr", [NE * D, FF], BF16, kind="Internal").ap()
    fb2b_d = nc.dram_tensor("fb2bscr", [8, D], BF16, kind="Internal").ap()
    w2b_d = nc.dram_tensor("w2bscr", [NE * FF, D], BF16, kind="Internal").ap()
    if debug:
        dbg_d = {nm: nc.dram_tensor("dbg_" + nm, shp, dt_, kind="ExternalOutput").ap()
                 for nm, shp, dt_ in [
                     ("u2", [128, 16 * 258], F32), ("s0c", [128, 2048], F32),
                     ("vch", [128, 2048], F32), ("gb16", [128, 2048], BF16),
                     ("lnT", [128, 2048], BF16), ("ring", [128, 12 * 1024], BF16),
                     ("ydbg", [128, 2048], F32), ("x0", [128, 2048], F32),
                     ("lnb16", [128, 2048], BF16), ("stat", [128, 16], F32),
                     ("slot", [15, 32], F32), ("rk", [NT], F32), ("cs", [NT], F32),
                     ("oh", [128, 256], F32), ("w1g", [128, FF], BF16),
                     ("xT0", [128, 512], BF16), ("h10", [128, 32 * 512], BF16),
                     ("o0", [128, 1024], BF16), ("w2g0", [128, 8192], BF16),
                     ("ps0", [128, 512], F32)]}

    es = ExitStack()

    def sb(name, shape, dt):
        return es.enter_context(nc.sbuf_tensor(name, shape, dt))

    sQX = Ctr(es.enter_context(nc.semaphore("sQX")))
    sQW = Ctr(es.enter_context(nc.semaphore("sQW")))
    sPE = Ctr(es.enter_context(nc.semaphore("sPE")))
    sDV = Ctr(es.enter_context(nc.semaphore("sDV")))
    sAC = Ctr(es.enter_context(nc.semaphore("sAC")))
    sGP = Ctr(es.enter_context(nc.semaphore("sGP")))
    sST = Ctr(es.enter_context(nc.semaphore("sST")))

    idf = sb("idf", [128, 128], F32)
    idb = sb("idb", [128, 128], BF16)
    cTs = sb("cTs", [128, 64], F32)
    lngb = sb("lngb", [8, D], BF16)
    lnbb = sb("lnbb", [8, D], BF16)
    ohsb = sb("ohsb", [128, 256], F32)
    ohsb16 = sb("ohsb16", [128, 256], BF16)
    alps = sb("alps", [128, 32], F32)
    m1s = sb("m1s", [128, 32], F32)
    r1s = sb("r1s", [128, 32], F32)
    mskc = sb("mskc", [128, 1], F32)
    onesk = sb("onesk", [1, 128], BF16)
    boutb = sb("boutb", [1, D], BF16)
    mcol = sb("mcol", [128, 1], F32)
    sscol = sb("sscol", [128, 1], F32)
    ecol = sb("ecol", [128, 1], F32)
    tcol = sb("tcol", [128, 1], F32)
    vcol = sb("vcol", [128, 1], F32)
    rcol = sb("rcol", [128, 1], F32)
    m2col = sb("m2col", [128, 1], F32)
    r2col = sb("r2col", [128, 1], F32)
    affs = sb("affs", [128, 256], F32)
    amax = sb("amax", [128, 1], F32)
    ohblk = sb("ohblk", [128, 8], F32)
    tm4 = sb("tm4", [128, 6], F32)
    ohTf = sb("ohTf", [8, 128], F32)
    ohTb = sb("ohTb", [8, 128], BF16)
    iw1a = sb("iw1a", [128, 15 * 8], mybir.dt.int32)
    iw2a = sb("iw2a", [128, 15 * 32], mybir.dt.int32)
    ib1a = sb("ib1a", [128, 15], mybir.dt.int32)
    iesa = sb("iesa", [128, 15], mybir.dt.int32)
    statd = sb("statd", [128, 16], F32)
    bnst = sb("bnst", [128, 12], F32)
    bnagg = sb("bnagg", [128, 2], F32)

    t, v, s, g, q = nc.tensor, nc.vector, nc.scalar, nc.gpsimd, nc.sync

    def qdma(out, in_):
        q.dma_start(out=out, in_=in_).then_inc(sQX.sem, 16)
        sQX.n += 16
        return sQX.n

    def gdma(out, in_):
        g.dma_start(out=out, in_=in_).then_inc(sQW.sem, 16)
        sQW.n += 16
        return sQW.n

    def barrier():
        marks = [(sPE, t), (sDV, v), (sAC, s)]
        for c, eng in marks:
            c.inc(eng.nop())
        for c, eng in marks:
            for c2, _ in marks:
                if c2 is not c:
                    eng.wait_ge(c2.sem, c2.n)

    def ln_stats(src_ap, sq_sink, mean_col, rstd_col, dv_ready=0):
        v.bn_stats(bnst[:, 0:6], src_ap[:, 0:512])
        ins = v.bn_stats(bnst[:, 6:12], src_ap[:, 512:1024])
        sDV.inc(ins)
        v.wait_ge(sDV.sem, sDV.n)
        ins = v.bn_aggr(bnagg[:], bnst[:])
        sDV.inc(ins)
        v.wait_ge(sDV.sem, sDV.n)
        ins = v.tensor_copy(mean_col, bnagg[:, 0:1])
        sDV.inc(ins)
        v.wait_ge(sDV.sem, sDV.n)
        ins = v.tensor_scalar_add(vcol[:], bnagg[:, 1:2], EPS)
        sDV.inc(ins)
        s.wait_ge(sDV.sem, sDV.n)
        ins = s.activation(rcol[:], vcol[:], AF.Ln)
        sAC.inc(ins)
        s.wait_ge(sAC.sem, sAC.n)
        ins = s.activation(rstd_col, rcol[:], AF.Exp, scale=-0.5)
        sAC.inc(ins)
        v.wait_ge(sAC.sem, sAC.n)

    esA = ExitStack()

    def sba(name, shape, dt):
        return esA.enter_context(nc.sbuf_tensor(name, shape, dt))

    winsb = sba("winsb", [128, 8 * 2048], BF16)
    woutsb = sba("woutsb", [128, 8 * 1024], BF16)
    Rsb = sba("Rsb", [128, 10 * 256], BF16)
    ring = sba("ring", [128, 12 * 1024], BF16)
    x0t = sba("x0t", [128, 2 * 2048], F32)
    lnb16 = sba("lnb16", [128, 2048], BF16)
    lnT = sba("lnT", [128, 8 * 256], BF16)
    u2 = sba("u2", [128, 16 * 258], F32)
    s0c = sba("s0c", [128, 2 * 8 * 256], F32)
    vch = sba("vch", [128, 2 * 8 * 256], F32)
    vrev = sba("vrev", [128, 16 * 128], BF16)
    gb16 = sba("gb16", [128, 8 * 256], BF16)
    tmpA = sba("tmpA", [128, 256], F32)
    tmpB = sba("tmpB", [128, 256], F32)
    tmpC = sba("tmpC", [128, 256], F32)
    x1t = sba("x1t", [128, 1024], F32)
    ln1t = sba("ln1t", [128, 1024], F32)
    xlnb = sba("xlnb", [128, 1024], BF16)
    wcb = sba("wcb", [128, 4096], BF16)
    kcs = [sba(f"kcs{j}", [128, 16], F32) for j in range(3)]
    bcvs = sba("bcvs", [128, 16], F32)
    bcins = sba("bcins", [128, 16], F32)
    bcinm = sba("bcinm", [128, 16], F32)
    fbcs = sba("fbcs", [128, 8], F32)
    b1s = sba("b1s", [64, 1], F32)
    w1fs = sba("w1fs", [64, 3], F32)
    w1ss = sba("w1ss", [64, 1], F32)
    w2fs = sba("w2fs", [64, 64], F32)
    b2fs = sba("b2fs", [64, 1], F32)
    w3fs = sba("w3fs", [64, D], F32)
    r1f = sba("r1f", [64, 1], F32)
    r2f = sba("r2f", [64, 1], F32)
    hcol = sba("hcol", [128, 8], F32)
    h8 = sba("h8", [8, 128], F32)
    h8b = sba("h8b", [8, 128], BF16)
    z8 = sba("z8", [8, 256], BF16)
    fb2bb = sba("fb2bb", [8, D], BF16)

    ln1T8 = sba("ln1T8", [128, 1024], F32)

    psA = [esA.enter_context(nc.psum_tensor(f"psA{i}", [128, 512], F32)) for i in range(6)]
    ptA = [esA.enter_context(nc.psum_tensor(f"ptA{i}", [128, 1024], BF16)) for i in range(2)]

    blkA = esA.enter_context(nc.Block())

    qdma(idf[:], idf_d[:])
    for k in range(8):
        qdma(cTs[:, k * 8:(k + 1) * 8], cT_d[k * 128:(k + 1) * 128, :])
    for j in range(3):
        qdma(kcs[j][:], kc_d[j][:])
    qdma(bcvs[:], bcv_d[:])
    qdma(bcins[:], bcin_d[:])
    qdma(fbcs[:], fbc_d[:])
    qdma(b1s[:], b1_d[:])
    qdma(w1fs[:], w1f_d[:])
    qdma(w2fs[:], w2f_d[:])
    qdma(b2fs[:], b2f_d[:])
    qdma(w3fs[:], w3f_d[:])
    qdma(mskc[:], AP(msk_d.tensor, 0, [[0, 128], [1, 1]]))
    gdma(idb[:], idb_d[:])
    gdma(fb2bb[:], fb2_d[:])
    gdma(lngb[:], lng_d[:])
    gdma(lnbb[:], lnb_d[:])
    gdma(boutb[:], bout_d[:])
    for k in range(8):
        gdma(winsb[:, k * 2048:(k + 1) * 2048], win_d[k * 128:(k + 1) * 128, :])
        gdma(woutsb[:, k * 1024:(k + 1) * 1024], wout_d[k * 128:(k + 1) * 128, :])
    setup_qw = sQW.n
    setup_qx = sQX.n

    v.memset(u2[:], 0.0)
    v.memset(onesk[:], 1.0)
    ins = v.memset(z8[:], 0.0)
    sDV.inc(ins)
    v.wait_ge(sQX.sem, setup_qx)
    ins = v.tensor_scalar(bcinm[:], bcins[:], mskc[:], None, ALU.mult)
    sDV.inc(ins)

    v.wait_ge(sQX.sem, setup_qx)
    ins = v.reduce_sum(w1ss[:], w1fs[:], axis=AX.X)
    sDV.inc(ins)
    v.wait_ge(sDV.sem, sDV.n)
    ins = v.scalar_tensor_tensor(w1ss[:], w1ss[:], 0.0, b1s[:], ALU.mult, ALU.add)
    sDV.inc(ins)
    s.wait_ge(sDV.sem, sDV.n)
    ins = s.activation(r1f[:], w1ss[:], AF.Silu)
    sAC.inc(ins)
    t.wait_ge(sAC.sem, sAC.n)
    t.matmul(psA[0][0:64, 0:1], w2fs[:], r1f[:], start=True, stop=True)
    ins = t.matmul(psA[3][:, 0:128], idb[:], idb[:], start=True, stop=True)
    sPE.inc(ins)
    s.wait_ge(sPE.sem, sPE.n)
    ins = s.activation(r2f[:], psA[0][0:64, 0:1], AF.Silu, bias=b2fs[:])
    sAC.inc(ins)
    t.wait_ge(sAC.sem, sAC.n)
    for c in range(8):
        ins = t.matmul(psA[1][:, c:c + 1], w3fs[:, c * 128:(c + 1) * 128], r2f[:],
                       start=True, stop=True)
    ins = t.matmul(psA[3][:, 0:128], idb[:], idb[:], start=True, stop=True)
    sPE.inc(ins)
    v.wait_ge(sPE.sem, sPE.n)
    ins = v.tensor_copy(hcol[:], psA[1][:, 0:8])
    sDV.inc(ins)
    t.wait_ge(sDV.sem, sDV.n)
    t.transpose(psA[2][0:8, 0:128], hcol[:], idf[:])
    ins = t.matmul(psA[3][:, 0:128], idb[:], idb[:], start=True, stop=True)
    sPE.inc(ins)
    v.wait_ge(sPE.sem, sPE.n)
    v.tensor_copy(h8[:], psA[2][0:8, 0:128])
    ins = v.tensor_copy(h8b[:], h8[:])
    sDV.inc(ins)
    q.wait_ge(sDV.sem, sDV.n)
    q.wait_ge(sQW.sem, setup_qw)
    qdma(fb2b_d[:], fb2bb[:])
    qdma(AP(hbuf_d.tensor, 0, [[256, 8], [1, 256]]), z8[:])
    q.wait_ge(sQX.sem, sQX.n)
    qdma(AP(hbuf_d.tensor, 512, [[128, 8], [1, 128]]), h8b[:])
    q.wait_ge(sQX.sem, sQX.n)
    for ip, p in enumerate(range(-1, 9)):
        qdma(Rsb[:, ip * 256:(ip + 1) * 256],
             AP(hbuf_d.tensor, 512 + 128 * p - 127, [[1, 128], [1, 256]]))
    setup_qx2 = sQX.n

    x0buf = [x0t[:, 0:2048], x0t[:, 2048:4096]]
    for b in range(2):
        qdma(x0buf[0][:, b * 1024:(b + 1) * 1024], x0_d[b * 128:(b + 1) * 128, :])
    x0_ready = [sQX.n]

    for eng in (t, v, s, g):
        eng.wait_ge(sQX.sem, setup_qx2)
        eng.wait_ge(sQW.sem, setup_qw)

    evac_u = [0, 0]
    evac_pt = [0, 0]
    evac_y = [0, 0]
    evac_w = [0, 0]
    x0_free = [0, 0]
    ring_rd = 0
    tr_i = 0
    prev_stores = 0

    def transpose_b16(src_ap, dst_ap):
        nonlocal tr_i
        i = tr_i % 2
        tr_i += 1
        t.wait_ge(sAC.sem, evac_pt[i])
        ins = t.transpose(ptA[i][:, 0:128], src_ap, idb[:])
        sPE.inc(ins)
        s.wait_ge(sPE.sem, sPE.n)
        ins = s.copy(dst_ap, ptA[i][:, 0:128])
        evac_pt[i] = sAC.inc(ins)

    def transpose_f32(src_ap, dst_ap, bank, width=128):
        if bank < 2:
            t.wait_ge(sAC.sem, evac_u[bank])
        ins = t.transpose(psA[bank][0:width, 0:128], src_ap, idf[:])
        sPE.inc(ins)
        s.wait_ge(sPE.sem, sPE.n)
        ins = s.copy(dst_ap, psA[bank][0:width, 0:128])
        if bank < 2:
            evac_u[bank] = sAC.inc(ins)
        else:
            sAC.inc(ins)

    dv_chunks_prev = 0

    def prefetch_x0(tgt):
        nxt = tgt % 2
        q.wait_ge(sDV.sem, x0_free[nxt])
        for b in range(2):
            r0 = tgt * STT + b * BLK
            qdma(x0buf[nxt][:, b * 1024:(b + 1) * 1024], x0_d[r0:r0 + 128, :])
        x0_ready.append(sQX.n)

    def front(st_i):
        nonlocal dv_chunks_prev
        halo = st_i < NHALO
        cur = st_i % 2
        if st_i + 1 <= NHALO + 1 and st_i + 1 < NST:
            prefetch_x0(st_i + 1)

        v.wait_ge(sQX.sem, x0_ready[st_i])
        s.wait_ge(sQX.sem, x0_ready[st_i])
        for b in range(2):
            xb = x0buf[cur][:, b * 1024:(b + 1) * 1024]
            ln_stats(xb, None, mcol[:], rcol[:])
            ins = v.tensor_scalar(lnb16[:, b * 1024:(b + 1) * 1024], xb, mcol[:], rcol[:],
                                  ALU.subtract, ALU.mult)
            sDV.inc(ins)
        lnb16_done = sDV.n
        t.wait_ge(sDV.sem, lnb16_done)
        for b in range(2):
            for k in range(8):
                transpose_b16(lnb16[:, b * 1024 + k * 128: b * 1024 + (k + 1) * 128],
                              lnT[:, k * 256 + b * 128: k * 256 + (b + 1) * 128])
        lnT_done = sAC.n

        s0cb = s0c[:, cur * 2048:(cur + 1) * 2048]
        vchb = vch[:, cur * 2048:(cur + 1) * 2048]
        for c in range(16):
            pi = c % 2
            t.wait_ge(sAC.sem, max(lnT_done, evac_u[pi]))
            for k in range(8):
                ins = t.matmul(psA[pi][:, 0:256],
                               winsb[:, k * 2048 + c * 128: k * 2048 + (c + 1) * 128],
                               lnT[:, k * 256:(k + 1) * 256],
                               start=(k == 0), stop=(k == 7))
            mm_done = sPE.inc(ins)
            s.wait_ge(sPE.sem, mm_done)
            if c == 0:
                s.wait_ge(sDV.sem, dv_chunks_prev)
            uc = u2[:, c * 258:(c + 1) * 258]
            if halo:
                ins = s.activation(uc[:, 2:258], psA[pi][:, 0:256], AF.Identity,
                                   scale=mskc[:], bias=bcinm[:, c:c + 1])
            else:
                ins = s.activation(uc[:, 2:258], psA[pi][:, 0:256], AF.Identity,
                                   bias=bcins[:, c:c + 1])
            evac_u[pi] = sAC.inc(ins)
            v.wait_ge(sAC.sem, evac_u[pi])
            v.tensor_scalar(tmpA[:], uc[:, 2:258], kcs[2][:, c:c + 1], bcvs[:, c:c + 1],
                            ALU.mult, ALU.add)
            ins = v.scalar_tensor_tensor(tmpB[:], uc[:, 1:257], kcs[1][:, c:c + 1], tmpA[:],
                                         ALU.mult, ALU.add)
            sDV.inc(ins)
            if c < 8:
                ins = v.scalar_tensor_tensor(s0cb[:, c * 256:(c + 1) * 256],
                                             uc[:, 0:256], kcs[0][:, c:c + 1], tmpB[:],
                                             ALU.mult, ALU.add)
                sDV.inc(ins)
            else:
                cc = c - 8
                vd = vchb[:, cc * 256:(cc + 1) * 256]
                ins = v.scalar_tensor_tensor(tmpA[:], uc[:, 0:256], kcs[0][:, c:c + 1], tmpB[:],
                                             ALU.mult, ALU.add)
                sDV.inc(ins)
                ins = v.tensor_tensor(vd, tmpA[:], s0cb[:, cc * 256:(cc + 1) * 256], ALU.mult)
                sDV.inc(ins)
                for b in range(2):
                    seg = vd[:, b * 128:(b + 1) * 128]
                    vcol_ = vrev[:, (cc * 2 + b) * 128:(cc * 2 + b + 1) * 128]
                    if halo:
                        ins = v.tensor_scalar(vcol_, seg[:, ::-1], mskc[:], None, ALU.mult)
                    else:
                        ins = v.tensor_copy(vcol_, seg[:, ::-1])
                    vrev_done = sDV.inc(ins)
                    slot = (2 * st_i + b) % 12
                    t.wait_ge(sDV.sem, vrev_done)
                    transpose_b16(vcol_,
                                  ring[:, slot * 1024 + cc * 128: slot * 1024 + (cc + 1) * 128])
            ins = v.tensor_copy(uc[:, 0:2], uc[:, 256:258])
            sDV.inc(ins)
        dv_chunks_prev = sDV.n
        if halo:
            x0_free[cur] = sDV.n
        return sAC.n

    def back(st_i, ring_wr):
        nonlocal prev_stores
        cur = st_i % 2
        s0cb = s0c[:, cur * 2048:(cur + 1) * 2048]
        vchb = vch[:, cur * 2048:(cur + 1) * 2048]
        t.wait_ge(sAC.sem, ring_wr)
        for cc in range(8):
            yi = 2 + (cc % 2)
            t.wait_ge(sDV.sem, evac_y[cc % 2])
            for ip, p in enumerate(range(-1, 9)):
                slot = (2 * st_i - p) % 12
                ins = t.matmul(psA[yi][:, 0:256],
                               ring[:, slot * 1024 + cc * 128: slot * 1024 + (cc + 1) * 128],
                               Rsb[:, ip * 256:(ip + 1) * 256],
                               start=(ip == 0), stop=(ip == 9))
            y_done = sPE.inc(ins)
            v.wait_ge(sPE.sem, y_done)
            ins = v.scalar_tensor_tensor(tmpC[:], vchb[:, cc * 256:(cc + 1) * 256],
                                         fbcs[:, cc:cc + 1], psA[yi][:, 0:256],
                                         ALU.mult, ALU.add)
            sDV.inc(ins)
            ins = v.tensor_tensor(gb16[:, cc * 256:(cc + 1) * 256], tmpC[:],
                                  s0cb[:, cc * 256:(cc + 1) * 256], ALU.mult)
            evac_y[cc % 2] = sDV.inc(ins)
        g_done = sDV.n

        for b in range(2):
            rb = 2 * st_i + b - 12
            v.wait_ge(sQX.sem, prev_stores)
            s.wait_ge(sQX.sem, prev_stores)
            t.wait_ge(sDV.sem, g_done)
            for grp in range(2):
                t.wait_ge(sDV.sem, evac_w[grp])
                for cc in range(8):
                    ins = t.matmul(psA[4 + grp][:, 0:512],
                                   gb16[:, cc * 256 + b * 128: cc * 256 + (b + 1) * 128],
                                   woutsb[:, cc * 1024 + grp * 512: cc * 1024 + (grp + 1) * 512],
                                   start=(cc == 0), stop=False)
                ins = t.matmul(psA[4 + grp][:, 0:512], onesk[:],
                               boutb[0:1, grp * 512:(grp + 1) * 512], start=False, stop=True)
                w_done = sPE.inc(ins)
                v.wait_ge(sPE.sem, w_done)
                ins = v.tensor_tensor(x1t[:, grp * 512:(grp + 1) * 512],
                                      x0buf[cur][:, b * 1024 + grp * 512: b * 1024 + (grp + 1) * 512],
                                      psA[4 + grp][:, 0:512], ALU.add)
                evac_w[grp] = sDV.inc(ins)
            x1_done = sDV.n
            if b == 1:
                x0_free[cur] = x1_done
            q.wait_ge(sDV.sem, x1_done)
            qdma(x1_d[rb * 128:(rb + 1) * 128, :], x1t[:])

            ln_stats(x1t[:], None, m2col[:], r2col[:], dv_ready=x1_done)
            v.tensor_copy(m1s[:, rb:rb + 1], m2col[:])
            v.tensor_copy(r1s[:, rb:rb + 1], r2col[:])
            ins = v.tensor_scalar(ln1t[:], x1t[:], m2col[:], r2col[:], ALU.subtract, ALU.mult)
            ln1_done = sDV.inc(ins)

            t.wait_ge(sDV.sem, ln1_done)
            for k in range(8):
                transpose_f32(ln1t[:, k * 128:(k + 1) * 128],
                              ln1T8[:, k * 128:(k + 1) * 128], k % 2)
            lt_done = sAC.n
            t.wait_ge(sAC.sem, lt_done)
            t.wait_ge(sDV.sem, evac_w[0])
            for k in range(8):
                ins = t.matmul(psA[4][:, 0:8], ln1T8[:, k * 128:(k + 1) * 128],
                               cTs[:, k * 8:(k + 1) * 8], start=(k == 0), stop=(k == 7))
            t.wait_ge(sDV.sem, evac_w[1])
            ins = t.matmul(psA[5][:, 0:128], idb[:], idb[:], start=True, stop=True)
            aff_done = sPE.inc(ins)
            s.wait_ge(sPE.sem, aff_done)
            ins = s.copy(affs[:], psA[4][:, 0:256])
            aff_ev = sAC.inc(ins)
            v.wait_ge(sAC.sem, aff_ev)
            ins = v.tensor_tensor(tm4[:, 0:4], affs[:, 0:4], affs[:, 4:8], ALU.max)
            sDV.inc(ins)
            v.wait_ge(sDV.sem, sDV.n)
            ins = v.tensor_tensor(tm4[:, 4:6], tm4[:, 0:2], tm4[:, 2:4], ALU.max)
            sDV.inc(ins)
            v.wait_ge(sDV.sem, sDV.n)
            ins = v.tensor_tensor(amax[:], tm4[:, 4:5], tm4[:, 5:6], ALU.max)
            sDV.inc(ins)
            v.wait_ge(sDV.sem, sDV.n)
            ins = v.tensor_scalar(ohblk[:], affs[:, 0:8], amax[:], None, ALU.is_equal)
            oh_done = sDV.inc(ins)
            v.wait_ge(sDV.sem, sDV.n)
            v.tensor_copy(alps[:, rb:rb + 1], amax[:])
            v.tensor_copy(ohsb[:, rb * 8:(rb + 1) * 8], ohblk[:])
            ins = v.tensor_copy(ohsb16[:, rb * 8:(rb + 1) * 8], ohblk[:])
            sDV.inc(ins)
            t.wait_ge(sDV.sem, oh_done)
            t.wait_ge(sDV.sem, evac_y[1])
            transpose_f32(ohblk[:], ohTf[:], 3, width=8)
            ohT_ev = sAC.n
            v.wait_ge(sAC.sem, ohT_ev)
            ins = v.tensor_copy(ohTb[:], ohTf[:])
            ohtb_done = sDV.inc(ins)

            ln_stats(ln1t[:], None, m2col[:], r2col[:], dv_ready=ln1_done)
            ins = v.tensor_scalar(ln1t[:], ln1t[:], m2col[:], r2col[:], ALU.subtract, ALU.mult)
            xn_done = sDV.inc(ins)

            t.wait_ge(sDV.sem, ohtb_done)
            for grp in range(2):
                t.wait_ge(sDV.sem, evac_w[grp])
                ins = t.matmul(psA[4 + grp][:, 0:512], ohTb[:],
                               lngb[0:8, grp * 512:(grp + 1) * 512], start=True, stop=True)
                sPE.inc(ins)
            for grp in range(2):
                t.wait_ge(sDV.sem, evac_y[grp])
                ins = t.matmul(psA[2 + grp][:, 0:512], ohTb[:],
                               lnbb[0:8, grp * 512:(grp + 1) * 512], start=True, stop=True)
                sPE.inc(ins)
            gains_done = sPE.n
            v.wait_ge(sPE.sem, gains_done)
            v.wait_ge(sDV.sem, xn_done)
            for grp in range(2):
                sl = slice(grp * 512, (grp + 1) * 512)
                v.tensor_tensor(ln1t[:, sl], ln1t[:, sl], psA[4 + grp][:, 0:512], ALU.mult)
                ins = v.tensor_tensor(ln1t[:, sl], ln1t[:, sl], psA[2 + grp][:, 0:512], ALU.add)
                evac_w[grp] = sDV.inc(ins)
                evac_y[grp] = evac_w[grp]
            ins = v.tensor_copy(xlnb[:], ln1t[:])
            xlnb_done = sDV.inc(ins)
            q.wait_ge(sDV.sem, xlnb_done)
            qdma(xln_d[rb * 128:(rb + 1) * 128, :], xlnb[:])
            prev_stores = sQX.n
        if st_i + 2 < NST:
            prefetch_x0(st_i + 2)

    ring_wrs = {}
    ring_wrs[0] = front(0)
    for st_i in range(NST):
        if st_i + 1 < NST:
            ring_wrs[st_i + 1] = front(st_i + 1)
        if st_i >= NHALO:
            back(st_i, ring_wrs[st_i])

    pc_jobs = []
    for e in range(NE):
        for k in range(8):
            off = (e * 1024 + k * 128) * FF
            pc_jobs.append((AP(w1_d.tensor, off, [[FF, 128], [1, FF]]),
                            AP(w1b_d.tensor, off, [[FF, 128], [1, FF]])))
    for e in range(NE):
        for j in range(8):
            off = (e * FF + j * 512) * D
            pc_jobs.append((AP(w2e_d.tensor, off, [[4096, 128], [1, 4096]]),
                            AP(w2b_d.tensor, off, [[4096, 128], [1, 4096]])))
    pc_out = 0
    for pi_, (srcap, dstap) in enumerate(pc_jobs):
        g.wait_ge(sQW.sem, pc_out)
        lin = gdma(wcb[:], srcap)
        g.wait_ge(sQW.sem, lin)
        pc_out = gdma(dstap, wcb[:])
    precast_done = sQW.n

    phaseA_stores = sQX.n
    esA.close()

    esS = ExitStack()

    def sbs(name, shape, dt):
        return esS.enter_context(nc.sbuf_tensor(name, shape, dt))

    sl128 = sbs("sl128s", [128, 128], BF16)
    su8 = sbs("su8s", [8, 8], F32)
    su32 = sbs("su32s", [32, 32], F32)
    ev8 = sbs("ev8s", [8, 1], F32)
    ones32 = sbs("ones32s", [32, 1], F32)
    iota15 = sbs("iota15s", [15, 1], F32)
    iotapf = sbs("iotapfs", [128, 32], F32)
    onec = sbs("onec", [128, 1], BF16)
    bcT = sbs("bcT", [8, 32], F32)
    bc32 = sbs("bc32", [32, 8], F32)
    csum = sbs("csumS", [32, 8], F32)
    n8 = sbs("n8", [8, 1], F32)
    m8 = sbs("m8", [8, 1], F32)
    t8a = sbs("t8a", [8, 1], F32)
    S8 = sbs("S8", [8, 1], F32)
    SM8 = sbs("SM8", [8, 1], F32)
    S5r = sbs("S5r", [1, 8], F32)
    r83 = sbs("r83", [8, 2], F32)
    Stile = sbs("Stile", [15, 8], F32)
    SMtile = sbs("SMtile", [15, 8], F32)
    both = sbs("both", [15, 8], F32)
    bothT = sbs("bothTs", [8, 16], F32)
    i15r = sbs("i15r", [8, 16], F32)
    evrow = sbs("evrow", [15, 8], F32)
    geT = sbs("geT", [8, 16], F32)
    ltT = sbs("ltT", [8, 16], F32)
    slotm = sbs("slotm", [16, 2], F32)
    rk8 = sbs("rk8", [128, 8], F32)
    rkev = sbs("rkev", [128, 256], F32)
    c2t = sbs("c2t", [128, 8], F32)
    selc = sbs("selc", [128, 1], F32)
    posf = sbs("posf", [128, 1], F32)
    posi = sbs("posi", [128, 4], mybir.dt.int32)
    xgs = sbs("xgs", [128, 4 * 1024], BF16)
    c2all = sbs("c2all", [128, 256], F32)
    escols = sbs("escols", [128, 15], F32)
    tmpe = sbs("tmpe", [128, 1], F32)
    tmpc8 = sbs("tmpc8", [128, 8], F32)
    onesf = sbs("onesf", [1, 32], F32)

    psS = [esS.enter_context(nc.psum_tensor(f"psS{i2}", [128, 512], F32)) for i2 in range(4)]

    blkS = esS.enter_context(nc.Block())
    barrier()
    ins = s.activation(alps[:], alps[:], AF.Sigmoid)
    sAC.inc(ins)

    qdma(su8[:], su8_d[:])
    qdma(su32[:], su32_d[:])
    qdma(ev8[:], ev8_d[:])
    qdma(ones32[:], ones32_d[:])
    qdma(iota15[:], iota15_d[:])
    qdma(i15r[:], AP(iota15r_d.tensor, 0, [[0, 8], [1, 16]]))
    qdma(iotapf[:], iotapf_d[:])
    gdma(sl128[:], sl128_d[:])
    sload = sQX.n
    sloadw = sQW.n
    ins = v.memset(onec[:], 1.0)
    sDV.inc(ins)
    v.wait_ge(sDV.sem, sDV.n)
    ins = v.memset(onesf[:], 1.0)
    sDV.inc(ins)
    for eng in (t, v, s):
        eng.wait_ge(sQX.sem, sload)
        eng.wait_ge(sQW.sem, sloadw)
        eng.wait_ge(sDV.sem, sDV.n)

    bc_ev = [0, 0]
    for rb in range(32):
        pb = rb % 2
        t.wait_ge(sAC.sem, bc_ev[pb])
        ins = t.matmul(psS[pb][0:8, 0:1], ohsb16[:, rb * 8:(rb + 1) * 8], onec[:],
                       start=True, stop=True)
        ins = t.matmul(psS[3][:, 128:256], idb[:], idb[:], start=True, stop=True)
        mm = sPE.inc(ins)
        s.wait_ge(sPE.sem, mm)
        ins = s.copy(bcT[:, rb:rb + 1], psS[pb][0:8, 0:1])
        bc_ev[pb] = sAC.inc(ins)
    t.wait_ge(sAC.sem, sAC.n)
    t.transpose(psS[2][0:32, 0:8], bcT[:], idf[0:8, 0:8])
    ins = t.matmul(psS[3][:, 128:256], idb[:], idb[:], start=True, stop=True)
    sPE.inc(ins)
    s.wait_ge(sPE.sem, sPE.n)
    ins = s.copy(bc32[:], psS[2][0:32, 0:8])
    sAC.inc(ins)
    t.wait_ge(sAC.sem, sAC.n)
    ins = t.matmul(psS[0][0:32, 0:8], su32[:], bc32[:], start=True, stop=False)
    sPE.inc(ins)
    t.matmul(psS[1][0:8, 0:1], bc32[:], ones32[:], start=True, stop=True)
    ins = t.matmul(psS[3][:, 128:256], idb[:], idb[:], start=True, stop=True)
    sPE.inc(ins)
    s.wait_ge(sPE.sem, sPE.n)
    ins = s.copy(n8[:], psS[1][0:8, 0:1])
    n8_ev = sAC.inc(ins)
    v.wait_ge(sAC.sem, n8_ev)
    ins = v.memset(m8[:], 0.0)
    sDV.inc(ins)
    v.wait_ge(sDV.sem, sDV.n)
    for jj in range(8):
        ins = v.tensor_scalar(t8a[:], n8[:], float(512 * jj), None, ALU.is_gt)
        sDV.inc(ins)
        v.wait_ge(sDV.sem, sDV.n)
        ins = v.tensor_tensor(m8[:], m8[:], t8a[:], ALU.add)
    m8_done = sDV.inc(ins)
    v.wait_ge(sDV.sem, sDV.n)
    t.wait_ge(sDV.sem, m8_done)
    ins = t.matmul(psS[1][0:8, 1:2], su8[:], m8[:], start=True, stop=True)
    sPE.inc(ins)
    v.wait_ge(sPE.sem, sPE.n)
    ins = v.tensor_copy(S8[:], psS[1][0:8, 1:2])
    sDV.inc(ins)
    v.wait_ge(sDV.sem, sDV.n)
    ins = v.tensor_tensor(SM8[:], psS[1][0:8, 1:2], m8[:], ALU.add)
    S8_done = sDV.inc(ins)
    v.wait_ge(sDV.sem, sDV.n)
    t.wait_ge(sDV.sem, S8_done)
    t.transpose(psS[3][0:1, 0:8], S8[:], idf[0:8, 0:8])
    ins = t.matmul(psS[2][:, 128:256], idb[:], idb[:], start=True, stop=True)
    sPE.inc(ins)
    v.wait_ge(sPE.sem, sPE.n)
    ins = v.tensor_scalar_mul(S5r[:], psS[3][0:1, 0:8], 512.0)
    s5_done = sDV.inc(ins)
    v.wait_ge(sDV.sem, sDV.n)
    t.wait_ge(sDV.sem, s5_done)
    t.matmul(psS[0][0:32, 0:8], onesf[:], S5r[:], start=False, stop=True)
    ins = t.matmul(psS[3][:, 128:256], idb[:], idb[:], start=True, stop=True)
    c2mm = sPE.inc(ins)
    s.wait_ge(sPE.sem, c2mm)
    ins = s.copy(csum[:], psS[0][0:32, 0:8])
    c2_ev = sAC.inc(ins)
    q.wait_ge(sDV.sem, S8_done)
    qdma(AP(sdat_d.tensor, 0, [[1, 8]]), S8[:])
    qdma(AP(sdat_d.tensor, 8, [[1, 8]]), SM8[:])
    q.wait_ge(sAC.sem, c2_ev)
    qdma(c2_d[:], csum[:])
    q.wait_ge(sQX.sem, sQX.n)
    qdma(Stile[:], AP(sdat_d.tensor, 0, [[0, 15], [1, 8]]))
    qdma(SMtile[:], AP(sdat_d.tensor, 8, [[0, 15], [1, 8]]))
    stile_ld = sQX.n
    v.wait_ge(sQX.sem, stile_ld)
    ins = v.tensor_scalar(Stile[:], Stile[:], iota15[:], None, ALU.is_le)
    sDV.inc(ins)
    v.wait_ge(sDV.sem, sDV.n)
    ins = v.tensor_scalar(SMtile[:], SMtile[:], iota15[:], None, ALU.is_gt)
    sDV.inc(ins)
    v.wait_ge(sDV.sem, sDV.n)
    ins = v.tensor_tensor(both[:], Stile[:], SMtile[:], ALU.mult)
    both_done = sDV.inc(ins)
    v.wait_ge(sDV.sem, sDV.n)
    ins = v.tensor_scalar(geT[:], i15r[:], S8[:], None, ALU.is_ge)
    sDV.inc(ins)
    v.wait_ge(sDV.sem, sDV.n)
    ins = v.tensor_scalar(ltT[:], i15r[:], SM8[:], None, ALU.is_lt)
    sDV.inc(ins)
    v.wait_ge(sDV.sem, sDV.n)
    ins = v.tensor_tensor(bothT[:], geT[:], ltT[:], ALU.mult)
    bothT_ev = sDV.inc(ins)
    v.wait_ge(sDV.sem, sDV.n)
    t.wait_ge(sDV.sem, bothT_ev)
    qdma(evrow[:], AP(evr_d.tensor, 0, [[0, 15], [1, 8]]))
    q.wait_ge(sQX.sem, sQX.n)
    v.wait_ge(sQX.sem, sQX.n)
    ins = v.tensor_tensor(evrow[:], evrow[:], both[:], ALU.mult)
    sDV.inc(ins)
    v.wait_ge(sDV.sem, sDV.n)
    ins = v.reduce_sum(slotm[0:15, 0:1], evrow[:], axis=AX.X)
    slotm_ev = sDV.inc(ins)
    v.wait_ge(sDV.sem, sDV.n)
    if debug:
        v.wait_ge(sDV.sem, slotm_ev)
        ins = v.tensor_copy(both[:], both[:])
        sDV.inc(ins)
        q.wait_ge(sDV.sem, sDV.n)
        qdma(dbg_d["slot"][:, 0:8], Stile[:])
        qdma(dbg_d["slot"][:, 8:16], SMtile[:])

        q.dma_start(out=dbg_d["slot"][0:8, 16:32], in_=bothT[:], allow_slow_non_contiguous=True).then_inc(sQX.sem, 16); sQX.n += 16
        q.dma_start(out=dbg_d["stat"][0:15, 3:5], in_=slotm[0:15, :], allow_slow_non_contiguous=True).then_inc(sQX.sem, 16); sQX.n += 16
        q.dma_start(out=dbg_d["stat"][0:8, 0:1], in_=S8[:], allow_slow_non_contiguous=True).then_inc(sQX.sem, 16); sQX.n += 16
        q.dma_start(out=dbg_d["stat"][0:8, 1:2], in_=SM8[:], allow_slow_non_contiguous=True).then_inc(sQX.sem, 16); sQX.n += 16
        q.dma_start(out=dbg_d["stat"][0:8, 2:3], in_=m8[:], allow_slow_non_contiguous=True).then_inc(sQX.sem, 16); sQX.n += 16
    q.wait_ge(sDV.sem, slotm_ev)
    qdma(AP(sdat_d.tensor, 16, [[1, 15]]), slotm[0:15, 0:1])
    q.wait_ge(sQX.sem, sQX.n)
    for s_ in range(NSLOT):
        qdma(escols[:, s_:s_ + 1], AP(sdat_d.tensor, 16 + s_, [[0, 128], [1, 1]]))
    escol_ld = sQX.n
    v.wait_ge(sQX.sem, escol_ld)
    for s_ in range(NSLOT):
        ins = v.tensor_scalar_mul(tmpe[:], escols[:, s_:s_ + 1], 1024.0)
        sDV.inc(ins)
        v.wait_ge(sDV.sem, sDV.n)
        ins = v.tensor_scalar(tmpc8[:], iotapf[:, 0:8], tmpe[:], None, ALU.add)
        sDV.inc(ins)
        v.wait_ge(sDV.sem, sDV.n)
        ins = v.tensor_copy(iw1a[:, s_ * 8:(s_ + 1) * 8], tmpc8[:])
        sDV.inc(ins)
        v.wait_ge(sDV.sem, sDV.n)
        ins = v.tensor_scalar_mul(tmpe[:], escols[:, s_:s_ + 1], 4096.0)
        sDV.inc(ins)
        v.wait_ge(sDV.sem, sDV.n)
        for qq in range(4):
            ins = v.tensor_scalar(tmpc8[:], iotapf[:, qq * 8:(qq + 1) * 8], tmpe[:], None, ALU.add)
            sDV.inc(ins)
            v.wait_ge(sDV.sem, sDV.n)
            ins = v.tensor_copy(iw2a[:, s_ * 32 + qq * 8: s_ * 32 + (qq + 1) * 8], tmpc8[:])
            sDV.inc(ins)
            v.wait_ge(sDV.sem, sDV.n)
        ins = v.tensor_scalar_mul(tmpe[:], escols[:, s_:s_ + 1], 128.0)
        sDV.inc(ins)
        v.wait_ge(sDV.sem, sDV.n)
        ins = v.tensor_scalar(selc[:], iotapf[:, 0:1], tmpe[:], None, ALU.add)
        sDV.inc(ins)
        v.wait_ge(sDV.sem, sDV.n)
        ins = v.tensor_copy(ib1a[:, s_:s_ + 1], selc[:])
        sDV.inc(ins)
        v.wait_ge(sDV.sem, sDV.n)
        ins = v.tensor_copy(iesa[:, s_:s_ + 1], escols[:, s_:s_ + 1])
        sDV.inc(ins)
    idx_done = sDV.n

    if debug:
        qdma(dbg_d["oh"][:], ohsb[:])
    for rb in range(32):
        qdma(c2all[:, rb * 8:(rb + 1) * 8], AP(c2_d.tensor, rb * 8, [[0, 128], [1, 8]]))
    c2all_ld = sQX.n
    ins = v.memset(xgs[:], 0.0)
    sDV.inc(ins)
    q.wait_ge(sDV.sem, sDV.n)
    qdma(AP(xls_d.tensor, 0, [[60 * 1024, 128], [1024, 60], [1, 1024]]),
         AP(xgs[:].tensor, 0, [[4096, 128], [0, 60], [1, 1024]]))
    xz_done = sQX.n
    g.wait_ge(sQX.sem, xz_done)
    rkv_ev = [0, 0]
    rk_evs = [0, 0]
    scat_by = [0, 0, 0, 0]
    pos_by = [0, 0, 0, 0]
    v.wait_ge(sQX.sem, c2all_ld)
    for rb in range(32):
        pb = rb % 2
        fb = rb % 4
        t.wait_ge(sAC.sem, rk_evs[pb])
        ins = t.matmul(psS[pb][:, 0:8], sl128[:], ohsb16[:, rb * 8:(rb + 1) * 8],
                       start=True, stop=True)
        ins = t.matmul(psS[2][:, 128:256], idb[:], idb[:], start=True, stop=True)
        mm = sPE.inc(ins)
        s.wait_ge(sPE.sem, mm)
        s.wait_ge(sDV.sem, rkv_ev[pb])
        ins = s.copy(rkev[:, pb * 128:(pb + 1) * 128], psS[pb][:, 0:128])
        rk_evs[pb] = sAC.inc(ins)
        v.wait_ge(sAC.sem, rk_evs[pb])
        v.wait_ge(sQW.sem, scat_by[fb])
        ins = v.tensor_tensor(rk8[:], rkev[:, pb * 128: pb * 128 + 8],
                              ohsb[:, rb * 8:(rb + 1) * 8], ALU.mult)
        rkv_ev[pb] = sDV.inc(ins)
        v.wait_ge(sDV.sem, sDV.n)
        ins = v.tensor_tensor(c2t[:], c2all[:, rb * 8:(rb + 1) * 8],
                              ohsb[:, rb * 8:(rb + 1) * 8], ALU.mult)
        sDV.inc(ins)
        v.wait_ge(sDV.sem, sDV.n)
        ins = v.reduce_sum(posf[:], rk8[:], axis=AX.X)
        sDV.inc(ins)
        v.wait_ge(sDV.sem, sDV.n)
        ins = v.reduce_sum(selc[:], c2t[:], axis=AX.X)
        sDV.inc(ins)
        v.wait_ge(sDV.sem, sDV.n)
        ins = v.tensor_tensor(posf[:], posf[:], selc[:], ALU.add)
        sDV.inc(ins)
        v.wait_ge(sDV.sem, sDV.n)
        ins = v.tensor_copy(posi[:, fb:fb + 1], posf[:])
        pos_done = sDV.inc(ins)
        v.wait_ge(sDV.sem, sDV.n)
        pos_by[fb] = pos_done
        q.wait_ge(sDV.sem, pos_done)
        qdma(AP(pos_d.tensor, rb * 128, [[1, 128]]), posi[:, fb:fb + 1])
        qdma(xgs[:, fb * 1024:(fb + 1) * 1024], xln_d[rb * 128:(rb + 1) * 128, :])
        xg_ld = sQX.n
        g.wait_ge(sQX.sem, xg_ld)
        ins = g.indirect_dma_start(xls_d[:], IndirectOffsetOnAxis(posi[:, fb:fb + 1], 0),
                                   xgs[:, fb * 1024:(fb + 1) * 1024], None,
                                   bounds_check=NSLOT * JT - 1, oob_is_err=False)
        ins.then_inc(sQW.sem, 16)
        sQW.n += 16
        scat_by[fb] = sQW.n
    phaseS_stores = sQX.n
    phaseS_scat = sQW.n
    del c2t

    esS.close()

    esB = ExitStack()

    def sbb(name, shape, dt):
        return esB.enter_context(nc.sbuf_tensor(name, shape, dt))

    w1g = sbb("w1g", [128, 8 * FF], BF16)
    w2g = sbb("w2g", [128, 8 * 1024], BF16)
    xg = sbb("xg", [128, 2 * 4096], BF16)
    xT = sbb("xT", [128, 8 * 512], BF16)
    h1sb = sbb("h1sb", [128, 32 * 512], BF16)
    outs = sbb("outs", [128, 4 * 1024], BF16)
    fb1g = sbb("fb1g", [128, 2 * 32], F32)
    fb2b = sbb("fb2b", [2, 2 * 1024], BF16)
    x1sl = sbb("x1sl", [128, 2 * 1024], F32)
    outb = sbb("outb", [128, 2 * 1024], F32)
    ln1c = sbb("ln1c", [128, 1024], F32)
    posb = sbb("posb", [128, 2], mybir.dt.int32)
    ffr = sbb("ffr", [128, 2 * 1024], BF16)
    acj = sbb("acj", [128, 512], F32)

    psB = [esB.enter_context(nc.psum_tensor(f"psB{i2}", [128, 512], F32)) for i2 in range(8)]

    blkB = esB.enter_context(nc.Block())
    barrier()
    g.wait_ge(sQW.sem, max(phaseS_scat, precast_done))

    w1_rd = 0
    w2_rd = [0] * 8
    h1_ev = [0, 0]
    tr_ev = [0, 0]
    out_st = 0
    xg_rd = 0
    w1_ready = 0
    fb_ready = 0
    w2_ready = [0] * 8

    def slot_pref(s_, buf):
        nonlocal w1_ready, fb_ready
        g.wait_ge(sPE.sem, w1_rd)
        for k in range(8):
            ins2 = g.indirect_dma_start(w1g[:, k * FF:(k + 1) * FF], None,
                                        w1b_d[:],
                                        IndirectOffsetOnAxis(iw1a[:, s_ * 8 + k: s_ * 8 + k + 1], 0))
            ins2.then_inc(sQW.sem, 16)
            sQW.n += 16
        ins2 = g.indirect_dma_start(fb1g[:, buf * 32:(buf + 1) * 32], None,
                                    fb1_d[:],
                                    IndirectOffsetOnAxis(ib1a[:, s_:s_ + 1], 0))
        ins2.then_inc(sQW.sem, 16)
        sQW.n += 16
        ins2 = g.indirect_dma_start(fb2b[0:2, buf * 1024:(buf + 1) * 1024], None,
                                    fb2b_d[:],
                                    IndirectOffsetOnAxis(iesa[0:2, s_:s_ + 1], 0))
        ins2.then_inc(sQW.sem, 16)
        sQW.n += 16
        w1_ready = sQW.n
        fb_ready = sQW.n

    slot_pref(0, 0)
    for s_ in range(NSLOT):
        buf = s_ % 2
        q.wait_ge(sPE.sem, xg_rd)
        q.wait_ge(sQW.sem, phaseS_scat)
        for sb_ in range(4):
            qdma(xg[:, buf * 4096 + sb_ * 1024: buf * 4096 + (sb_ + 1) * 1024],
                 xls_d[s_ * JT + sb_ * 128: s_ * JT + (sb_ + 1) * 128, :])
        xld = sQX.n
        for fc in range(8):
            g.wait_ge(sPE.sem, w2_rd[fc])
            ins = g.indirect_dma_start(w2g[:, fc * 1024:(fc + 1) * 1024], None,
                                       w2b_d[:],
                                       IndirectOffsetOnAxis(iw2a[:, s_ * 32 + fc: s_ * 32 + fc + 1], 0))
            ins.then_inc(sQW.sem, 16)
            sQW.n += 16
            w2_ready[fc] = sQW.n
        t.wait_ge(sQX.sem, xld)
        for sb_ in range(4):
            for k in range(8):
                tb = (sb_ * 8 + k) % 2
                t.wait_ge(sAC.sem, tr_ev[tb])
                ins = t.matmul(psB[tb][:, 0:128],
                               xg[:, buf * 4096 + sb_ * 1024 + k * 128: buf * 4096 + sb_ * 1024 + (k + 1) * 128],
                               idb[:], start=True, stop=True)
                mm = sPE.inc(ins)
                s.wait_ge(sPE.sem, mm)
                ins = s.copy(xT[:, k * 512 + sb_ * 128: k * 512 + (sb_ + 1) * 128],
                             psB[tb][:, 0:128])
                tr_ev[tb] = sAC.inc(ins)
        ins = s.copy(acj[:], acj[:])
        xT_done = sAC.inc(ins)
        xg_rd = sPE.n
        t.wait_ge(sQW.sem, w1_ready)
        t.wait_ge(sQW.sem, fb_ready)
        for m in range(32):
            pb = m % 2
            t.wait_ge(sAC.sem, h1_ev[pb] if m >= 2 else max(h1_ev[pb], xT_done))
            for k in range(8):
                ins = t.matmul(psB[pb][:, 0:512],
                               w1g[:, k * FF + m * 128: k * FF + (m + 1) * 128],
                               xT[:, k * 512:(k + 1) * 512],
                               start=(k == 0), stop=(k == 7))
            mm = sPE.inc(ins)
            s.wait_ge(sPE.sem, mm)
            s.wait_ge(sQW.sem, fb_ready)
            ins = s.activation(h1sb[:, m * 512:(m + 1) * 512], psB[pb][:, 0:512],
                               AF.Relu, bias=fb1g[:, buf * 32 + m: buf * 32 + m + 1])
            h1_ev[pb] = sAC.inc(ins)
        ins = s.copy(acj[:], acj[:])
        sAC.inc(ins)
        w1_rd = sPE.n
        h1_done = sAC.n
        if debug and s_ == 0:
            q.wait_ge(sPE.sem, w1_rd)
            qdma(dbg_d["w1g"][:], w1g[:, 0:FF])
            q.wait_ge(sAC.sem, xT_done)
            qdma(dbg_d["xT0"][:], xT[:, 0:512])
            q.wait_ge(sAC.sem, h1_done)
            qdma(dbg_d["h10"][:], h1sb[:])
        fb2b_done = 0
        if s_ + 1 < NSLOT:
            slot_pref(s_ + 1, 1 - buf)
        if debug and s_ == 0:
            q.wait_ge(sQW.sem, w2_ready[7])
            qdma(dbg_d["w2g0"][:], w2g[:])
            dbg_w2ld = sQX.n
            t.wait_ge(sQX.sem, dbg_w2ld)
        t.wait_ge(sAC.sem, h1_done)
        t.wait_ge(sQW.sem, fb_ready)
        w2_next = [0] * 8
        for fc in range(32):
            slot = fc % 8
            t.wait_ge(sQW.sem, w2_ready[slot] if fc < 8 else w2_next[slot])
            for sb_ in range(4):
                for grp in range(2):
                    ins = t.matmul(psB[sb_ * 2 + grp][:, 0:512],
                                   h1sb[:, fc * 512 + sb_ * 128: fc * 512 + (sb_ + 1) * 128],
                                   w2g[:, slot * 1024 + grp * 512: slot * 1024 + (grp + 1) * 512],
                                   start=(fc == 0), stop=False)
            fc_done = sPE.inc(ins)
            w2_rd[slot] = fc_done
            if fc + 8 < 32:
                g.wait_ge(sPE.sem, fc_done)
                ins = g.indirect_dma_start(w2g[:, slot * 1024:(slot + 1) * 1024], None,
                                           w2b_d[:],
                                           IndirectOffsetOnAxis(iw2a[:, s_ * 32 + fc + 8: s_ * 32 + fc + 9], 0))
                ins.then_inc(sQW.sem, 16)
                sQW.n += 16
                w2_next[slot] = sQW.n
        if debug and s_ == 0:
            ins = t.nop()
            sPE.inc(ins)
            s.wait_ge(sPE.sem, sPE.n)
            for _ in range(5):
                ins = s.copy(acj[:], acj[:])
            sAC.inc(ins)
            s.wait_ge(sAC.sem, sAC.n)
            ins = s.copy(acj[:], psB[0][:, 0:512])
            sAC.inc(ins)
            q.wait_ge(sAC.sem, sAC.n)
            qdma(dbg_d["ps0"][:], acj[:])
            t.wait_ge(sQX.sem, sQX.n)
        for sb_ in range(4):
            for grp in range(2):
                ins = t.matmul(psB[sb_ * 2 + grp][:, 0:512], onesk[:],
                               fb2b[0:1, buf * 1024 + grp * 512: buf * 1024 + (grp + 1) * 512],
                               start=False, stop=True)
        w2_done = sPE.inc(ins)
        s.wait_ge(sPE.sem, w2_done)
        s.wait_ge(sQX.sem, out_st)
        for sb_ in range(4):
            for grp in range(2):
                ins = s.copy(outs[:, sb_ * 1024 + grp * 512: sb_ * 1024 + (grp + 1) * 512],
                             psB[sb_ * 2 + grp][:, 0:512])
        sAC.inc(ins)
        ins = s.copy(acj[:], acj[:])
        out_ev = sAC.inc(ins)
        q.wait_ge(sAC.sem, out_ev)
        for sb_ in range(4):
            qdma(ffs_d[s_ * JT + sb_ * 128: s_ * JT + (sb_ + 1) * 128, :],
                 outs[:, sb_ * 1024:(sb_ + 1) * 1024])
        if debug and s_ == 0:
            qdma(dbg_d["o0"][:], outs[:, 0:1024])
        out_st = sQX.n
        tr_ev = [out_ev, out_ev]
        h1_ev = [out_ev, out_ev]
    ffs_done = sQX.n

    x1_free = [0, 0]
    out_free = [0, 0]
    ff_rd = [0, 0]
    for rb in range(32):
        cb = rb % 2
        q.wait_ge(sDV.sem, x1_free[cb])
        q.wait_ge(sQX.sem, ffs_done)
        qdma(x1sl[:, cb * 1024:(cb + 1) * 1024], x1_d[rb * 128:(rb + 1) * 128, :])
        qdma(posb[:, cb:cb + 1], AP(pos_d.tensor, rb * 128, [[1, 128]]))
        x1ld = sQX.n
        g.wait_ge(sQX.sem, x1ld)
        g.wait_ge(sDV.sem, ff_rd[cb])
        ins = g.indirect_dma_start(ffr[:, cb * 1024:(cb + 1) * 1024], None,
                                   ffs_d[:], IndirectOffsetOnAxis(posb[:, cb:cb + 1], 0))
        ins.then_inc(sQW.sem, 16)
        sQW.n += 16
        ffld = sQW.n
        v.wait_ge(sQX.sem, x1ld)
        v.wait_ge(sQW.sem, ffld)
        xs_ = x1sl[:, cb * 1024:(cb + 1) * 1024]
        v.tensor_scalar(ln1c[:], xs_, m1s[:, rb:rb + 1], r1s[:, rb:rb + 1],
                        ALU.subtract, ALU.mult)
        ins = v.tensor_tensor(ln1c[:], ln1c[:], xs_, ALU.add)
        x1_free[cb] = sDV.inc(ins)
        v.wait_ge(sST.sem, out_free[cb])
        ins = v.scalar_tensor_tensor(outb[:, cb * 1024:(cb + 1) * 1024],
                                     ffr[:, cb * 1024:(cb + 1) * 1024],
                                     alps[:, rb:rb + 1], ln1c[:], ALU.mult, ALU.add)
        o_done = sDV.inc(ins)
        ff_rd[cb] = o_done
        q.wait_ge(sDV.sem, o_done)
        q.dma_start(out=y_d[rb * 128:(rb + 1) * 128, :],
                    in_=outb[:, cb * 1024:(cb + 1) * 1024]).then_inc(sST.sem, 16)
        sST.n += 16
        out_free[cb] = sST.n

    q.wait_ge(sST.sem, sST.n)

    esB.close()
    es.close()
    return nc


def _host_inputs(inputs):
    x = np.ascontiguousarray(np.asarray(inputs["L0_attn_x"], dtype=np.float32))
    w_in = np.asarray(inputs["L0_attn_w_in"], dtype=np.float32)
    b_in = np.asarray(inputs["L0_attn_b_in"], dtype=np.float32)
    w_conv = np.asarray(inputs["L0_attn_w_conv"], dtype=np.float32)
    b_conv = np.asarray(inputs["L0_attn_b_conv"], dtype=np.float32)
    b1 = np.asarray(inputs["L0_attn_b1"], dtype=np.float32)
    w2 = np.asarray(inputs["L0_attn_w2"], dtype=np.float32)
    b2 = np.asarray(inputs["L0_attn_b2"], dtype=np.float32)
    w3 = np.asarray(inputs["L0_attn_w3"], dtype=np.float32)
    filt_bias = np.asarray(inputs["L0_attn_filt_bias"], dtype=np.float32)
    w_out = np.asarray(inputs["L0_attn_w_out"], dtype=np.float32)
    b_out = np.asarray(inputs["L0_attn_b_out"], dtype=np.float32)
    cent = np.asarray(inputs["L0_ffn_centroids"], dtype=np.float32)
    ln_g = np.asarray(inputs["L0_ffn_ln_g"], dtype=np.float32)
    ln_b = np.asarray(inputs["L0_ffn_ln_b"], dtype=np.float32)
    fw1 = np.asarray(inputs["L0_ffn_w1"], dtype=np.float32)
    fb1 = np.asarray(inputs["L0_ffn_b1"], dtype=np.float32)
    fw2 = np.asarray(inputs["L0_ffn_w2"], dtype=np.float32)
    fb2 = np.asarray(inputs["L0_ffn_b2"], dtype=np.float32)

    sel = np.r_[0:1024, 2048:3072]
    com = {
        "w_in_s": np.ascontiguousarray(w_in[:, sel]),
        "b_in_cm": np.ascontiguousarray(b_in[sel].reshape(16, 128).T),
        "bcv_cm": np.ascontiguousarray(b_conv[sel].reshape(16, 128).T),
        "fb_cm": np.ascontiguousarray(filt_bias.reshape(8, 128).T),
        "b1c": np.ascontiguousarray(b1.reshape(64, 1)),
        "w1fT": np.ascontiguousarray(np.asarray(inputs["L0_attn_w1"], dtype=np.float32).T),
        "w2f": np.ascontiguousarray(w2),
        "b2f": np.ascontiguousarray(b2.reshape(64, 1)),
        "w3f": np.ascontiguousarray(w3),
        "w_out": np.ascontiguousarray(w_out),
        "b_out_r": np.ascontiguousarray(b_out.reshape(1, D)),
        "cT": np.ascontiguousarray(cent.T),
        "ln_g": np.ascontiguousarray(ln_g),
        "ln_b": np.ascontiguousarray(ln_b),
        "fw1": np.ascontiguousarray(fw1),
        "fb1_cm": np.ascontiguousarray(fb1.reshape(NE, 32, 128).transpose(0, 2, 1).reshape(NE * 128, 32)),
        "fw2": np.ascontiguousarray(fw2),
        "fb2": np.ascontiguousarray(fb2),
        "id_f32": np.eye(128, dtype=np.float32),
        "id_b16": np.eye(128, dtype=np.float32),
        "sl128": np.triu(np.ones((128, 128), np.float32), 1),
        "su8": np.triu(np.ones((8, 8), np.float32), 1),
        "su32": np.triu(np.ones((32, 32), np.float32), 1),
        "ev8": np.arange(8, dtype=np.float32).reshape(8, 1),
        "ones32c": np.ones((32, 1), np.float32),
        "iota15": np.arange(15, dtype=np.float32).reshape(15, 1),
        "iota15r": np.arange(16, dtype=np.float32).reshape(1, 16),
        "evrow_in": np.arange(8, dtype=np.float32).reshape(1, 8),
        "iotapf": (np.arange(128, dtype=np.float32)[:, None]
                   + 128.0 * np.arange(32, dtype=np.float32)[None, :]),
    }
    for j in range(3):
        com[f"kc{j}"] = np.ascontiguousarray(w_conv[sel, 0, j].reshape(16, 128).T)

    in_maps = []
    for c in range(8):
        b, h = c // 2, c % 2
        start = h * NT
        if h == 0:
            halox = np.zeros((HALO, D), np.float32)
            mk = np.zeros((1, 1), np.float32)
        else:
            halox = x[b, start - HALO:start]
            mk = np.ones((1, 1), np.float32)
        m = dict(com)
        m["x0s"] = np.ascontiguousarray(np.concatenate([halox, x[b, start:start + NT]], 0))
        m["mask1"] = mk
        in_maps.append(m)
    return in_maps


def kernel(**inputs):
    in_maps = _host_inputs(inputs)
    nc = build(debug=False)
    res = run_bass_kernel_spmd(nc, in_maps, core_ids=list(range(8)))
    out = np.empty((4, 8192, D), np.float32)
    for c in range(8):
        b, h = c // 2, c % 2
        out[b, h * NT:(h + 1) * NT] = res.results[c]["y"]
    return out
